# version 05
# baseline (speedup 1.0000x reference)
import math
import numpy as np
import ml_dtypes
from contextlib import ExitStack
from types import SimpleNamespace as NS

import concourse.bass as bass
import concourse.mybir as mybir
from concourse.bass_utils import run_bass_kernel_spmd

F32 = mybir.dt.float32
BF16 = mybir.dt.bfloat16
ACT = mybir.ActivationFunctionType
ALU = mybir.AluOpType
AX = mybir.AxisListType

NCORES = 8
GLA_H, GLA_DK, GLA_DV = 16, 64, 128
LNSCALE = -0.5 * math.log(128.0)
G16 = 1.0 / 16.0
TOP_K = 6
EPS = 1e-5

ENGINES = ("tensor", "vector", "scalar", "sync", "gpsimd")


class Pass:

    def __init__(self, nc, eng_name, e, sems, t, al, cfg):
        self.nc, self.eng, self.e = nc, eng_name, e
        self.sem, self.t, self.al, self.cfg = sems, t, al, cfg
        self.cnt = {k: 0 for k in sems if k != "bar"}
        self.bar_val = 0

    def emit(self, eng, f, sig=None, inc=1):
        if self.eng == eng:
            inst = f(self.e)
            if sig is not None:
                inst.then_inc(self.sem[sig], inc)
        if sig is not None:
            self.cnt[sig] += inc

    def wait(self, eng, sig, val=None):
        v = self.cnt[sig] if val is None else val
        if v <= 0:
            return
        if self.eng == eng:
            self.e.wait_ge(self.sem[sig], v)

    def drain(self, eng):
        if self.eng == eng:
            self.e.drain()

    def snap(self, sig):
        return self.cnt[sig]

    def barrier(self):
        self.bar_val += len(ENGINES)
        self.e.sem_inc(self.sem["bar"], 1)
        self.e.wait_ge(self.sem["bar"], self.bar_val)


def bank_new(t):
    return {"t": t, "rd": ("ve", 0), "wr": ("te", 0)}


class Slot:

    def __init__(self, p, sem):
        self.p, self.sem = p, sem
        self.free = ("te", 0)
        self.ld = 0

    def dma(self, eng, f):
        self.p.wait(eng, *self.free)
        self.p.emit(eng, f, sig=self.sem, inc=16)
        self.ld = self.p.cnt[self.sem]

    def wait_data(self, eng):
        self.p.wait(eng, self.sem, self.ld)

    def set_free(self, sig):
        self.free = (sig, self.p.snap(sig))


def bank_read(p, eng, bank, f, sig):
    p.wait(eng, bank["wr"][0], bank["wr"][1])
    p.emit(eng, f, sig=sig)
    bank["rd"] = (sig, p.snap(sig))


def host_layout(inp):
    g = {}
    f32 = np.float32
    asc = np.ascontiguousarray
    g["gla_wq"] = asc(inp["L0_attn_a_Wq"].T.reshape(16, 128, 8, 128).transpose(2, 0, 1, 3))
    g["gla_wk"] = asc(inp["L0_attn_a_Wk"].T.reshape(16, 128, 8, 128).transpose(2, 0, 1, 3))
    g["gla_wv"] = asc(inp["L0_attn_a_Wv"].T.reshape(16, 128, 16, 128).transpose(2, 0, 1, 3))
    g["gla_wg"] = asc(inp["L0_attn_a_Wg"].T.reshape(16, 128, 16, 128).transpose(2, 0, 1, 3))
    g["gla_wkg1"] = asc(inp["L0_attn_a_Wkg1"].T.reshape(16, 128, 16).transpose(1, 0, 2).reshape(128, 256))
    g["gla_wkg2"] = asc(inp["L0_attn_a_Wkg2"].T)
    g["gla_wo"] = asc(inp["L0_attn_a_Wo"].T.reshape(16, 128, 2048))
    g["gla_bkg2"] = asc(inp["L0_attn_a_bkg2"].reshape(1, 1024))
    g["gla_bg"] = asc(inp["L0_attn_a_bg"].reshape(16, 128).T)

    wg = inp["L0_ffn_wg"].reshape(32, 16, 128, 6, 128).transpose(0, 3, 1, 2, 4)
    sg = inp["L0_ffn_sg"].reshape(16, 128, 2, 6, 128).transpose(2, 3, 0, 1, 4)
    g["ffn_wg"] = asc(np.concatenate([wg, sg], axis=0))
    wu = inp["L0_ffn_wu"].reshape(32, 16, 128, 6, 128).transpose(0, 3, 1, 2, 4)
    su = inp["L0_ffn_su"].reshape(16, 128, 2, 6, 128).transpose(2, 3, 0, 1, 4)
    g["ffn_wu"] = asc(np.concatenate([wu, su], axis=0))
    wd = inp["L0_ffn_wd"].reshape(32, 6, 128, 2048)
    sd = inp["L0_ffn_sd"].reshape(2, 6, 128, 2048)
    g["ffn_wd"] = asc(np.concatenate([wd, sd], axis=0))
    g["ffn_gatew"] = asc(inp["L0_ffn_gate_w"].reshape(16, 128, 32).transpose(1, 0, 2).reshape(128, 512))

    ipw = inp["L0_attn_b_in_proj_w"]
    g["m_win_gate"] = asc(ipw[0:4096].T.reshape(16, 128, 32, 128).transpose(2, 0, 1, 3))
    g["m_win_hbc"] = asc(ipw[4096:8448].T.reshape(16, 128, 34, 128).transpose(2, 0, 1, 3))
    g["m_win_dt"] = asc(ipw[8448:8480].T.reshape(16, 128, 32).transpose(1, 0, 2).reshape(128, 512))
    g["m_convw"] = asc(inp["L0_attn_b_conv_w"].reshape(34, 128, 4).transpose(1, 0, 2).reshape(128, 136))
    g["m_convb"] = asc(inp["L0_attn_b_conv_b"].reshape(34, 128).T)
    g["m_alog"] = asc(inp["L0_attn_b_A_log"].reshape(1, 32))
    g["m_dtb"] = asc(inp["L0_attn_b_dt_bias"].reshape(1, 32))
    g["m_d"] = asc(inp["L0_attn_b_D"].reshape(1, 32))
    g["m_normw"] = asc(inp["L0_attn_b_norm_w"].reshape(1, 4096))
    g["m_wout"] = asc(inp["L0_attn_b_out_proj_w"].T.reshape(32, 128, 2048))

    s = np.arange(128)[:, None]
    tt = np.arange(128)[None, :]
    g["tri"] = (s <= tt).astype(f32)
    g["ones1"] = np.ones((1, 128), dtype=f32)
    g["ident32"] = np.eye(128, dtype=f32)
    g["ident16"] = np.eye(128, dtype=f32).astype(ml_dtypes.bfloat16)
    return g


def percore_inputs(inp, g, T):
    x = np.asarray(inp["L0_attn_a_x"], dtype=np.float32)
    maps = []
    for c in range(NCORES):
        b, half = c // 2, c % 2
        m = dict(g)
        m["x_in"] = np.ascontiguousarray(x[b, half * T:(half + 1) * T])
        m["odd_col"] = np.full((128, 1), float(half), np.float32)
        maps.append(m)
    return maps


def declare_io(nc, cfg):
    T = cfg.T
    dr = NS()
    di = lambda n, s, dt=F32: nc.dram_tensor(n, list(s), dt, kind="ExternalInput").ap()
    dint = lambda n, s, dt=F32: nc.dram_tensor(n, list(s), dt).ap()
    dr.x_in = di("x_in", (T, 2048))
    dr.odd_col = di("odd_col", (128, 1))
    dr.tri = di("tri", (128, 128))
    dr.ident32 = di("ident32", (128, 128))
    dr.ident16 = di("ident16", (128, 128), BF16)
    dr.ones1 = di("ones1", (1, 128))
    dr.gla_wq = di("gla_wq", (8, 16, 128, 128))
    dr.gla_wk = di("gla_wk", (8, 16, 128, 128))
    dr.gla_wv = di("gla_wv", (16, 16, 128, 128))
    dr.gla_wg = di("gla_wg", (16, 16, 128, 128))
    dr.gla_wkg1 = di("gla_wkg1", (128, 256))
    dr.gla_wkg2 = di("gla_wkg2", (16, 1024))
    dr.gla_wo = di("gla_wo", (16, 128, 2048))
    dr.gla_bkg2 = di("gla_bkg2", (1, 1024))
    dr.gla_bg = di("gla_bg", (128, 16))
    dr.ffn_wg = di("ffn_wg", (34, 6, 16, 128, 128))
    dr.ffn_wu = di("ffn_wu", (34, 6, 16, 128, 128))
    dr.ffn_wd = di("ffn_wd", (34, 6, 128, 2048))
    dr.ffn_gatew = di("ffn_gatew", (128, 512))
    dr.m_win_gate = di("m_win_gate", (32, 16, 128, 128))
    dr.m_win_hbc = di("m_win_hbc", (34, 16, 128, 128))
    dr.m_win_dt = di("m_win_dt", (128, 512))
    dr.m_convw = di("m_convw", (128, 136))
    dr.m_convb = di("m_convb", (128, 34))
    dr.m_alog = di("m_alog", (1, 32))
    dr.m_dtb = di("m_dtb", (1, 32))
    dr.m_d = di("m_d", (1, 32))
    dr.m_normw = di("m_normw", (1, 4096))
    dr.m_wout = di("m_wout", (32, 128, 2048))
    dr.out = nc.dram_tensor("out", [T, 2048], F32, kind="ExternalOutput").ap()
    dr.x_hbm = dint("x_hbm", (T, 2048))
    dr.o_hbm = dint("o_hbm", (T, 2048), BF16)
    dr.sg_hbm = dint("sg_hbm", (2048, T), BF16)
    dr.hbc_hbm = dint("hbc_hbm", (34, 128, 3 + T), BF16)
    dr.sgate_hbm = dint("sgate_hbm", (4096, T), BF16)
    dr.y_hbm = dint("y_hbm", (T, 4096), BF16)
    dr.cb1_in = dint("cb1_in", (1024, 128))
    dr.cb1_out = dint("cb1_out", (2048, 128))
    dr.cb2_in = dint("cb2_in", (34 * 128, 3))
    dr.cb2_out = dint("cb2_out", (2 * 34 * 128, 3))
    dr.cb3_in = dint("cb3_in", (4096, 128))
    dr.cb3_out = dint("cb3_out", (8192, 128))
    return dr


def alloc_persistent(nc, es, cfg):
    t = NS()
    sb = lambda n, s, dt=F32: es.enter_context(nc.sbuf_tensor(n, list(s), dt))
    t.tri = sb("c_tri", (128, 128))
    t.id32 = sb("c_id32", (128, 128))
    t.id16 = sb("c_id16", (128, 128), BF16)
    t.odd = sb("c_odd", (128, 1))
    t.bkg2 = sb("c_bkg2", (128, 1024))
    t.bg = sb("c_bg", (128, 16))
    t.apos = sb("c_apos", (128, 32))
    t.dpanel = sb("c_dpanel", (128, 32))
    t.dtbias = sb("c_dtbias", (128, 32))
    t.normw = sb("c_normw", (128, 4096), BF16)
    t.convw = sb("c_convw", (128, 136))
    t.convb = sb("c_convb", (128, 34))
    t.gatew = sb("c_gatew", (128, 512))
    t.ones1 = sb("c_ones1", (1, 128))
    n_tt = cfg.T // 128
    t.combine = sb("c_combine", (128, n_tt, 34))
    t.scores = sb("c_scores", (128, n_tt, 32))
    t.dt_tm = sb("c_dt", (128, n_tt, 32))
    t.acum_tm = sb("c_acum", (128, n_tt, 32))
    t.sdecabs = sb("c_sdecabs", (128, n_tt, 32))
    t.c_keep = sb("c_ckeep", (128, cfg.T), BF16)
    t.acrow = sb("c_acrow", (1, 4096))
    t.m32 = sb("c_m32", (128, 12, 32))
    t.prev8 = sb("c_prev8", (128, 8))
    t.epscol = sb("c_epscol", (128, 1))
    t.lnscol = sb("c_lnscol", (128, 1))
    return t


def emit_setup(p, cfg):
    t, d = p.t, p.t.dram
    loads = [
        (lambda: (t.tri[:], d.tri)), (lambda: (t.id32[:], d.ident32)),
        (lambda: (t.id16[:], d.ident16)), (lambda: (t.odd[:], d.odd_col)),
        (lambda: (t.gatew[:], d.ffn_gatew)), (lambda: (t.convw[:], d.m_convw)),
        (lambda: (t.convb[:], d.m_convb)), (lambda: (t.bg[:], d.gla_bg)),
        (lambda: (t.ones1[:], d.ones1)),
    ]
    for lf in loads:
        p.emit("sync", lambda e, lf=lf: e.dma_start(out=lf()[0], in_=lf()[1]), sig="yb", inc=16)
    bcasts = [
        (lambda: t.bkg2[:], lambda: d.gla_bkg2, 1024),
        (lambda: t.apos[:], lambda: d.m_alog, 32),
        (lambda: t.dpanel[:], lambda: d.m_d, 32),
        (lambda: t.dtbias[:], lambda: d.m_dtb, 32),
        (lambda: t.normw[:], lambda: d.m_normw, 4096),
    ]
    for dstf, srcf, n in bcasts:
        p.emit("gpsimd", lambda e, dstf=dstf, srcf=srcf, n=n:
               e.dma_start(out=dstf(), in_=srcf().broadcast_to((128, n))), sig="gb", inc=16)
    p.wait("gpsimd", "gb")
    p.wait("sync", "yb")
    p.barrier()
    p.emit("scalar", lambda e: e.activation(out=t.apos[:], in_=t.apos[:], func=ACT.Exp), sig="se")
    p.emit("vector", lambda e: e.memset(t.epscol[:], EPS))
    p.emit("vector", lambda e: e.memset(t.lnscol[:], LNSCALE))
    p.emit("vector", lambda e: e.memset(t.combine[:], 0.0))
    p.drain("vector")
    p.emit("vector", lambda e: e.memset(t.combine[:, :, 32:34], 1.0), sig="ve")
    p.barrier()


def alloc_ln(nc, es, tag, with_scores, with_xt, tr_banks=None):
    sb = lambda n, s, dt=F32: es.enter_context(nc.sbuf_tensor(tag + n, list(s), dt))
    ps = lambda n, s, dt=F32: es.enter_context(nc.psum_tensor(tag + n, list(s), dt))
    ph = NS()
    ph.with_scores = with_scores
    if with_xt:
        ph.x_t = [sb("_xt0", (128, 2048)), sb("_xt1", (128, 2048))]
    else:
        ph.x_t = None
    ph.xln = [sb("_xln0", (128, 2048))]
    ph.t32 = [sb("_t320", (128, 128)), sb("_t321", (128, 128))]
    ph.bns = sb("_bns", (128, 4, 6))
    ph.mv = sb("_mv", (128, 2))
    ph.sd = sb("_sd", (128, 1))
    ph.rs = sb("_rs", (128, 1))
    ph.tr = tr_banks if tr_banks is not None else [ps("_ptr0", (128, 512)), ps("_ptr1", (128, 512))]
    if with_scores:
        ph.sc = [ps("_psc0", (128, 512)), ps("_psc1", (128, 512))]
    return ph


def ln_state(p, ph, xslot_sems=("y0", "y1")):
    st = NS()
    st.b_tr = [bank_new(ph.tr[0]), bank_new(ph.tr[1])]
    if ph.with_scores:
        st.b_sc = [bank_new(ph.sc[0]), bank_new(ph.sc[1])]
    st.xslots = [Slot(p, xslot_sems[0]), Slot(p, xslot_sems[1])]
    st.xln_free_te = [0, 0]
    st.t32_free_se = [0, 0]
    st.t32_free_te = [0, 0]
    return st


def emit_ln_transpose(p, ph, st, x_hbm_src, x_sb_tiles, r0, n_tiles, xlnT, col_off):
    t = p.t
    scores = ph.with_scores
    for i in range(n_tiles):
        r = r0 + i
        if x_sb_tiles is None:
            xbuf = ph.x_t[i % 2]
            sl = st.xslots[i % 2]
            sl.dma("sync", lambda e, xbuf=xbuf, r=r: e.dma_start(
                out=xbuf[:], in_=x_hbm_src[r * 128:(r + 1) * 128, :]))
            sl.wait_data("vector")
            getx = lambda xbuf=xbuf: xbuf[:]
        else:
            getx = lambda r=r: x_sb_tiles(r)
        p.drain("vector")
        for jj in range(4):
            p.emit("vector", lambda e, getx=getx, jj=jj: e.bn_stats(
                out=ph.bns[:, jj, 0:6], in_=getx()[:, jj * 512:(jj + 1) * 512]))
        p.drain("vector")
        p.emit("vector", lambda e: e.bn_aggr(out=ph.mv[:, 0:2], in_=ph.bns[:, 0:4, 0:6]), sig="ve")
        p.wait("scalar", "ve")
        p.emit("scalar", lambda e: e.activation(out=ph.sd[:, 0:1], in_=ph.mv[:, 1:2],
                                                func=ACT.Sqrt, bias=t.epscol[:, 0:1]), sig="se")
        p.wait("vector", "se")
        p.emit("vector", lambda e: e.reciprocal(out=ph.rs[:, 0:1], in_=ph.sd[:, 0:1]))
        p.drain("vector")
        xl = ph.xln[0]
        p.wait("vector", "te", st.xln_free_te[0])
        p.emit("vector", lambda e, getx=getx, xl=xl: e.tensor_scalar(
            out=xl[:], in0=getx(), scalar1=ph.mv[:, 0:1], scalar2=ph.rs[:, 0:1],
            op0=ALU.subtract, op1=ALU.mult), sig="ve")
        if x_sb_tiles is None:
            st.xslots[i % 2].set_free("ve")
        xln_done = p.snap("ve")

        for dch in range(16):
            bank = st.b_tr[dch % 2]
            p.wait("tensor", bank["rd"][0], bank["rd"][1])
            p.wait("tensor", "ve", xln_done)
            p.emit("tensor", lambda e, xl=xl, dch=dch, bank=bank: e.transpose(
                bank["t"][:, 0:128], xl[:, dch * 128:(dch + 1) * 128], t.id32[:]), sig="te")
            bank["wr"] = ("te", p.snap("te"))
            s32 = ph.t32[dch % 2]
            p.wait("vector", "se", st.t32_free_se[dch % 2])
            p.wait("vector", "te", st.t32_free_te[dch % 2])
            bank_read(p, "vector", bank,
                      lambda e, s32=s32, bank=bank: e.tensor_copy(s32[:], bank["t"][:, 0:128]),
                      "ve")
            ev = p.snap("ve")
            p.wait("scalar", "ve", ev)
            p.emit("scalar", lambda e, s32=s32, dch=dch, i=i: e.copy(
                out=xlnT[:, dch, col_off + i * 128: col_off + (i + 1) * 128], in_=s32[:]),
                sig="se")
            st.t32_free_se[dch % 2] = p.snap("se")
            if scores:
                scb = st.b_sc[r % 2]
                p.wait("tensor", scb["rd"][0], scb["rd"][1])
                p.wait("tensor", "ve", ev)
                p.emit("tensor", lambda e, s32=s32, dch=dch, scb=scb: e.matmul(
                    scb["t"][:, 0:32], s32[:], t.gatew[:, dch * 32:(dch + 1) * 32],
                    start=(dch == 0), stop=(dch == 15)),
                    sig=("te" if dch == 15 else None))
                if dch == 15:
                    scb["wr"] = ("te", p.snap("te"))
                st.t32_free_te[dch % 2] = p.snap("te")
        st.xln_free_te[0] = p.snap("te")
        if scores:
            scb = st.b_sc[r % 2]
            bank_read(p, "scalar", scb,
                      lambda e, scb=scb, r=r: e.activation(
                          out=t.scores[:, r, :], in_=scb["t"][:, 0:32], func=ACT.Sigmoid),
                      "se")


def alloc_ffn(nc, es, cfg):
    T = cfg.T
    n_tt = T // 128
    sb = lambda n, s, dt=F32: es.enter_context(nc.sbuf_tensor(n, list(s), dt))
    f = NS()
    f.x = sb("f_x", (128, n_tt, 2048))
    f.xlnT = sb("f_xlnT", (128, 16, T), BF16)
    f.act = sb("f_act", (128, 6, T), BF16)
    f.sg = [sb("f_sg0", (128, 512), BF16), sb("f_sg1", (128, 512), BF16)]
    f.wg = [sb("f_wg%d" % i, (128, 2048), BF16) for i in range(2)]
    f.wu = [sb("f_wu%d" % i, (128, 2048), BF16) for i in range(2)]
    f.wd = [sb("f_wd%d" % i, (128, 2048), BF16) for i in range(7)]
    f.srt = sb("f_srt", (128, 32))
    f.sel = sb("f_sel", (128, 32))
    f.mcol = sb("f_mcol", (128, 4))
    es_ln = ExitStack()
    f.ln = alloc_ln(nc, es_ln, "fl", with_scores=True, with_xt=False)
    es.enter_context(_CloseOnExit(es_ln))
    ps = lambda n, s, dt=F32: es.enter_context(nc.psum_tensor(n, list(s), dt))
    f.p_g = [ps("fp_g0", (128, 512)), ps("fp_g1", (128, 512))]
    f.p_u = [ps("fp_u0", (128, 512)), ps("fp_u1", (128, 512))]
    return f


class _CloseOnExit:
    def __init__(self, es):
        self.es = es

    def __enter__(self):
        return self

    def __exit__(self, *a):
        self.es.close()
        return False


def emit_ffn(p, cfg, f, x_src, x_dst):
    t, d = p.t, p.t.dram
    T = cfg.T
    n_tt = T // 128
    NB = T // 512

    st = ln_state(p, f.ln)
    b_g = [bank_new(f.p_g[0]), bank_new(f.p_g[1])]
    b_u = [bank_new(f.p_u[0]), bank_new(f.p_u[1])]
    b_d = [bank_new(None), bank_new(None)]

    slots = {"wg": [Slot(p, "g%d" % i) for i in range(2)],
             "wu": [Slot(p, "g%d" % (3 + i)) for i in range(2)],
             "wd": [Slot(p, "g%d" % (6 + i)) for i in range(7)]}
    bufs = {"wg": f.wg, "wu": f.wu, "wd": f.wd}

    def stream(kind, e_i, fc):
        idx = e_i * 6 + fc
        slot = idx % len(bufs[kind])
        buf = bufs[kind][slot]
        if kind == "wd":
            slots[kind][slot].dma("gpsimd", lambda e, buf=buf, e_i=e_i, fc=fc: e.dma_start(
                out=buf[:], in_=d.ffn_wd[e_i, fc]))
        else:
            src = d.ffn_wg if kind == "wg" else d.ffn_wu
            slots[kind][slot].dma("gpsimd", lambda e, buf=buf, src=src, e_i=e_i, fc=fc: e.dma_start(
                out=buf[:].rearrange("p (k m) -> p k m", k=16),
                in_=src[e_i, fc].rearrange("k p m -> p k m")))

    for r in range(n_tt):
        p.emit("sync", lambda e, r=r: e.dma_start(
            out=f.x[:, r, :], in_=x_src[r * 128:(r + 1) * 128, :]), sig="yb", inc=16)
    xld = p.snap("yb")
    p.wait("vector", "yb", xld)
    emit_ln_transpose(p, f.ln, st, None, lambda r: f.x[:, r, :], 0, n_tt, f.xlnT, 0)
    xlnT_done = p.snap("se")

    scores_done = p.snap("se")
    p.wait("vector", "se", scores_done)
    for r in range(n_tt):
        p.drain("vector")
        p.emit("vector", lambda e, r=r: e.tensor_copy(f.srt[:], t.scores[:, r, :]))
        p.drain("vector")
        for it in range(TOP_K):
            p.emit("vector", lambda e: e.reduce_max(out=f.mcol[:, 0:1], in_=f.srt[:], axis=AX.X))
            p.drain("vector")
            if it < TOP_K - 1:
                p.emit("vector", lambda e: e.tensor_scalar(
                    out=f.sel[:], in0=f.srt[:], scalar1=f.mcol[:, 0:1], scalar2=None,
                    op0=ALU.is_ge))
                p.drain("vector")
                p.emit("vector", lambda e: e.scalar_tensor_tensor(
                    out=f.srt[:], in0=f.sel[:], scalar=-1e30, in1=f.srt[:],
                    op0=ALU.mult, op1=ALU.add))
                p.drain("vector")
        p.emit("vector", lambda e, r=r: e.tensor_scalar(
            out=f.sel[:], in0=t.scores[:, r, :], scalar1=f.mcol[:, 0:1], scalar2=None,
            op0=ALU.is_ge))
        p.drain("vector")
        p.emit("vector", lambda e, r=r: e.tensor_tensor(
            out=f.srt[:], in0=f.sel[:], in1=t.scores[:, r, :], op=ALU.mult))
        p.drain("vector")
        p.emit("vector", lambda e: e.tensor_reduce(
            out=f.mcol[:, 2:3], in_=f.srt[:], axis=AX.X, op=ALU.add))
        p.drain("vector")
        p.emit("vector", lambda e: e.tensor_scalar(
            out=f.mcol[:, 2:3], in0=f.mcol[:, 2:3], scalar1=1e-20, scalar2=None, op0=ALU.add))
        p.drain("vector")
        p.emit("vector", lambda e: e.reciprocal(out=f.mcol[:, 3:4], in_=f.mcol[:, 2:3]))
        p.drain("vector")
        p.emit("vector", lambda e, r=r: e.tensor_scalar(
            out=t.combine[:, r, 0:32], in0=f.srt[:], scalar1=f.mcol[:, 3:4], scalar2=None,
            op0=ALU.mult), sig="ve")
    comb_done = p.snap("ve")

    for fc in range(2):
        stream("wg", 0, fc)
        stream("wu", 0, fc)
    for fc in range(6):
        stream("wd", 0, fc)

    sg_free = [0, 0]
    act_free = 0
    for ei in range(34):
        for fc in range(6):
            gslot = (ei * 6 + fc) % 2
            for kind, bpair, wbuf in (("wg", b_g, f.wg[gslot]), ("wu", b_u, f.wu[gslot])):
                for nb in range(NB):
                    bank = bpair[(fc * NB + nb) % 2]
                    p.wait("tensor", bank["rd"][0], bank["rd"][1])
                    slots[kind][gslot].wait_data("tensor")
                    p.wait("tensor", "se", xlnT_done)
                    for dch in range(16):
                        p.emit("tensor", lambda e, wbuf=wbuf, dch=dch, nb=nb, bank=bank: e.matmul(
                            bank["t"][:, 0:512],
                            wbuf[:, dch * 128:(dch + 1) * 128],
                            f.xlnT[:, dch, nb * 512:(nb + 1) * 512],
                            start=(dch == 0), stop=(dch == 15)),
                            sig=("te" if dch == 15 else None))
                    bank["wr"] = ("te", p.snap("te"))
            slots["wg"][gslot].set_free("te")
            slots["wu"][gslot].set_free("te")
            nf = ei * 6 + fc + 2
            if nf < 34 * 6:
                stream("wg", nf // 6, nf % 6)
                stream("wu", nf // 6, nf % 6)
            for nb in range(NB):
                gb = b_g[(fc * NB + nb) % 2]
                ub = b_u[(fc * NB + nb) % 2]
                sgb = f.sg[nb % 2]
                p.wait("scalar", gb["wr"][0], gb["wr"][1])
                p.wait("scalar", "ve", sg_free[nb % 2])
                p.emit("scalar", lambda e, gb=gb, sgb=sgb: e.activation(
                    out=sgb[:], in_=gb["t"][:, 0:512], func=ACT.Silu), sig="se")
                gb["rd"] = ("se", p.snap("se"))
                sv = p.snap("se")
                p.wait("vector", "se", sv)
                p.wait("vector", ub["wr"][0], ub["wr"][1])
                p.wait("vector", "te", act_free)
                p.emit("vector", lambda e, ub=ub, sgb=sgb, fc=fc, nb=nb: e.tensor_tensor(
                    out=f.act[:, fc, nb * 512:(nb + 1) * 512],
                    in0=sgb[:], in1=ub["t"][:, 0:512], op=ALU.mult), sig="ve")
                ub["rd"] = ("ve", p.snap("ve"))
                sg_free[nb % 2] = p.snap("ve")
        act_done = p.snap("ve")

        for r in range(n_tt):
            for h2 in range(2):
                bank = b_d[h2]
                p.wait("tensor", bank["rd"][0], bank["rd"][1])
                p.wait("tensor", b_g[h2]["rd"][0], b_g[h2]["rd"][1])
                p.wait("tensor", b_u[h2]["rd"][0], b_u[h2]["rd"][1])
                p.wait("tensor", "ve", act_done)
                for fc in range(6):
                    dslot = (ei * 6 + fc) % 7
                    slots["wd"][dslot].wait_data("tensor")
                    for nn in range(2):
                        tgt = f.p_g[h2] if nn == 0 else f.p_u[h2]
                        p.emit("tensor", lambda e, fc=fc, r=r, h2=h2, nn=nn, dslot=dslot, tgt=tgt: e.matmul(
                            tgt[:, 0:512],
                            f.act[:, fc, r * 128:(r + 1) * 128],
                            f.wd[dslot][:, h2 * 1024 + nn * 512: h2 * 1024 + (nn + 1) * 512],
                            start=(fc == 0), stop=(fc == 5)),
                            sig=("te" if (fc == 5 and nn == 1) else None))
                bank["wr"] = ("te", p.snap("te"))
                p.wait("vector", bank["wr"][0], bank["wr"][1])
                p.wait("vector", "ve", comb_done)
                p.drain("vector")
                for nn in range(2):
                    src_ps = f.p_g[h2] if nn == 0 else f.p_u[h2]
                    p.emit("vector", lambda e, src_ps=src_ps, r=r, h2=h2, nn=nn, ei=ei: e.scalar_tensor_tensor(
                        out=f.x[:, r, h2 * 1024 + nn * 512: h2 * 1024 + (nn + 1) * 512],
                        in0=src_ps[:, 0:512],
                        scalar=t.combine[:, r, ei:ei + 1],
                        in1=f.x[:, r, h2 * 1024 + nn * 512: h2 * 1024 + (nn + 1) * 512],
                        op0=ALU.mult, op1=ALU.add), sig="ve")
                bank["rd"] = ("ve", p.snap("ve"))
                b_g[h2]["rd"] = ("ve", p.snap("ve"))
                b_u[h2]["rd"] = ("ve", p.snap("ve"))
        for fc in range(6):
            slots["wd"][(ei * 6 + fc) % 7].set_free("te")
        act_free = p.snap("te")
        if ei + 1 < 34:
            for fc in range(6):
                stream("wd", ei + 1, fc)

    xdone = p.snap("ve")
    p.wait("sync", "ve", xdone)
    for r in range(n_tt):
        p.emit("sync", lambda e, r=r: e.dma_start(
            out=x_dst[r * 128:(r + 1) * 128, :], in_=f.x[:, r, :]), sig="yb", inc=16)
    p.wait("sync", "yb")
    p.barrier()


def build_program(cfg):
    nc = bass.Bass("TRN2", target_bir_lowering=False, debug=False)
    dram = declare_io(nc, cfg)
    es_pers = ExitStack()
    pers = alloc_persistent(nc, es_pers, cfg)
    pers.dram = dram

    al = {}
    if "gla" in cfg.phases:
        es_ph = ExitStack()
        al["gla"] = alloc_gla(nc, es_ph, cfg)
        es_ph.close()
    if "ffn1" in cfg.phases or "ffn2" in cfg.phases:
        es_ph = ExitStack()
        al["ffn"] = alloc_ffn(nc, es_ph, cfg)
        es_ph.close()
    if "mamba" in cfg.phases:
        es_ph = ExitStack()
        al["mamba"] = alloc_mamba(nc, es_ph, cfg)
        es_ph.close()

    sems = {}
    sem_es = ExitStack()
    names = ["te", "ve", "se", "gp", "cc", "bar", "gb", "yb"]
    names += ["g%d" % i for i in range(16)]
    names += ["y%d" % i for i in range(16)]
    for name in names:
        sems[name] = sem_es.enter_context(nc.semaphore("s_" + name))

    block_es = ExitStack()
    block = block_es.enter_context(nc.Block())

    def make_section(eng_name):
        def section(e):
            pp = Pass(nc, eng_name, e, sems, pers, al, cfg)
            emit_all(pp, cfg)
        return section

    for eng_name in ENGINES:
        getattr(block, eng_name)(make_section(eng_name))
    block_es.close()
    sem_es.close()
    es_pers.close()
    return nc


def emit_all(p, cfg):
    d = p.t.dram
    emit_setup(p, cfg)
    x_cur = d.x_in
    if "gla" in cfg.phases:
        emit_gla(p, cfg, p.al["gla"], x_cur, d.x_hbm)
        x_cur = d.x_hbm
    if "ffn1" in cfg.phases:
        dst = d.out if cfg.phases[-1] == "ffn1" else d.x_hbm
        emit_ffn(p, cfg, p.al["ffn"], x_cur, dst)
        x_cur = dst
    if "mamba" in cfg.phases:
        emit_mamba(p, cfg, p.al["mamba"], x_cur, d.x_hbm)
        x_cur = d.x_hbm
    if "ffn2" in cfg.phases:
        emit_ffn(p, cfg, p.al["ffn"], x_cur, d.out)
        x_cur = d.out
    if x_cur is not d.out:
        for r in range(cfg.T // 128):
            p.emit("sync", lambda e, r=r, x_cur=x_cur: e.dma_start(
                out=d.out[r * 128:(r + 1) * 128, :], in_=x_cur[r * 128:(r + 1) * 128, :]),
                sig="yb", inc=16)
    p.wait("sync", "yb")
    p.wait("gpsimd", "gb")
    for i in range(16):
        p.wait("gpsimd", "g%d" % i)
    for i in range(16):
        p.wait("sync", "y%d" % i)
    p.wait("gpsimd", "cc")
    p.barrier()


def alloc_gla(nc, es, cfg):
    T = cfg.T
    HT = min(4, T // 128)
    sb = lambda n, s, dt=F32: es.enter_context(nc.sbuf_tensor(n, list(s), dt))
    ps = lambda n, s, dt=F32: es.enter_context(nc.psum_tensor(n, list(s), dt))
    g = NS()
    g.HT = HT
    g.xlnT = sb("g_xlnT", (128, 16, HT * 128), BF16)
    g.q = sb("g_q", (128, 8, T), BF16)
    g.kf = sb("g_kf", (128, 8, HT * 128), BF16)
    g.v = sb("g_v", (128, HT, 2048), BF16)
    g.kg = sb("g_kg", (128, HT, 1024), BF16)
    es_a = ExitStack()
    g.vf = es_a.enter_context(nc.sbuf_tensor("g_vf", [128, 16, HT * 128], BF16))
    es_a.close()
    g.lsig = sb("g_lsig", (128, 1024))
    g.gab = sb("g_gab", (128, 8, 128))
    g.ep = sb("g_ep", (128, 8, 128))
    g.en = sb("g_en", (128, 8, 128))
    g.ktm = sb("g_ktm", (128, 8, 128), BF16)
    g.u32 = sb("g_u32", (128, 8, 128))
    g.u16 = sb("g_u16", (128, 8, 128), BF16)
    g.o = sb("g_o", (128, HT, 2048), BF16)
    g.sgf = sb("g_sgf", (128, 16, HT * 128), BF16)
    g.sr32 = sb("g_sr32", (128, 8, 128))
    g.sr16 = sb("g_sr16", (128, 8, 128), BF16)
    g.zf = sb("g_zf", (16, HT * 128), BF16)
    g.at = [sb("g_at0", (128, 128), BF16), sb("g_at1", (128, 128), BF16)]
    g.ont = [sb("g_ont0", (128, 128), BF16), sb("g_ont1", (128, 128), BF16)]
    g.ecol = sb("g_ecol", (128, 8))
    g.wqk = [sb("g_wqk0", (128, 2048), BF16), sb("g_wqk1", (128, 2048), BF16)]
    g.wkg1 = sb("g_wkg1", (128, 256), BF16)
    g.wkg2 = sb("g_wkg2", (16, 1024), BF16)
    g.pP = [ps("gp_P0", (128, 512)), ps("gp_P1", (128, 512))]
    es_ln = ExitStack()
    g.ln = alloc_ln(nc, es_ln, "gl", with_scores=False, with_xt=True, tr_banks=g.pP)
    es_ln.close()
    g.bns16 = sb("g_bns16", (128, 16, 6))
    g.mv16 = sb("g_mv16", (128, 16, 2))
    g.sd16 = sb("g_sd16", (128, 16))
    g.rs16 = sb("g_rs16", (128, 16))
    g.gnt = sb("g_gnt", (128, 128), BF16)
    g.xst = [sb("g_xst0", (128, 512)), sb("g_xst1", (128, 512))]
    g.ost = [sb("g_ost0", (128, 512)), sb("g_ost1", (128, 512))]
    g.wos = [sb("g_wos0", (128, 512), BF16), sb("g_wos1", (128, 512), BF16)]
    g.pE = ps("gp_E", (128, 512))
    g.pF = ps("gp_F", (128, 512))
    g.pG = ps("gp_G", (128, 512))
    g.pH = ps("gp_H", (128, 512))
    g.pT = [ps("gp_T0", (128, 512), BF16), ps("gp_T1", (128, 512), BF16)]
    return g


def emit_gla(p, cfg, g, x_src, x_dst):
    t, d = p.t, p.t.dram
    T = cfg.T
    HT = g.HT
    HTOK = HT * 128
    n_half = T // HTOK
    st = ln_state(p, g.ln)
    bE = bank_new(g.pE)
    bF = bank_new(g.pF)
    bG = bank_new(g.pG)
    bH = bank_new(g.pH)
    bT = [bank_new(g.pT[0]), bank_new(g.pT[1])]
    bP = [bank_new(g.ln.tr[0]), bank_new(g.ln.tr[1])]
    s_wqk = [Slot(p, "g0"), Slot(p, "g1")]
    s_one = Slot(p, "g2")
    s_cb = Slot(p, "g5")
    s_ost = Slot(p, "y2")
    s_sgst = Slot(p, "y3")
    s_xrt = [Slot(p, "y4"), Slot(p, "y5")]
    s_orl = Slot(p, "y6")
    s_sgrl = Slot(p, "y7")
    s_xout = [Slot(p, "y8"), Slot(p, "y9")]

    p.emit("vector", lambda e: e.memset(g.u32[:], 0.0))
    p.emit("vector", lambda e: e.memset(g.u16[:], 0.0))
    p.emit("vector", lambda e: e.memset(t.prev8[:], 0.0), sig="ve")
    uz = p.snap("ve")
    p.wait("tensor", "ve", uz)
    p.wait("scalar", "ve", uz)
    s_one.dma("gpsimd", lambda e: e.dma_start(out=g.wkg1[:], in_=d.gla_wkg1))
    s_one.dma("gpsimd", lambda e: e.dma_start(out=g.wkg2[:], in_=d.gla_wkg2))
    wsmall = s_one
    s_wo = [Slot(p, "g3"), Slot(p, "g4")]

    o_spill_done = [0, 0]

    for H in range(n_half):
        p.wait("scalar", "te", p.snap("te"))
        p.wait("vector", "te", p.snap("te"))
        p.wait("vector", "se", p.snap("se"))
        emit_ln_transpose(p, g.ln, st, x_src, None, H * HT, HT, g.xlnT, 0)
        xlnT_done = p.snap("se")

        def proj_fm(wsrc, n_m, dst_fn, post):
            for m in range(n_m):
                sl = s_wqk[m % 2]
                sl.dma("gpsimd", lambda e, wsrc=wsrc, m=m, sl=sl: e.dma_start(
                    out=g.wqk[s_wqk.index(sl)][:].rearrange("p (k m) -> p k m", k=16),
                    in_=wsrc[m].rearrange("k p m -> p k m")))
                bank = bP[m % 2]
                p.wait("tensor", bank["rd"][0], bank["rd"][1])
                sl.wait_data("tensor")
                p.wait("tensor", "se", xlnT_done)
                wbuf = g.wqk[m % 2]
                for dch in range(16):
                    for nb in range(HTOK // 512 if HTOK >= 512 else 1):
                        w = min(512, HTOK)
                        p.emit("tensor", lambda e, wbuf=wbuf, dch=dch, nb=nb, w=w, bank=bank: e.matmul(
                            bank["t"][:, 0:w],
                            wbuf[:, dch * 128:(dch + 1) * 128],
                            g.xlnT[:, dch, nb * w:(nb + 1) * w],
                            start=(dch == 0), stop=(dch == 15)),
                            sig=("te" if (dch == 15 and nb == (HTOK - 1) // 512) else None))
                bank["wr"] = ("te", p.snap("te"))
                sl.set_free("te")
                post(m, bank)

        def post_q(m, bank):
            bank_read(p, "vector", bank,
                      lambda e, m=m, bank=bank: e.tensor_copy(
                          g.q[:, m, H * HTOK:(H + 1) * HTOK], bank["t"][:, 0:HTOK]), "ve")

        def post_k(m, bank):
            bank_read(p, "vector", bank,
                      lambda e, m=m, bank=bank: e.tensor_copy(
                          g.kf[:, m, 0:HTOK], bank["t"][:, 0:HTOK]), "ve")

        def post_v(m, bank):
            bank_read(p, "vector", bank,
                      lambda e, m=m, bank=bank: e.tensor_copy(
                          g.vf[:, m, 0:HTOK], bank["t"][:, 0:HTOK]), "ve")

        def post_sg(m, bank):
            p.wait("scalar", "y3", s_sgst.ld)
            bank_read(p, "scalar", bank,
                      lambda e, m=m, bank=bank: e.activation(
                          out=g.sgf[:, m, 0:HTOK], in_=bank["t"][:, 0:HTOK],
                          func=ACT.Silu, bias=t.bg[:, m:m + 1]), "se")

        proj_fm(d.gla_wq, 8, None, post_q)
        proj_fm(d.gla_wk, 8, None, post_k)
        proj_fm(d.gla_wv, 16, None, post_v)
        proj_fm(d.gla_wg, 16, None, post_sg)
        sg_done = p.snap("se")

        p.wait("sync", "se", sg_done)
        s_sgst.dma("sync", lambda e, H=H: e.dma_start(
            out=d.sg_hbm[:, H * HTOK:(H + 1) * HTOK].rearrange("(a p) t -> p a t", p=128),
            in_=g.sgf[:]))

        vtd = p.snap("ve")
        for vm in range(16):
            for r4 in range(HT):
                bank = bT[(vm * HT + r4) % 2]
                p.wait("tensor", bank["rd"][0], bank["rd"][1])
                p.wait("tensor", "ve", vtd)
                p.emit("tensor", lambda e, vm=vm, r4=r4, bank=bank: e.transpose(
                    bank["t"][:, 0:128], g.vf[:, vm, r4 * 128:(r4 + 1) * 128], t.id16[:]),
                    sig="te")
                bank["wr"] = ("te", p.snap("te"))
                bank_read(p, "vector", bank,
                          lambda e, vm=vm, r4=r4, bank=bank: e.tensor_copy(
                              g.v[:, r4, vm * 128:(vm + 1) * 128], bank["t"][:, 0:128]), "ve")
        v_done = p.snap("ve")

        p.wait("tensor", bH["rd"][0], bH["rd"][1])
        wsmall.wait_data("tensor")
        p.wait("tensor", "se", xlnT_done)
        for dch in range(16):
            p.emit("tensor", lambda e, dch=dch: e.matmul(
                g.pH[0:16, 0:HTOK], g.wkg1[:, dch * 16:(dch + 1) * 16],
                g.xlnT[:, dch, 0:HTOK],
                start=(dch == 0), stop=(dch == 15)), sig=("te" if dch == 15 else None))
        bH["wr"] = ("te", p.snap("te"))
        bank_read(p, "vector", bH,
                  lambda e: e.tensor_copy(g.zf[:, 0:HTOK], g.pH[0:16, 0:HTOK]), "ve")
        z_done = p.snap("ve")
        bkg_banks = [bE, bF]
        for r4 in range(HT):
            for nch in range(2):
                bank = bkg_banks[nch]
                p.wait("tensor", bank["rd"][0], bank["rd"][1])
                p.wait("tensor", "ve", z_done)
                p.emit("tensor", lambda e, r4=r4, nch=nch, bank=bank: e.matmul(
                    bank["t"][:, 0:512], g.zf[:, r4 * 128:(r4 + 1) * 128],
                    g.wkg2[:, nch * 512:(nch + 1) * 512],
                    start=True, stop=True), sig="te")
                bank["wr"] = ("te", p.snap("te"))
                bank_read(p, "vector", bank,
                          lambda e, r4=r4, nch=nch, bank=bank: e.tensor_tensor(
                              out=g.kg[:, r4, nch * 512:(nch + 1) * 512],
                              in0=bank["t"][:, 0:512],
                              in1=t.bkg2[:, nch * 512:(nch + 1) * 512], op=ALU.add), "ve")
        kg_done = p.snap("ve")

        for c in range(HT):
            gch = H * HT + c
            p.drain("vector")
            p.wait("vector", "se", p.snap("se"))
            p.wait("scalar", "ve", kg_done)
            p.wait("scalar", "te", p.snap("te"))
            p.emit("scalar", lambda e, c=c: e.activation(
                out=g.lsig[:], in_=g.kg[:, c, :], func=ACT.Sigmoid), sig="se")
            p.drain("scalar")
            p.emit("scalar", lambda e: e.activation(
                out=g.lsig[:], in_=g.lsig[:], func=ACT.Ln), sig="se")
            lsig_done = p.snap("se")
            for m in range(8):
                p.wait("tensor", bH["rd"][0], bH["rd"][1])
                p.wait("tensor", "se", lsig_done)
                p.emit("tensor", lambda e, m=m: e.matmul(
                    g.pH[:, 0:128], g.lsig[:, m * 128:(m + 1) * 128], t.tri[:],
                    start=True, stop=True), sig="te")
                bH["wr"] = ("te", p.snap("te"))
                bank_read(p, "vector", bH,
                          lambda e, m=m: e.tensor_scalar(
                              out=g.gab[:, m, :], in0=g.pH[:, 0:128],
                              scalar1=t.prev8[:, m:m + 1], scalar2=None, op0=ALU.add), "ve")
            gab_done = p.snap("ve")
            p.wait("scalar", "ve", gab_done)
            p.emit("scalar", lambda e: e.copy(out=t.prev8[:], in_=g.gab[:, :, 127:128]), sig="se")
            p.emit("scalar", lambda e: e.activation(
                out=g.ep[:].rearrange("p a b -> p (a b)"),
                in_=g.gab[:].rearrange("p a b -> p (a b)"), func=ACT.Exp, scale=G16), sig="se")
            p.emit("scalar", lambda e: e.activation(
                out=g.en[:].rearrange("p a b -> p (a b)"),
                in_=g.gab[:].rearrange("p a b -> p (a b)"), func=ACT.Exp, scale=-G16,
                bias=t.lnscol[:, 0:1]), sig="se")
            exps_done = p.snap("se")
            p.wait("vector", "se", exps_done)
            p.emit("vector", lambda e, gch=gch: e.tensor_tensor(
                out=g.q[:, :, gch * 128:(gch + 1) * 128], in0=g.q[:, :, gch * 128:(gch + 1) * 128],
                in1=g.ep[:], op=ALU.mult), sig="ve")
            p.emit("vector", lambda e, c=c: e.tensor_tensor(
                out=g.kf[:, :, c * 128:(c + 1) * 128], in0=g.kf[:, :, c * 128:(c + 1) * 128],
                in1=g.en[:], op=ALU.mult), sig="ve")
            qk_done = p.snap("ve")
            for m in range(8):
                bank = bT[m % 2]
                p.wait("tensor", bank["rd"][0], bank["rd"][1])
                p.wait("tensor", "ve", qk_done)
                p.emit("tensor", lambda e, m=m, c=c, bank=bank: e.transpose(
                    bank["t"][:, 0:128], g.kf[:, m, c * 128:(c + 1) * 128], t.id16[:]), sig="te")
                bank["wr"] = ("te", p.snap("te"))
                bank_read(p, "vector", bank,
                          lambda e, m=m, bank=bank: e.tensor_copy(
                              g.ktm[:, m, :], bank["t"][:, 0:128]), "ve")
            ktm_done = p.snap("ve")

            u16_done = p.snap("se")
            for h in range(16):
                b = (h % 2) * 64
                hm = h // 2
                p.wait("tensor", bE["rd"][0], bE["rd"][1])
                p.wait("tensor", "ve", qk_done)
                p.emit("tensor", lambda e, b=b, hm=hm, c=c, gch=gch: e.matmul(
                    g.pE[:, 0:128], g.kf[b:b + 64, hm, c * 128:(c + 1) * 128],
                    g.q[b:b + 64, hm, gch * 128:(gch + 1) * 128],
                    start=True, stop=True), sig="te")
                bE["wr"] = ("te", p.snap("te"))
                atb = g.at[h % 2]
                p.wait("vector", "te", getattr(st, "at_free", [0, 0])[h % 2] if hasattr(st, "at_free") else 0)
                bank_read(p, "vector", bE,
                          lambda e, atb=atb: e.tensor_tensor(
                              out=atb[:], in0=g.pE[:, 0:128], in1=t.tri[:], op=ALU.mult), "ve")
                at_done = p.snap("ve")
                if not hasattr(st, "at_free"):
                    st.at_free = [0, 0]
                p.wait("tensor", bF["rd"][0], bF["rd"][1])
                p.wait("tensor", "ve", at_done)
                p.wait("tensor", "ve", v_done)
                p.wait("tensor", "se", u16_done)
                p.emit("tensor", lambda e, atb=atb, c=c, h=h: e.matmul(
                    g.pF[:, 0:128], atb[:], g.v[:, c, h * 128:(h + 1) * 128],
                    start=True, stop=False))
                p.emit("tensor", lambda e, b=b, hm=hm, gch=gch: e.matmul(
                    g.pF[:, 0:128], g.q[b:b + 64, hm, gch * 128:(gch + 1) * 128],
                    g.u16[b:b + 64, hm, :], start=False, stop=True), sig="te")
                bF["wr"] = ("te", p.snap("te"))
                st.at_free[h % 2] = p.snap("te")
                p.wait("scalar", "y2", s_ost.ld)
                bank_read(p, "scalar", bF,
                          lambda e, c=c, h=h: e.copy(
                              out=g.o[:, c, h * 128:(h + 1) * 128], in_=g.pF[:, 0:128]), "se")
                p.wait("tensor", bG["rd"][0], bG["rd"][1])
                p.wait("tensor", "ve", ktm_done)
                p.emit("tensor", lambda e, b=b, hm=hm, c=c, h=h: e.matmul(
                    g.pG[b:b + 64, 0:128], g.ktm[:, hm, b:b + 64],
                    g.v[:, c, h * 128:(h + 1) * 128], start=True, stop=True), sig="te")
                bG["wr"] = ("te", p.snap("te"))
                bank_read(p, "vector", bG,
                          lambda e, b=b, hm=hm: e.tensor_tensor(
                              out=g.u32[b:b + 64, hm, :], in0=g.u32[b:b + 64, hm, :],
                              in1=g.pG[b:b + 64, 0:128], op=ALU.add), "ve")
                uadd = p.snap("ve")
                p.wait("scalar", "ve", uadd)
                p.emit("scalar", lambda e, b=b, hm=hm: e.copy(
                    out=g.u16[b:b + 64, hm, :], in_=g.u32[b:b + 64, hm, :]), sig="se")
                u16_done = p.snap("se")

        odone = p.snap("se")
        p.wait("sync", "se", odone)
        p.wait("sync", "ve", p.snap("ve"))
        s_ost.dma("sync", lambda e, H=H: e.dma_start(
            out=d.o_hbm[H * HTOK:(H + 1) * HTOK, :].rearrange("(a p) n -> p a n", p=128),
            in_=g.o[:]))
        o_spill_done[H] = s_ost.ld

    lastg = p.snap("ve")
    p.wait("scalar", "ve", lastg)
    p.emit("scalar", lambda e: e.activation(
        out=g.ecol[:], in_=g.gab[:, :, 127:128], func=ACT.Exp, scale=G16), sig="se")
    ecol_done = p.snap("se")
    p.wait("vector", "se", ecol_done)
    p.drain("vector")
    for hm in range(8):
        p.emit("vector", lambda e, hm=hm: e.tensor_scalar(
            out=g.ep[:, hm, :], in0=g.u32[:, hm, :], scalar1=g.ecol[:, hm:hm + 1],
            scalar2=None, op0=ALU.mult), sig="ve")
    snd_done = p.snap("ve")
    p.wait("gpsimd", "ve", snd_done)
    s_cb.dma("gpsimd", lambda e: e.dma_start(
        out=d.cb1_in.rearrange("(a p) v -> p a v", p=128), in_=g.ep[:]))
    s_cb.wait_data("gpsimd")
    p.emit("gpsimd", lambda e: e.collective_compute(
        "AllGather", ALU.bypass,
        replica_groups=[[0, 1], [2, 3], [4, 5], [6, 7]],
        ins=[d.cb1_in.opt()], outs=[d.cb1_out.opt()]), sig="cc")
    p.wait("gpsimd", "cc")
    s_cb.dma("gpsimd", lambda e: e.dma_start(
        out=g.sr32[:], in_=d.cb1_out[0:1024, :].rearrange("(a p) v -> p a v", p=128)))
    s_cb.wait_data("vector")
    p.drain("vector")
    p.emit("vector", lambda e: e.tensor_scalar(
        out=g.sr32[:].rearrange("p a b -> p (a b)"),
        in0=g.sr32[:].rearrange("p a b -> p (a b)"),
        scalar1=t.odd[:, 0:1], scalar2=None, op0=ALU.mult), sig="ve")
    srm = p.snap("ve")
    p.wait("scalar", "ve", srm)
    p.emit("scalar", lambda e: e.copy(
        out=g.sr16[:].rearrange("p a b -> p (a b)"),
        in_=g.sr32[:].rearrange("p a b -> p (a b)")), sig="se")
    sr_done = p.snap("se")

    p.barrier()
    s_orl.free = ("y2", s_ost.ld)
    s_sgrl.free = ("y3", s_sgst.ld)
    for H in range(n_half):
        s_orl.dma("sync", lambda e, H=H: e.dma_start(
            out=g.o[:], in_=d.o_hbm[H * HTOK:(H + 1) * HTOK, :].rearrange("(a p) n -> p a n", p=128)))
        s_sgrl.dma("sync", lambda e, H=H: e.dma_start(
            out=g.sgf[:], in_=d.sg_hbm[:, H * HTOK:(H + 1) * HTOK].rearrange("(a p) t -> p a t", p=128)))
        s_orl.wait_data("vector")
        s_sgrl.wait_data("vector")
        orl_done = p.snap("ve")

        for c in range(HT):
            gch = H * HT + c
            for h in range(16):
                b = (h % 2) * 64
                hm = h // 2
                p.wait("tensor", bE["rd"][0], bE["rd"][1])
                p.wait("tensor", "se", sr_done)
                p.emit("tensor", lambda e, b=b, hm=hm, gch=gch: e.matmul(
                    g.pE[:, 0:128], g.q[b:b + 64, hm, gch * 128:(gch + 1) * 128],
                    g.sr16[b:b + 64, hm, :], start=True, stop=True), sig="te")
                bE["wr"] = ("te", p.snap("te"))
                s_orl.wait_data("vector")
                bank_read(p, "vector", bE,
                          lambda e, c=c, h=h: e.tensor_tensor(
                              out=g.o[:, c, h * 128:(h + 1) * 128],
                              in0=g.pE[:, 0:128], in1=g.o[:, c, h * 128:(h + 1) * 128],
                              op=ALU.add), "ve")
            corr_done = p.snap("ve")

            p.wait("vector", "ve", corr_done)
            p.drain("vector")
            for h in range(16):
                p.emit("vector", lambda e, c=c, h=h: e.bn_stats(
                    out=g.bns16[:, h, :], in_=g.o[:, c, h * 128:(h + 1) * 128]))
            p.drain("vector")
            for h in range(16):
                p.emit("vector", lambda e, h=h: e.bn_aggr(
                    out=g.mv16[:, h, :], in_=g.bns16[:, h, 0:6]))
            p.drain("vector")
            p.emit("vector", lambda e: e.tensor_copy(g.sd16[:], g.mv16[:, :, 1:2]), sig="ve")
            gn1 = p.snap("ve")
            p.wait("scalar", "ve", gn1)
            p.emit("scalar", lambda e: e.activation(
                out=g.sd16[:], in_=g.sd16[:], func=ACT.Sqrt, bias=t.epscol[:, 0:1]), sig="se")
            gn2 = p.snap("se")
            p.wait("vector", "se", gn2)
            p.emit("vector", lambda e: e.reciprocal(out=g.rs16[:], in_=g.sd16[:]))
            p.drain("vector")
            gnt_free = 0
            ont_free = [0, 0]
            for h in range(16):
                p.wait("vector", "te", gnt_free)
                p.emit("vector", lambda e, c=c, h=h: e.tensor_scalar(
                    out=g.gnt[:], in0=g.o[:, c, h * 128:(h + 1) * 128],
                    scalar1=g.mv16[:, h, 0:1], scalar2=g.rs16[:, h:h + 1],
                    op0=ALU.subtract, op1=ALU.mult), sig="ve")
                onr = p.snap("ve")
                bank = bT[h % 2]
                p.wait("tensor", bank["rd"][0], bank["rd"][1])
                p.wait("tensor", "ve", onr)
                p.emit("tensor", lambda e, bank=bank: e.transpose(
                    bank["t"][:, 0:128], g.gnt[:], t.id16[:]), sig="te")
                bank["wr"] = ("te", p.snap("te"))
                gnt_free = p.snap("te")
                ontb = g.ont[h % 2]
                p.wait("vector", "ve", ont_free[h % 2])
                bank_read(p, "vector", bank,
                          lambda e, ontb=ontb, bank=bank: e.tensor_copy(ontb[:], bank["t"][:, 0:128]), "ve")
                p.drain("vector")
                p.emit("vector", lambda e, ontb=ontb, c=c, h=h: e.tensor_tensor(
                    out=g.sgf[:, h, c * 128:(c + 1) * 128], in0=ontb[:],
                    in1=g.sgf[:, h, c * 128:(c + 1) * 128], op=ALU.mult), sig="ve")
                ont_free[h % 2] = p.snap("ve")
            p_done = p.snap("ve")

        p_done_all = p.snap("ve")
        cbanks = [bank_new(g.pE), bank_new(g.pF), bank_new(g.pG), bank_new(g.pH)]
        cbanks[0]["rd"] = bE["rd"]
        cbanks[1]["rd"] = bF["rd"]
        cbanks[2]["rd"] = bG["rd"]
        cbanks[3]["rd"] = bH["rd"]
        for nch in range(4):
            for dch in range(16):
                wsl = s_wo[(nch * 16 + dch) % 2]
                wsl.dma("gpsimd", lambda e, dch=dch, nch=nch, wsl=wsl: e.dma_start(
                    out=g.wos[s_wo.index(wsl)][:],
                    in_=d.gla_wo[dch, :, nch * 512:(nch + 1) * 512]))
                for c in range(HT):
                    bank = cbanks[c]
                    if dch == 0:
                        p.wait("tensor", bank["rd"][0], bank["rd"][1])
                        p.wait("tensor", "ve", p_done_all)
                    wsl.wait_data("tensor")
                    p.emit("tensor", lambda e, c=c, dch=dch, wsl=wsl, bank=bank: e.matmul(
                        bank["t"][:, 0:512], g.sgf[:, dch, c * 128:(c + 1) * 128],
                        g.wos[s_wo.index(wsl)][:],
                        start=(dch == 0), stop=(dch == 15)),
                        sig=("te" if c == HT - 1 else None))
                if dch == 15:
                    for c in range(HT):
                        cbanks[c]["wr"] = ("te", p.snap("te"))
                wsl.set_free("te")
            for c in range(HT):
                gch = H * HT + c
                xsl = s_xrt[c % 2]
                xsl.dma("sync", lambda e, gch=gch, nch=nch, c=c: e.dma_start(
                    out=g.xst[c % 2][:],
                    in_=x_src[gch * 128:(gch + 1) * 128, nch * 512:(nch + 1) * 512]))
                xsl.wait_data("vector")
                bank = cbanks[c]
                p.wait("vector", bank["wr"][0], bank["wr"][1])
                p.wait("vector", "y8" if c % 2 == 0 else "y9", s_xout[c % 2].ld)
                p.emit("vector", lambda e, c=c, bank=bank: e.tensor_tensor(
                    out=g.ost[c % 2][:], in0=bank["t"][:, 0:512],
                    in1=g.xst[c % 2][:], op=ALU.add), sig="ve")
                bank["rd"] = ("ve", p.snap("ve"))
                xsl.set_free("ve")
                xo_done = p.snap("ve")
                p.wait("sync", "ve", xo_done)
                s_xout[c % 2].dma("sync", lambda e, gch=gch, nch=nch, c=c: e.dma_start(
                    out=x_dst[gch * 128:(gch + 1) * 128, nch * 512:(nch + 1) * 512],
                    in_=g.ost[c % 2][:]))
        bE["rd"] = cbanks[0]["rd"]
        bF["rd"] = cbanks[1]["rd"]
        bG["rd"] = cbanks[2]["rd"]
        bH["rd"] = cbanks[3]["rd"]
        bE["wr"] = cbanks[0]["wr"]
        bF["wr"] = cbanks[1]["wr"]
        bG["wr"] = cbanks[2]["wr"]
        bH["wr"] = cbanks[3]["wr"]
        s_orl.free = ("ve", p.snap("ve"))
        s_sgrl.free = ("te", p.snap("te"))
    p.wait("sync", "y8", s_xout[0].ld)
    p.wait("sync", "y9", s_xout[1].ld)
    p.barrier()


def alloc_mamba(nc, es, cfg):
    T = cfg.T
    n_tt = T // 128
    sb = lambda n, s, dt=F32: es.enter_context(nc.sbuf_tensor(n, list(s), dt))
    ps = lambda n, s, dt=F32: es.enter_context(nc.psum_tensor(n, list(s), dt))
    m = NS()
    m.xlnT = sb("m_xlnT", (128, 16, T), BF16)
    m.S32 = sb("m_S32", (128, 32, 128))
    m.S16 = sb("m_S16", (128, 32, 128), BF16)
    m.strip = [sb("m_st0", (128, 1024), BF16), sb("m_st1", (128, 1024), BF16)]
    m.gstrip = [sb("m_gs0", (128, 1024), BF16), sb("m_gs1", (128, 1024), BF16)]
    m.tails = sb("m_tails", (128, 34, 3), BF16)
    m.dts = [sb("m_dts0", (128, 32)), sb("m_dts1", (128, 32)), sb("m_dts2", (128, 32))]
    m.wtile = [sb("m_wt0", (128, 2048), BF16), sb("m_wt1", (128, 2048), BF16)]
    es_b = ExitStack()
    sbb = lambda n, s, dt=F32: es_b.enter_context(nc.sbuf_tensor(n, list(s), dt))
    m.hbc = sbb("m_hbc", (128, 34, 259), BF16)
    m.xs = sbb("m_xs", (128, 2, 4096), BF16)
    m.yq = sbb("m_yq", (128, 2, 4096), BF16)
    m.acf = sbb("m_acf", (32, 128))
    m.ac32 = sbb("m_ac32", (128, 8, 32))
    m.btm = sbb("m_btm", (128, 128), BF16)
    m.bdec = sbb("m_bdec", (128, 128), BF16)
    m.xdt = sbb("m_xdt", (128, 128), BF16)
    m.gtri = sbb("m_gtri", (128, 128), BF16)
    m.dsl = sbb("m_dsl", (128, 128))
    m.esl = sbb("m_esl", (128, 128), BF16)
    m.msl = sbb("m_msl", (128, 128), BF16)
    m.ytmp = sbb("m_ytmp", (128, 128))
    m.ytmp2 = sbb("m_ytmp2", (128, 128))
    m.cva = sbb("m_cva", (128, 256))
    m.cvb = sbb("m_cvb", (128, 256))
    m.cvc = sbb("m_cvc", (128, 256))
    es_b.close()
    es_t = ExitStack()
    sbt = lambda n, s, dt=F32: es_t.enter_context(nc.sbuf_tensor(n, list(s), dt))
    m.yt = sbt("m_yt", (128, 2, 4096), BF16)
    m.sgt = sbt("m_sgt", (128, 2, 4096), BF16)
    m.ynT = sbt("m_ynT", (128, 32, 256), BF16)
    m.xst = [sbt("m_xst0", (128, 512)), sbt("m_xst1", (128, 512))]
    m.ost = [sbt("m_ost0", (128, 512)), sbt("m_ost1", (128, 512))]
    m.wos = [sbt("m_wos0", (128, 512), BF16), sbt("m_wos1", (128, 512), BF16)]
    m.ssq = sbt("m_ssq", (128, 4))
    es_t.close()
    es_ln = ExitStack()
    m.ln = alloc_ln(nc, es_ln, "ml", with_scores=False, with_xt=True)
    es.enter_context(_CloseOnExit(es_ln))
    m.pA = ps("mp_A", (128, 512))
    m.pB = ps("mp_B", (128, 512))
    m.pC = ps("mp_C", (128, 512))
    m.pD = ps("mp_D", (128, 512))
    m.pT = [ps("mp_T0", (128, 512), BF16), ps("mp_T1", (128, 512), BF16)]
    return m


def emit_mamba(p, cfg, m, x_src, x_dst):
    t, d = p.t, p.t.dram
    T = cfg.T
    n_tt = T // 128
    NB = T // 512
    NQ = T // 256
    st = ln_state(p, m.ln)
    bA = bank_new(m.pA)
    bB = bank_new(m.pB)
    bC = bank_new(m.pC)
    bD = bank_new(m.pD)
    bT = [bank_new(m.pT[0]), bank_new(m.pT[1])]
    bP = [bank_new(m.ln.tr[0]), bank_new(m.ln.tr[1])]
    s_w = [Slot(p, "g0"), Slot(p, "g1")]
    s_acr = Slot(p, "g7")
    s_csp = Slot(p, "g8")
    s_cb = Slot(p, "g5")
    s_cb2 = Slot(p, "g9")
    s_hst = [Slot(p, "y2"), Slot(p, "y3")]
    s_gst = [Slot(p, "y10"), Slot(p, "y11")]
    s_yst = Slot(p, "y6")
    s_hrl = Slot(p, "y7")
    s_yrl = Slot(p, "y8")
    s_sgrl = [Slot(p, "y9"), Slot(p, "y12")]
    s_xst = [Slot(p, "y4"), Slot(p, "y5")]
    s_xout = [Slot(p, "y13"), Slot(p, "y14")]
    s_wo = [Slot(p, "g3"), Slot(p, "g4")]

    p.emit("vector", lambda e: e.memset(m.S32[:], 0.0))
    p.emit("vector", lambda e: e.memset(m.S16[:], 0.0))
    p.emit("vector", lambda e: e.memset(m.ac32[:, 4, :], 0.0), sig="ve")
    sz = p.snap("ve")
    p.wait("tensor", "ve", sz)
    p.wait("scalar", "ve", sz)

    for part in range(NB - 1, -1, -1):
        emit_ln_transpose(p, m.ln, st, x_src, None, part * 4, 4, m.xlnT, part * 512)
    xlnT_done = p.snap("se")

    ns_order = list(range(NB - 1, -1, -1))
    hprog = 0
    for ct in range(34):
        sl = s_w[ct % 2]
        sl.dma("gpsimd", lambda e, ct=ct, sl=sl: e.dma_start(
            out=m.wtile[s_w.index(sl)][:].rearrange("p (k m) -> p k m", k=16),
            in_=d.m_win_hbc[ct].rearrange("k p m -> p k m")))
        for ns in ns_order:
            bank = bP[(ct * NB + ns) % 2]
            p.wait("tensor", bank["rd"][0], bank["rd"][1])
            sl.wait_data("tensor")
            p.wait("tensor", "se", xlnT_done)
            wbuf = m.wtile[ct % 2]
            for dch in range(16):
                p.emit("tensor", lambda e, wbuf=wbuf, dch=dch, ns=ns, bank=bank: e.matmul(
                    bank["t"][:, 0:512], wbuf[:, dch * 128:(dch + 1) * 128],
                    m.xlnT[:, dch, ns * 512:(ns + 1) * 512],
                    start=(dch == 0), stop=(dch == 15)),
                    sig=("te" if dch == 15 else None))
            bank["wr"] = ("te", p.snap("te"))
            stn = (ct * NB + ns) % 2
            p.wait("vector", "y2" if stn == 0 else "y3", s_hst[stn].ld)
            bank_read(p, "vector", bank,
                      lambda e, stn=stn, bank=bank: e.tensor_copy(
                          m.strip[stn][:, 0:512], bank["t"][:, 0:512]), "ve")
            sdone = p.snap("ve")
            p.wait("sync", "ve", sdone)
            s_hst[stn].dma("sync", lambda e, ct=ct, ns=ns, stn=stn: e.dma_start(
                out=d.hbc_hbm[ct, :, 3 + ns * 512: 3 + (ns + 1) * 512], in_=m.strip[stn][:, 0:512]))
        sl.set_free("te")
        if ct == 33:
            pass
    hbc_done_y2 = s_hst[0].ld
    hbc_done_y3 = s_hst[1].ld

    p.wait("gpsimd", "y2", hbc_done_y2)
    p.wait("gpsimd", "y3", hbc_done_y3)
    s_cb2.dma("gpsimd", lambda e: e.dma_start(
        out=d.cb2_in.rearrange("(c p) k -> c p k", p=128),
        in_=d.hbc_hbm[:, :, 3 + T - 3: 3 + T]))
    s_cb2.wait_data("gpsimd")
    p.emit("gpsimd", lambda e: e.collective_compute(
        "AllGather", ALU.bypass,
        replica_groups=[[0, 1], [2, 3], [4, 5], [6, 7]],
        ins=[d.cb2_in.opt()], outs=[d.cb2_out.opt()]), sig="cc")
    p.wait("gpsimd", "cc")
    s_cb2.dma("gpsimd", lambda e: e.dma_start(
        out=m.tails[:].rearrange("p c k -> p c k"),
        in_=d.cb2_out[0:34 * 128].rearrange("(c p) k -> p c k", p=128)))
    s_cb2.wait_data("vector")
    p.drain("vector")
    p.emit("vector", lambda e: e.tensor_scalar(
        out=m.tails[:], in0=m.tails[:], scalar1=t.odd[:, 0:1], scalar2=None,
        op0=ALU.mult), sig="ve")
    tl = p.snap("ve")
    p.wait("sync", "ve", tl)
    s_hst[0].dma("sync", lambda e: e.dma_start(
        out=d.hbc_hbm[:, :, 0:3].rearrange("c p k -> p c k"), in_=m.tails[:]))
    tails_st = s_hst[0].ld

    for mt in range(32):
        sl = s_w[mt % 2]
        sl.dma("gpsimd", lambda e, mt=mt, sl=sl: e.dma_start(
            out=m.wtile[s_w.index(sl)][:].rearrange("p (k m) -> p k m", k=16),
            in_=d.m_win_gate[mt].rearrange("k p m -> p k m")))
        for ns in range(NB):
            bank = bP[(mt * NB + ns) % 2]
            p.wait("tensor", bank["rd"][0], bank["rd"][1])
            sl.wait_data("tensor")
            for dch in range(16):
                p.emit("tensor", lambda e, mt=mt, dch=dch, ns=ns, bank=bank: e.matmul(
                    bank["t"][:, 0:512], m.wtile[mt % 2][:, dch * 128:(dch + 1) * 128],
                    m.xlnT[:, dch, ns * 512:(ns + 1) * 512],
                    start=(dch == 0), stop=(dch == 15)),
                    sig=("te" if dch == 15 else None))
            bank["wr"] = ("te", p.snap("te"))
            stn = (mt * NB + ns) % 2
            p.wait("scalar", "y10" if stn == 0 else "y11", s_gst[stn].ld)
            bank_read(p, "scalar", bank,
                      lambda e, stn=stn, bank=bank: e.activation(
                          out=m.gstrip[stn][:, 0:512], in_=bank["t"][:, 0:512],
                          func=ACT.Silu), "se")
            gdone = p.snap("se")
            p.wait("sync", "se", gdone)
            s_gst[stn].dma("sync", lambda e, mt=mt, ns=ns, stn=stn: e.dma_start(
                out=d.sgate_hbm[mt * 128:(mt + 1) * 128, ns * 512:(ns + 1) * 512],
                in_=m.gstrip[stn][:, 0:512]))
        sl.set_free("te")

    for r in range(n_tt):
        bank = bC if r % 2 == 0 else bD
        p.wait("tensor", bank["rd"][0], bank["rd"][1])
        wsl = s_w[r % 2]
        wsl.dma("gpsimd", lambda e, wsl=wsl: e.dma_start(
            out=m.wtile[s_w.index(wsl)][:, 0:512], in_=d.m_win_dt))
        wsl.wait_data("tensor")
        p.wait("tensor", "se", xlnT_done)
        for dch in range(16):
            p.emit("tensor", lambda e, r=r, dch=dch, wsl=wsl, bank=bank: e.matmul(
                bank["t"][:, 0:32], m.xlnT[:, dch, r * 128:(r + 1) * 128],
                m.wtile[s_w.index(wsl)][:, dch * 32:(dch + 1) * 32],
                start=(dch == 0), stop=(dch == 15)),
                sig=("te" if dch == 15 else None))
        bank["wr"] = ("te", p.snap("te"))
        wsl.set_free("te")
        bank_read(p, "vector", bank,
                  lambda e, bank=bank: e.tensor_tensor(
                      out=m.dts[0][:], in0=bank["t"][:, 0:32], in1=t.dtbias[:], op=ALU.add), "ve")
        z1 = p.snap("ve")
        p.wait("scalar", "ve", z1)
        p.emit("scalar", lambda e: e.activation(
            out=m.dts[1][:], in_=m.dts[0][:], func=ACT.Sigmoid, scale=-1.0), sig="se")
        p.drain("scalar")
        p.emit("scalar", lambda e: e.activation(
            out=m.dts[2][:], in_=m.dts[1][:], func=ACT.Ln), sig="se")
        z3 = p.snap("se")
        p.wait("vector", "se", z3)
        p.drain("vector")
        p.emit("vector", lambda e, r=r: e.tensor_scalar(
            out=t.dt_tm[:, r, :], in0=m.dts[2][:], scalar1=-1.0, scalar2=100.0,
            op0=ALU.mult, op1=ALU.min), sig="ve")
    dt_done = p.snap("ve")

    for q in range(NQ):
        if q == 0:
            s_hrl.free = ("y2", s_hst[0].ld)
        s_hrl.dma("sync", lambda e, q=q: e.dma_start(
            out=m.hbc[:], in_=d.hbc_hbm[:, :, q * 256: q * 256 + 259].rearrange("c p k -> p c k")))
        s_hrl.wait_data("vector")
        s_hrl.wait_data("scalar")
        hrl_ld = p.snap("ve")

        for ct in range(34):
            p.wait("scalar", "ve", getattr(st, "cv_free", 0))
            p.emit("scalar", lambda e, ct=ct: e.activation(
                out=m.cva[:], in_=m.hbc[:, ct, 0:256],
                func=ACT.Copy, scale=t.convw[:, ct * 4: ct * 4 + 1], bias=0.0), sig="se")
            tap0 = p.snap("se")
            p.wait("vector", "se", tap0)
            p.emit("vector", lambda e, ct=ct: e.scalar_tensor_tensor(
                out=m.cvb[:], in0=m.hbc[:, ct, 1:257], scalar=t.convw[:, ct * 4 + 1: ct * 4 + 2],
                in1=m.cva[:], op0=ALU.mult, op1=ALU.add), sig="ve")
            p.drain("vector")
            p.emit("vector", lambda e, ct=ct: e.scalar_tensor_tensor(
                out=m.cvc[:], in0=m.hbc[:, ct, 2:258], scalar=t.convw[:, ct * 4 + 2: ct * 4 + 3],
                in1=m.cvb[:], op0=ALU.mult, op1=ALU.add), sig="ve")
            p.drain("vector")
            p.emit("vector", lambda e, ct=ct: e.scalar_tensor_tensor(
                out=m.cvb[:], in0=m.hbc[:, ct, 3:259], scalar=t.convw[:, ct * 4 + 3: ct * 4 + 4],
                in1=m.cvc[:], op0=ALU.mult, op1=ALU.add), sig="ve")
            cv = p.snap("ve")
            p.wait("scalar", "ve", cv)
            p.emit("scalar", lambda e, ct=ct: e.activation(
                out=m.hbc[:, ct, 3:259], in_=m.cvb[:], func=ACT.Silu,
                bias=t.convb[:, ct:ct + 1]), sig="se")
            st.cv_free = p.snap("se")
        conv_done = p.snap("se")

        p.wait("vector", "se", conv_done)
        for c2 in range(2):
            p.emit("vector", lambda e, q=q, c2=c2: e.tensor_copy(
                t.c_keep[:, (q * 2 + c2) * 128:(q * 2 + c2 + 1) * 128],
                m.hbc[:, 33, 3 + c2 * 128: 3 + (c2 + 1) * 128]), sig="ve")
        for ct in range(32):
            for c2 in range(2):
                bank = bT[(ct * 2 + c2) % 2]
                p.wait("tensor", bank["rd"][0], bank["rd"][1])
                p.wait("tensor", "se", conv_done)
                p.emit("tensor", lambda e, ct=ct, c2=c2, bank=bank: e.transpose(
                    bank["t"][:, 0:128], m.hbc[:, ct, 3 + c2 * 128: 3 + (c2 + 1) * 128],
                    t.id16[:]), sig="te")
                bank["wr"] = ("te", p.snap("te"))
                bank_read(p, "vector", bank,
                          lambda e, ct=ct, c2=c2, bank=bank: e.tensor_copy(
                              m.xs[:, c2, ct * 128:(ct + 1) * 128], bank["t"][:, 0:128]), "ve")
        xs_done = p.snap("ve")

        for c2 in range(2):
            gch = q * 2 + c2
            s16_prev = p.snap("se")
            p.wait("vector", "ve", dt_done)
            p.drain("vector")
            p.emit("vector", lambda e, gch=gch: e.scalar_tensor_tensor(
                out=m.ac32[:, 0, :], in0=t.dt_tm[:, gch, :], scalar=-1.0, in1=t.apos[:],
                op0=ALU.mult, op1=ALU.mult), sig="ve")
            ad_done = p.snap("ve")
            p.wait("tensor", bC["rd"][0], bC["rd"][1])
            p.wait("tensor", bD["rd"][0], bD["rd"][1])
            p.wait("tensor", "ve", ad_done)
            p.emit("tensor", lambda e: e.matmul(
                m.pC[:, 0:32], t.tri[:], m.ac32[:, 0, :], start=True, stop=True), sig="te")
            bC["wr"] = ("te", p.snap("te"))
            p.emit("tensor", lambda e: e.matmul(
                m.pD[0:32, 0:128], m.ac32[:, 0, :], t.tri[:], start=True, stop=True), sig="te")
            bD["wr"] = ("te", p.snap("te"))
            bank_read(p, "vector", bC,
                      lambda e, gch=gch: e.tensor_copy(t.acum_tm[:, gch, :], m.pC[:, 0:32]), "ve")
            bank_read(p, "vector", bD,
                      lambda e: e.tensor_copy(m.acf[:], m.pD[0:32, 0:128]), "ve")
            ac_done = p.snap("ve")
            p.wait("gpsimd", "ve", ac_done)
            s_acr.dma("gpsimd", lambda e: e.dma_start(out=t.acrow[:, 0:4096], in_=m.acf[:]))
            p.wait("tensor", bB["rd"][0], bB["rd"][1])
            s_acr.wait_data("tensor")
            p.emit("tensor", lambda e: e.matmul(
                m.pB[:, 0:32], t.ones1[:], t.acrow[0:1, 127:4096:128],
                start=True, stop=True), sig="te")
            bB["wr"] = ("te", p.snap("te"))
            bank_read(p, "vector", bB,
                      lambda e: e.tensor_copy(m.ac32[:, 1, :], m.pB[:, 0:32]), "ve")
            csp = p.snap("ve")
            p.wait("scalar", "ve", csp)
            p.emit("scalar", lambda e: e.activation(
                out=m.ac32[:, 2, :], in_=m.ac32[:, 1, :], func=ACT.Exp), sig="se")
            p.wait("vector", "ve", ac_done)
            p.drain("vector")
            p.emit("vector", lambda e, gch=gch: e.tensor_tensor(
                out=m.ac32[:, 3, :], in0=t.acum_tm[:, gch, :], in1=m.ac32[:, 1, :],
                op=ALU.subtract), sig="ve")
            d1 = p.snap("ve")
            p.wait("scalar", "ve", d1)
            p.emit("scalar", lambda e: e.activation(
                out=m.ac32[:, 3, :], in_=m.ac32[:, 3, :], func=ACT.Exp, scale=-1.0), sig="se")
            p.emit("scalar", lambda e, gch=gch: e.activation(
                out=m.ac32[:, 5, :], in_=t.acum_tm[:, gch, :], func=ACT.Exp), sig="se")
            p.emit("vector", lambda e, gch=gch: e.tensor_tensor(
                out=m.ac32[:, 6, :], in0=t.acum_tm[:, gch, :], in1=m.ac32[:, 4, :],
                op=ALU.add), sig="ve")
            d2 = p.snap("ve")
            p.wait("scalar", "ve", d2)
            p.emit("scalar", lambda e, gch=gch: e.activation(
                out=t.sdecabs[:, gch, :], in_=m.ac32[:, 6, :], func=ACT.Exp), sig="se")
            p.drain("vector")
            p.emit("vector", lambda e: e.tensor_tensor(
                out=m.ac32[:, 4, :], in0=m.ac32[:, 4, :], in1=m.ac32[:, 1, :], op=ALU.add), sig="ve")
            pan_done_se = p.snap("se")
            pan_done_ve = p.snap("ve")

            p.wait("tensor", bD["rd"][0], bD["rd"][1])
            p.wait("tensor", "se", conv_done)
            p.emit("tensor", lambda e, c2=c2: e.matmul(
                m.pD[:, 0:128], m.hbc[:, 32, 3 + c2 * 128: 3 + (c2 + 1) * 128],
                m.hbc[:, 33, 3 + c2 * 128: 3 + (c2 + 1) * 128], start=True, stop=True), sig="te")
            bD["wr"] = ("te", p.snap("te"))
            bank_read(p, "vector", bD,
                      lambda e: e.tensor_tensor(
                          out=m.gtri[:], in0=m.pD[:, 0:128], in1=t.tri[:], op=ALU.mult), "ve")
            bank = bT[0]
            p.wait("tensor", bank["rd"][0], bank["rd"][1])
            p.wait("tensor", "se", conv_done)
            p.emit("tensor", lambda e, c2=c2, bank=bank: e.transpose(
                bank["t"][:, 0:128], m.hbc[:, 32, 3 + c2 * 128: 3 + (c2 + 1) * 128],
                t.id16[:]), sig="te")
            bank["wr"] = ("te", p.snap("te"))
            bank_read(p, "vector", bank,
                      lambda e, bank=bank: e.tensor_copy(m.btm[:], bank["t"][:, 0:128]), "ve")
            gq_done = p.snap("ve")

            for h in range(32):
                p.wait("vector", "ve", xs_done)
                p.wait("vector", "se", pan_done_se)
                p.wait("vector", "te", getattr(st, "mscr_free", 0))
                p.drain("vector")
                p.emit("vector", lambda e, c2=c2, h=h, gch=gch: e.tensor_scalar(
                    out=m.xdt[:], in0=m.xs[:, c2, h * 128:(h + 1) * 128],
                    scalar1=t.dt_tm[:, gch, h:h + 1], scalar2=None, op0=ALU.mult), sig="ve")
                p.emit("vector", lambda e, h=h: e.tensor_scalar(
                    out=m.bdec[:], in0=m.btm[:], scalar1=m.ac32[:, 3, h:h + 1],
                    scalar2=None, op0=ALU.mult), sig="ve")
                xb_done = p.snap("ve")
                p.wait("tensor", bB["rd"][0], bB["rd"][1])
                s_acr.wait_data("tensor")
                p.emit("tensor", lambda e, h=h: e.matmul(
                    m.pB[:, 0:128], t.ones1[:], t.acrow[0:1, h * 128:(h + 1) * 128],
                    start=True, stop=True), sig="te")
                bB["wr"] = ("te", p.snap("te"))
                bank_read(p, "vector", bB,
                          lambda e, gch=gch, h=h: e.tensor_scalar(
                              out=m.dsl[:], in0=m.pB[:, 0:128],
                              scalar1=t.acum_tm[:, gch, h:h + 1], scalar2=0.0,
                              op0=ALU.subtract, op1=ALU.min), "ve")
                dsl_done = p.snap("ve")
                p.wait("scalar", "ve", dsl_done)
                p.emit("scalar", lambda e: e.activation(
                    out=m.esl[:], in_=m.dsl[:], func=ACT.Exp), sig="se")
                esl_done = p.snap("se")
                p.wait("vector", "se", esl_done)
                p.emit("vector", lambda e: e.tensor_tensor(
                    out=m.msl[:], in0=m.esl[:], in1=m.gtri[:], op=ALU.mult), sig="ve")
                msl_done = p.snap("ve")
                p.wait("tensor", bA["rd"][0], bA["rd"][1])
                p.wait("tensor", "ve", msl_done)
                p.emit("tensor", lambda e, h=h: e.matmul(
                    m.pA[:, 0:128], m.msl[:], m.xdt[:], start=True, stop=True), sig="te")
                bA["wr"] = ("te", p.snap("te"))
                p.wait("tensor", bC["rd"][0], bC["rd"][1])
                p.wait("tensor", "se", s16_prev)
                p.emit("tensor", lambda e, c2=c2, h=h: e.matmul(
                    m.pC[:, 0:128], m.hbc[:, 33, 3 + c2 * 128: 3 + (c2 + 1) * 128],
                    m.S16[:, h, :], start=True, stop=True), sig="te")
                bC["wr"] = ("te", p.snap("te"))
                p.wait("tensor", bD["rd"][0], bD["rd"][1])
                p.emit("tensor", lambda e, h=h: e.matmul(
                    m.pD[:, 0:128], m.bdec[:], m.xdt[:], start=True, stop=True), sig="te")
                bD["wr"] = ("te", p.snap("te"))
                p.wait("vector", bC["wr"][0], bC["wr"][1])
                p.emit("vector", lambda e, h=h: e.tensor_scalar(
                    out=m.ytmp[:], in0=m.pC[:, 0:128], scalar1=m.ac32[:, 5, h:h + 1],
                    scalar2=None, op0=ALU.mult), sig="ve")
                bC["rd"] = ("ve", p.snap("ve"))
                p.wait("vector", bA["wr"][0], bA["wr"][1])
                p.emit("vector", lambda e, c2=c2, h=h: e.scalar_tensor_tensor(
                    out=m.ytmp2[:], in0=m.xs[:, c2, h * 128:(h + 1) * 128],
                    scalar=t.dpanel[:, h:h + 1], in1=m.pA[:, 0:128],
                    op0=ALU.mult, op1=ALU.add), sig="ve")
                bA["rd"] = ("ve", p.snap("ve"))
                p.drain("vector")
                p.wait("vector", "y6", s_yst.ld)
                p.emit("vector", lambda e, c2=c2, h=h: e.tensor_tensor(
                    out=m.yq[:, c2, h * 128:(h + 1) * 128], in0=m.ytmp[:], in1=m.ytmp2[:],
                    op=ALU.add), sig="ve")
                st.mscr_free = p.snap("te")
                p.wait("vector", bD["wr"][0], bD["wr"][1])
                p.wait("vector", "se", pan_done_se)
                p.emit("vector", lambda e, h=h: e.scalar_tensor_tensor(
                    out=m.S32[:, h, :], in0=m.S32[:, h, :], scalar=m.ac32[:, 2, h:h + 1],
                    in1=m.pD[:, 0:128], op0=ALU.mult, op1=ALU.add), sig="ve")
                bD["rd"] = ("ve", p.snap("ve"))
                sup = p.snap("ve")
                p.wait("scalar", "ve", sup)
                p.emit("scalar", lambda e, h=h: e.copy(
                    out=m.S16[:, h, :], in_=m.S32[:, h, :]), sig="se")
        yq_done = p.snap("ve")
        p.wait("sync", "ve", yq_done)
        s_yst.dma("sync", lambda e, q=q: e.dma_start(
            out=d.y_hbm[q * 256:(q + 1) * 256, :].rearrange("(a p) n -> p a n", p=128),
            in_=m.yq[:]))

    slast = p.snap("se")
    p.wait("gpsimd", "se", slast)
    p.wait("gpsimd", "ve", p.snap("ve"))
    s_cb.dma("gpsimd", lambda e: e.dma_start(
        out=d.cb3_in.rearrange("(a p) v -> p a v", p=128), in_=m.S32[:]))
    s_cb.wait_data("gpsimd")
    p.emit("gpsimd", lambda e: e.collective_compute(
        "AllGather", ALU.bypass,
        replica_groups=[[0, 1], [2, 3], [4, 5], [6, 7]],
        ins=[d.cb3_in.opt()], outs=[d.cb3_out.opt()]), sig="cc")
    p.wait("gpsimd", "cc")
    s_cb.dma("gpsimd", lambda e: e.dma_start(
        out=m.S32[:], in_=d.cb3_out[0:4096, :].rearrange("(a p) v -> p a v", p=128)))
    s_cb.wait_data("vector")
    p.drain("vector")
    p.emit("vector", lambda e: e.tensor_scalar(
        out=m.S32[:].rearrange("p a b -> p (a b)"),
        in0=m.S32[:].rearrange("p a b -> p (a b)"),
        scalar1=t.odd[:, 0:1], scalar2=None, op0=ALU.mult), sig="ve")
    sm = p.snap("ve")
    p.wait("scalar", "ve", sm)
    p.emit("scalar", lambda e: e.copy(
        out=m.S16[:].rearrange("p a b -> p (a b)"),
        in_=m.S32[:].rearrange("p a b -> p (a b)")), sig="se")
    srdone = p.snap("se")

    s_yrl.free = ("y6", s_yst.ld)
    p.wait("sync", "y10", s_gst[0].ld)
    p.wait("sync", "y11", s_gst[1].ld)
    for q in range(NQ):
        s_yrl.dma("sync", lambda e, q=q: e.dma_start(
            out=m.yt[:], in_=d.y_hbm[q * 256:(q + 1) * 256, :].rearrange("(a p) n -> p a n", p=128)))
        s_yrl.wait_data("vector")
        for r2 in range(2):
            ssl = s_sgrl[r2]
            ssl.dma("sync", lambda e, q=q, r2=r2: e.dma_start(
                out=m.sgt[:, r2, :],
                in_=d.sgate_hbm[:, q * 256 + r2 * 128: q * 256 + (r2 + 1) * 128],
                transpose=True))
            ssl.wait_data("vector")
        for c2 in range(2):
            gch = q * 2 + c2
            for h in range(32):
                p.wait("tensor", bB["rd"][0], bB["rd"][1])
                p.wait("tensor", "se", srdone)
                p.emit("tensor", lambda e, gch=gch, h=h: e.matmul(
                    m.pB[:, 0:128], t.c_keep[:, gch * 128:(gch + 1) * 128],
                    m.S16[:, h, :], start=True, stop=True), sig="te")
                bB["wr"] = ("te", p.snap("te"))
                bank_read(p, "vector", bB,
                          lambda e, c2=c2, h=h, gch=gch: e.scalar_tensor_tensor(
                              out=m.yt[:, c2, h * 128:(h + 1) * 128], in0=m.pB[:, 0:128],
                              scalar=t.sdecabs[:, gch, h:h + 1],
                              in1=m.yt[:, c2, h * 128:(h + 1) * 128],
                              op0=ALU.mult, op1=ALU.add), "ve")
        corr_done = p.snap("ve")
        for r2 in range(2):
            p.wait("vector", "ve", corr_done)
            p.drain("vector")
            p.emit("vector", lambda e, r2=r2: e.tensor_tensor(
                out=m.yt[:, r2, :], in0=m.yt[:, r2, :], in1=m.sgt[:, r2, :], op=ALU.mult), sig="ve")
            yg1 = p.snap("ve")
            p.wait("scalar", "ve", yg1)
            p.emit("scalar", lambda e, r2=r2: e.activation(
                out=m.sgt[:, r2, :], in_=m.yt[:, r2, :], func=ACT.Square,
                accum_out=m.ssq[:, r2:r2 + 1]), sig="se")
            rms1 = p.snap("se")
            p.wait("scalar", "se", rms1)
            p.drain("scalar")
            p.emit("scalar", lambda e, r2=r2: e.activation(
                out=m.ssq[:, 2 + r2:3 + r2], in_=m.ssq[:, r2:r2 + 1], func=ACT.Sqrt,
                scale=1.0 / 4096.0, bias=t.epscol[:, 0:1]), sig="se")
            rms2 = p.snap("se")
            p.wait("vector", "se", rms2)
            p.emit("vector", lambda e, r2=r2: e.reciprocal(
                out=m.ssq[:, r2:r2 + 1], in_=m.ssq[:, 2 + r2:3 + r2]))
            p.drain("vector")
            p.emit("vector", lambda e, r2=r2: e.tensor_scalar(
                out=m.yt[:, r2, :], in0=m.yt[:, r2, :], scalar1=m.ssq[:, r2:r2 + 1],
                scalar2=None, op0=ALU.mult), sig="ve")
            p.drain("vector")
            p.emit("vector", lambda e, r2=r2: e.tensor_tensor(
                out=m.yt[:, r2, :], in0=m.yt[:, r2, :], in1=t.normw[:], op=ALU.mult), sig="ve")
        yn_done = p.snap("ve")
        for fch in range(32):
            for r2 in range(2):
                bank = bT[(fch * 2 + r2) % 2]
                p.wait("tensor", bank["rd"][0], bank["rd"][1])
                p.wait("tensor", "ve", yn_done)
                p.emit("tensor", lambda e, fch=fch, r2=r2, bank=bank: e.transpose(
                    bank["t"][:, 0:128], m.yt[:, r2, fch * 128:(fch + 1) * 128], t.id16[:]),
                    sig="te")
                bank["wr"] = ("te", p.snap("te"))
                bank_read(p, "vector", bank,
                          lambda e, fch=fch, r2=r2, bank=bank: e.tensor_copy(
                              m.ynT[:, fch, r2 * 128:(r2 + 1) * 128], bank["t"][:, 0:128]), "ve")
        ynT_done = p.snap("ve")
        obanks = [bank_new(m.pA), bank_new(m.pC)]
        obanks[0]["rd"] = bA["rd"]
        obanks[1]["rd"] = bC["rd"]
        for nch in range(4):
            for fch in range(32):
                wsl = s_wo[(nch * 32 + fch) % 2]
                wsl.dma("gpsimd", lambda e, fch=fch, nch=nch, wsl=wsl: e.dma_start(
                    out=m.wos[s_wo.index(wsl)][:],
                    in_=d.m_wout[fch, :, nch * 512:(nch + 1) * 512]))
                for r2 in range(2):
                    bank = obanks[r2]
                    if fch == 0:
                        p.wait("tensor", bank["rd"][0], bank["rd"][1])
                        p.wait("tensor", "ve", ynT_done)
                    wsl.wait_data("tensor")
                    p.emit("tensor", lambda e, r2=r2, fch=fch, wsl=wsl, bank=bank: e.matmul(
                        bank["t"][:, 0:512], m.ynT[:, fch, r2 * 128:(r2 + 1) * 128],
                        m.wos[s_wo.index(wsl)][:],
                        start=(fch == 0), stop=(fch == 31)),
                        sig=("te" if r2 == 1 else None))
                if fch == 31:
                    for r2 in range(2):
                        obanks[r2]["wr"] = ("te", p.snap("te"))
                wsl.set_free("te")
            for r2 in range(2):
                gt = q * 2 + r2
                xsl = s_xst[r2]
                xsl.dma("sync", lambda e, gt=gt, nch=nch: e.dma_start(
                    out=m.xst[gt % 2][:],
                    in_=x_src[gt * 128:(gt + 1) * 128, nch * 512:(nch + 1) * 512]))
                xsl.wait_data("vector")
                bank = obanks[r2]
                p.wait("vector", bank["wr"][0], bank["wr"][1])
                p.wait("vector", "y13" if r2 == 0 else "y14", s_xout[r2].ld)
                p.emit("vector", lambda e, r2=r2, gt=gt, bank=bank: e.tensor_tensor(
                    out=m.ost[r2][:], in0=bank["t"][:, 0:512], in1=m.xst[gt % 2][:],
                    op=ALU.add), sig="ve")
                bank["rd"] = ("ve", p.snap("ve"))
                xsl.set_free("ve")
                od = p.snap("ve")
                p.wait("sync", "ve", od)
                s_xout[r2].dma("sync", lambda e, gt=gt, nch=nch, r2=r2: e.dma_start(
                    out=x_dst[gt * 128:(gt + 1) * 128, nch * 512:(nch + 1) * 512],
                    in_=m.ost[r2][:]))
        bA["rd"] = obanks[0]["rd"]
        bC["rd"] = obanks[1]["rd"]
        bA["wr"] = obanks[0]["wr"]
        bC["wr"] = obanks[1]["wr"]
        s_yrl.free = ("te", p.snap("te"))
        for r2 in range(2):
            s_sgrl[r2].free = ("ve", p.snap("ve"))
    p.wait("sync", "y13", s_xout[0].ld)
    p.wait("sync", "y14", s_xout[1].ld)
    p.barrier()


def kernel(**inputs):
    cfg = NS(T=1024, phases=("gla", "ffn1", "mamba", "ffn2"))
    g = host_layout(inputs)
    maps = percore_inputs(inputs, g, cfg.T)
    nc = build_program(cfg)
    res = run_bass_kernel_spmd(nc, maps, core_ids=list(range(NCORES)))
    out = np.empty((4, 2048, 2048), dtype=np.float32)
    for c in range(NCORES):
        b, half = c // 2, c % 2
        out[b, half * cfg.T:(half + 1) * cfg.T] = res.results[c]["out"]
    return out


# version 07
# speedup vs baseline: 1.0487x; 1.0487x over previous
import math
import numpy as np
import ml_dtypes
from contextlib import ExitStack
from types import SimpleNamespace as NS

import concourse.bass as bass
import concourse.mybir as mybir
from concourse.bass_utils import run_bass_kernel_spmd

F32 = mybir.dt.float32
BF16 = mybir.dt.bfloat16
ACT = mybir.ActivationFunctionType
ALU = mybir.AluOpType
AX = mybir.AxisListType

NCORES = 8
GLA_H, GLA_DK, GLA_DV = 16, 64, 128
LNSCALE = -0.5 * math.log(128.0)
G16 = 1.0 / 16.0
TOP_K = 6
EPS = 1e-5

ENGINES = ("tensor", "vector", "scalar", "sync", "gpsimd")


class Pass:

    def __init__(self, nc, eng_name, e, sems, t, al, cfg):
        self.nc, self.eng, self.e = nc, eng_name, e
        self.sem, self.t, self.al, self.cfg = sems, t, al, cfg
        self.cnt = {k: 0 for k in sems if k != "bar"}
        self.bar_val = 0

    def emit(self, eng, f, sig=None, inc=1):
        if self.eng == eng:
            inst = f(self.e)
            if sig is not None:
                inst.then_inc(self.sem[sig], inc)
        if sig is not None:
            self.cnt[sig] += inc

    def wait(self, eng, sig, val=None):
        v = self.cnt[sig] if val is None else val
        if v <= 0:
            return
        if self.eng == eng:
            self.e.wait_ge(self.sem[sig], v)

    def drain(self, eng):
        if self.eng == eng:
            self.e.drain()

    def snap(self, sig):
        return self.cnt[sig]

    def barrier(self):
        self.bar_val += len(ENGINES)
        self.e.sem_inc(self.sem["bar"], 1)
        self.e.wait_ge(self.sem["bar"], self.bar_val)


def bank_new(t):
    return {"t": t, "rd": ("ve", 0), "wr": ("te", 0)}


class Slot:

    def __init__(self, p, sem):
        self.p, self.sem = p, sem
        self.free = ("te", 0)
        self.ld = 0

    def dma(self, eng, f):
        self.p.wait(eng, *self.free)
        self.p.emit(eng, f, sig=self.sem, inc=16)
        self.ld = self.p.cnt[self.sem]

    def wait_data(self, eng):
        self.p.wait(eng, self.sem, self.ld)

    def set_free(self, sig):
        self.free = (sig, self.p.snap(sig))


def bank_read(p, eng, bank, f, sig):
    p.wait(eng, bank["wr"][0], bank["wr"][1])
    p.emit(eng, f, sig=sig)
    bank["rd"] = (sig, p.snap(sig))


def host_layout(inp):
    g = {}
    f32 = np.float32
    asc = np.ascontiguousarray
    g["gla_wq"] = asc(inp["L0_attn_a_Wq"].T.reshape(16, 128, 8, 128).transpose(2, 0, 1, 3))
    g["gla_wk"] = asc(inp["L0_attn_a_Wk"].T.reshape(16, 128, 8, 128).transpose(2, 0, 1, 3))
    g["gla_wv"] = asc(inp["L0_attn_a_Wv"].T.reshape(16, 128, 16, 128).transpose(2, 0, 1, 3))
    g["gla_wg"] = asc(inp["L0_attn_a_Wg"].T.reshape(16, 128, 16, 128).transpose(2, 0, 1, 3))
    g["gla_wkg1"] = asc(inp["L0_attn_a_Wkg1"].T.reshape(16, 128, 16).transpose(1, 0, 2).reshape(128, 256))
    g["gla_wkg2"] = asc(inp["L0_attn_a_Wkg2"].T)
    g["gla_wo"] = asc(inp["L0_attn_a_Wo"].T.reshape(16, 128, 2048))
    g["gla_bkg2"] = asc(inp["L0_attn_a_bkg2"].reshape(1, 1024))
    g["gla_bg"] = asc(inp["L0_attn_a_bg"].reshape(16, 128).T)

    wg = inp["L0_ffn_wg"].reshape(32, 16, 128, 6, 128).transpose(0, 3, 1, 2, 4)
    sg = inp["L0_ffn_sg"].reshape(16, 128, 2, 6, 128).transpose(2, 3, 0, 1, 4)
    g["ffn_wg"] = asc(np.concatenate([wg, sg], axis=0))
    wu = inp["L0_ffn_wu"].reshape(32, 16, 128, 6, 128).transpose(0, 3, 1, 2, 4)
    su = inp["L0_ffn_su"].reshape(16, 128, 2, 6, 128).transpose(2, 3, 0, 1, 4)
    g["ffn_wu"] = asc(np.concatenate([wu, su], axis=0))
    wd = inp["L0_ffn_wd"].reshape(32, 6, 128, 2048)
    sd = inp["L0_ffn_sd"].reshape(2, 6, 128, 2048)
    g["ffn_wd"] = asc(np.concatenate([wd, sd], axis=0))
    g["ffn_gatew"] = asc(inp["L0_ffn_gate_w"].reshape(16, 128, 32).transpose(1, 0, 2).reshape(128, 512))

    ipw = inp["L0_attn_b_in_proj_w"]
    g["m_win_gate"] = asc(ipw[0:4096].T.reshape(16, 128, 32, 128).transpose(2, 0, 1, 3))
    g["m_win_hbc"] = asc(ipw[4096:8448].T.reshape(16, 128, 34, 128).transpose(2, 0, 1, 3))
    g["m_win_dt"] = asc(ipw[8448:8480].T.reshape(16, 128, 32).transpose(1, 0, 2).reshape(128, 512))
    g["m_convw"] = asc(inp["L0_attn_b_conv_w"].reshape(34, 128, 4).transpose(1, 0, 2).reshape(128, 136))
    g["m_convb"] = asc(inp["L0_attn_b_conv_b"].reshape(34, 128).T)
    g["m_alog"] = asc(inp["L0_attn_b_A_log"].reshape(1, 32))
    g["m_dtb"] = asc(inp["L0_attn_b_dt_bias"].reshape(1, 32))
    g["m_d"] = asc(inp["L0_attn_b_D"].reshape(1, 32))
    g["m_normw"] = asc(inp["L0_attn_b_norm_w"].reshape(1, 4096))
    g["m_wout"] = asc(inp["L0_attn_b_out_proj_w"].T.reshape(32, 128, 2048))

    s = np.arange(128)[:, None]
    tt = np.arange(128)[None, :]
    g["tri"] = (s <= tt).astype(f32)
    g["ones1"] = np.ones((1, 128), dtype=f32)
    g["ident32"] = np.eye(128, dtype=f32)
    g["ident16"] = np.eye(128, dtype=f32).astype(ml_dtypes.bfloat16)
    return g


def percore_inputs(inp, g, T):
    x = np.asarray(inp["L0_attn_a_x"], dtype=np.float32)
    maps = []
    for c in range(NCORES):
        b, half = c // 2, c % 2
        m = dict(g)
        m["x_in"] = np.ascontiguousarray(x[b, half * T:(half + 1) * T])
        m["odd_col"] = np.full((128, 1), float(half), np.float32)
        maps.append(m)
    return maps


def declare_io(nc, cfg):
    T = cfg.T
    dr = NS()
    di = lambda n, s, dt=F32: nc.dram_tensor(n, list(s), dt, kind="ExternalInput").ap()
    dint = lambda n, s, dt=F32: nc.dram_tensor(n, list(s), dt).ap()
    dr.x_in = di("x_in", (T, 2048))
    dr.odd_col = di("odd_col", (128, 1))
    dr.tri = di("tri", (128, 128))
    dr.ident32 = di("ident32", (128, 128))
    dr.ident16 = di("ident16", (128, 128), BF16)
    dr.ones1 = di("ones1", (1, 128))
    dr.gla_wq = di("gla_wq", (8, 16, 128, 128))
    dr.gla_wk = di("gla_wk", (8, 16, 128, 128))
    dr.gla_wv = di("gla_wv", (16, 16, 128, 128))
    dr.gla_wg = di("gla_wg", (16, 16, 128, 128))
    dr.gla_wkg1 = di("gla_wkg1", (128, 256))
    dr.gla_wkg2 = di("gla_wkg2", (16, 1024))
    dr.gla_wo = di("gla_wo", (16, 128, 2048))
    dr.gla_bkg2 = di("gla_bkg2", (1, 1024))
    dr.gla_bg = di("gla_bg", (128, 16))
    dr.ffn_wg = di("ffn_wg", (34, 6, 16, 128, 128))
    dr.ffn_wu = di("ffn_wu", (34, 6, 16, 128, 128))
    dr.ffn_wd = di("ffn_wd", (34, 6, 128, 2048))
    dr.ffn_gatew = di("ffn_gatew", (128, 512))
    dr.m_win_gate = di("m_win_gate", (32, 16, 128, 128))
    dr.m_win_hbc = di("m_win_hbc", (34, 16, 128, 128))
    dr.m_win_dt = di("m_win_dt", (128, 512))
    dr.m_convw = di("m_convw", (128, 136))
    dr.m_convb = di("m_convb", (128, 34))
    dr.m_alog = di("m_alog", (1, 32))
    dr.m_dtb = di("m_dtb", (1, 32))
    dr.m_d = di("m_d", (1, 32))
    dr.m_normw = di("m_normw", (1, 4096))
    dr.m_wout = di("m_wout", (32, 128, 2048))
    dr.out = nc.dram_tensor("out", [T, 2048], F32, kind="ExternalOutput").ap()
    dr.x_hbm = dint("x_hbm", (T, 2048))
    dr.o_hbm = dint("o_hbm", (T, 2048), BF16)
    dr.sg_hbm = dint("sg_hbm", (2048, T), BF16)
    dr.hbc_hbm = dint("hbc_hbm", (34, 128, 3 + T), BF16)
    dr.sgate_hbm = dint("sgate_hbm", (4096, T), BF16)
    dr.y_hbm = dint("y_hbm", (T, 4096), BF16)
    dr.cb1_in = dint("cb1_in", (1024, 128))
    dr.cb1_out = dint("cb1_out", (2048, 128))
    dr.cb2_in = dint("cb2_in", (34 * 128, 3))
    dr.cb2_out = dint("cb2_out", (2 * 34 * 128, 3))
    dr.cb3_in = dint("cb3_in", (4096, 128))
    dr.cb3_out = dint("cb3_out", (8192, 128))
    return dr


def alloc_persistent(nc, es, cfg):
    t = NS()
    sb = lambda n, s, dt=F32: es.enter_context(nc.sbuf_tensor(n, list(s), dt))
    t.tri = sb("c_tri", (128, 128))
    t.id32 = sb("c_id32", (128, 128))
    t.id16 = sb("c_id16", (128, 128), BF16)
    t.odd = sb("c_odd", (128, 1))
    t.bkg2 = sb("c_bkg2", (128, 1024))
    t.bg = sb("c_bg", (128, 16))
    t.apos = sb("c_apos", (128, 32))
    t.dpanel = sb("c_dpanel", (128, 32))
    t.dtbias = sb("c_dtbias", (128, 32))
    t.normw = sb("c_normw", (128, 4096), BF16)
    t.convw = sb("c_convw", (128, 136))
    t.convb = sb("c_convb", (128, 34))
    t.gatew = sb("c_gatew", (128, 512))
    t.ones1 = sb("c_ones1", (1, 128))
    n_tt = cfg.T // 128
    t.combine = sb("c_combine", (128, n_tt, 34))
    t.scores = sb("c_scores", (128, n_tt, 32))
    t.dt_tm = sb("c_dt", (128, n_tt, 32))
    t.acum_tm = sb("c_acum", (128, n_tt, 32))
    t.sdecabs = sb("c_sdecabs", (128, n_tt, 32))
    t.c_keep = sb("c_ckeep", (128, cfg.T), BF16)
    t.acrow = sb("c_acrow", (1, 4096))
    t.m32 = sb("c_m32", (128, 12, 32))
    t.prev8 = sb("c_prev8", (128, 8))
    t.epscol = sb("c_epscol", (128, 1))
    t.lnscol = sb("c_lnscol", (128, 1))
    return t


def emit_setup(p, cfg):
    t, d = p.t, p.t.dram
    loads = [
        (lambda: (t.tri[:], d.tri)), (lambda: (t.id32[:], d.ident32)),
        (lambda: (t.id16[:], d.ident16)), (lambda: (t.odd[:], d.odd_col)),
        (lambda: (t.gatew[:], d.ffn_gatew)), (lambda: (t.convw[:], d.m_convw)),
        (lambda: (t.convb[:], d.m_convb)), (lambda: (t.bg[:], d.gla_bg)),
        (lambda: (t.ones1[:], d.ones1)),
    ]
    for lf in loads:
        p.emit("sync", lambda e, lf=lf: e.dma_start(out=lf()[0], in_=lf()[1]), sig="yb", inc=16)
    bcasts = [
        (lambda: t.bkg2[:], lambda: d.gla_bkg2, 1024),
        (lambda: t.apos[:], lambda: d.m_alog, 32),
        (lambda: t.dpanel[:], lambda: d.m_d, 32),
        (lambda: t.dtbias[:], lambda: d.m_dtb, 32),
        (lambda: t.normw[:], lambda: d.m_normw, 4096),
    ]
    for dstf, srcf, n in bcasts:
        p.emit("gpsimd", lambda e, dstf=dstf, srcf=srcf, n=n:
               e.dma_start(out=dstf(), in_=srcf().broadcast_to((128, n))), sig="gb", inc=16)
    p.wait("gpsimd", "gb")
    p.wait("sync", "yb")
    p.barrier()
    p.emit("scalar", lambda e: e.activation(out=t.apos[:], in_=t.apos[:], func=ACT.Exp), sig="se")
    p.emit("vector", lambda e: e.memset(t.epscol[:], EPS))
    p.emit("vector", lambda e: e.memset(t.lnscol[:], LNSCALE))
    p.emit("vector", lambda e: e.memset(t.combine[:], 0.0))
    p.drain("vector")
    p.emit("vector", lambda e: e.memset(t.combine[:, :, 32:34], 1.0), sig="ve")
    p.barrier()


def alloc_ln(nc, es, tag, with_scores, with_xt, tr_banks=None):
    sb = lambda n, s, dt=F32: es.enter_context(nc.sbuf_tensor(tag + n, list(s), dt))
    ps = lambda n, s, dt=F32: es.enter_context(nc.psum_tensor(tag + n, list(s), dt))
    ph = NS()
    ph.with_scores = with_scores
    if with_xt:
        ph.x_t = [sb("_xt0", (128, 2048)), sb("_xt1", (128, 2048))]
    else:
        ph.x_t = None
    ph.xln = [sb("_xln0", (128, 2048))]
    ph.t32 = [sb("_t320", (128, 128)), sb("_t321", (128, 128))]
    ph.bns = sb("_bns", (128, 4, 6))
    ph.mv = sb("_mv", (128, 2))
    ph.sd = sb("_sd", (128, 1))
    ph.rs = sb("_rs", (128, 1))
    ph.tr = tr_banks if tr_banks is not None else [ps("_ptr0", (128, 512)), ps("_ptr1", (128, 512))]
    if with_scores:
        ph.sc = [ps("_psc0", (128, 512)), ps("_psc1", (128, 512))]
    return ph


def ln_state(p, ph, xslot_sems=("y0", "y1")):
    st = NS()
    st.b_tr = [bank_new(ph.tr[0]), bank_new(ph.tr[1])]
    if ph.with_scores:
        st.b_sc = [bank_new(ph.sc[0]), bank_new(ph.sc[1])]
    st.xslots = [Slot(p, xslot_sems[0]), Slot(p, xslot_sems[1])]
    st.xln_free_te = [0, 0]
    st.t32_free_se = [0, 0]
    st.t32_free_te = [0, 0]
    return st


def emit_ln_transpose(p, ph, st, x_hbm_src, x_sb_tiles, r0, n_tiles, xlnT, col_off):
    t = p.t
    scores = ph.with_scores
    for i in range(n_tiles):
        r = r0 + i
        if x_sb_tiles is None:
            xbuf = ph.x_t[i % 2]
            sl = st.xslots[i % 2]
            sl.dma("sync", lambda e, xbuf=xbuf, r=r: e.dma_start(
                out=xbuf[:], in_=x_hbm_src[r * 128:(r + 1) * 128, :]))
            sl.wait_data("vector")
            getx = lambda xbuf=xbuf: xbuf[:]
        else:
            getx = lambda r=r: x_sb_tiles(r)
        p.drain("vector")
        for jj in range(4):
            p.emit("vector", lambda e, getx=getx, jj=jj: e.bn_stats(
                out=ph.bns[:, jj, 0:6], in_=getx()[:, jj * 512:(jj + 1) * 512]))
        p.drain("vector")
        p.emit("vector", lambda e: e.bn_aggr(out=ph.mv[:, 0:2], in_=ph.bns[:, 0:4, 0:6]), sig="ve")
        p.wait("scalar", "ve")
        p.emit("scalar", lambda e: e.activation(out=ph.sd[:, 0:1], in_=ph.mv[:, 1:2],
                                                func=ACT.Sqrt, bias=t.epscol[:, 0:1]), sig="se")
        p.wait("vector", "se")
        p.emit("vector", lambda e: e.reciprocal(out=ph.rs[:, 0:1], in_=ph.sd[:, 0:1]))
        p.drain("vector")
        xl = ph.xln[0]
        p.wait("vector", "te", st.xln_free_te[0])
        p.emit("vector", lambda e, getx=getx, xl=xl: e.tensor_scalar(
            out=xl[:], in0=getx(), scalar1=ph.mv[:, 0:1], scalar2=ph.rs[:, 0:1],
            op0=ALU.subtract, op1=ALU.mult), sig="ve")
        if x_sb_tiles is None:
            st.xslots[i % 2].set_free("ve")
        xln_done = p.snap("ve")

        for dch in range(16):
            bank = st.b_tr[dch % 2]
            p.wait("tensor", bank["rd"][0], bank["rd"][1])
            p.wait("tensor", "ve", xln_done)
            p.emit("tensor", lambda e, xl=xl, dch=dch, bank=bank: e.transpose(
                bank["t"][:, 0:128], xl[:, dch * 128:(dch + 1) * 128], t.id32[:]), sig="te")
            bank["wr"] = ("te", p.snap("te"))
            s32 = ph.t32[dch % 2]
            p.wait("vector", "se", st.t32_free_se[dch % 2])
            p.wait("vector", "te", st.t32_free_te[dch % 2])
            bank_read(p, "vector", bank,
                      lambda e, s32=s32, bank=bank: e.tensor_copy(s32[:], bank["t"][:, 0:128]),
                      "ve")
            ev = p.snap("ve")
            p.wait("scalar", "ve", ev)
            p.emit("scalar", lambda e, s32=s32, dch=dch, i=i: e.copy(
                out=xlnT[:, dch, col_off + i * 128: col_off + (i + 1) * 128], in_=s32[:]),
                sig="se")
            st.t32_free_se[dch % 2] = p.snap("se")
            if scores:
                scb = st.b_sc[r % 2]
                p.wait("tensor", scb["rd"][0], scb["rd"][1])
                p.wait("tensor", "ve", ev)
                p.emit("tensor", lambda e, s32=s32, dch=dch, scb=scb: e.matmul(
                    scb["t"][:, 0:32], s32[:], t.gatew[:, dch * 32:(dch + 1) * 32],
                    start=(dch == 0), stop=(dch == 15)),
                    sig=("te" if dch == 15 else None))
                if dch == 15:
                    scb["wr"] = ("te", p.snap("te"))
                st.t32_free_te[dch % 2] = p.snap("te")
        st.xln_free_te[0] = p.snap("te")
        if scores:
            scb = st.b_sc[r % 2]
            bank_read(p, "scalar", scb,
                      lambda e, scb=scb, r=r: e.activation(
                          out=t.scores[:, r, :], in_=scb["t"][:, 0:32], func=ACT.Sigmoid),
                      "se")


def alloc_ffn(nc, es, cfg):
    T = cfg.T
    n_tt = T // 128
    sb = lambda n, s, dt=F32: es.enter_context(nc.sbuf_tensor(n, list(s), dt))
    f = NS()
    f.x = sb("f_x", (128, n_tt, 2048))
    f.xlnT = sb("f_xlnT", (128, 16, T), BF16)
    f.act = sb("f_act", (128, 6, T), BF16)
    f.sg = [sb("f_sg0", (128, 512), BF16), sb("f_sg1", (128, 512), BF16)]
    f.wg = [sb("f_wg%d" % i, (128, 2048), BF16) for i in range(2)]
    f.wu = [sb("f_wu%d" % i, (128, 2048), BF16) for i in range(2)]
    f.wd = [sb("f_wd%d" % i, (128, 2048), BF16) for i in range(7)]
    f.srt = sb("f_srt", (128, 32))
    f.sel = sb("f_sel", (128, 32))
    f.mcol = sb("f_mcol", (128, 4))
    es_ln = ExitStack()
    f.ln = alloc_ln(nc, es_ln, "fl", with_scores=True, with_xt=False)
    es.enter_context(_CloseOnExit(es_ln))
    ps = lambda n, s, dt=F32: es.enter_context(nc.psum_tensor(n, list(s), dt))
    f.p_g = [ps("fp_g0", (128, 512)), ps("fp_g1", (128, 512))]
    f.p_u = [ps("fp_u0", (128, 512)), ps("fp_u1", (128, 512))]
    return f


class _CloseOnExit:
    def __init__(self, es):
        self.es = es

    def __enter__(self):
        return self

    def __exit__(self, *a):
        self.es.close()
        return False


def emit_ffn(p, cfg, f, x_src, x_dst):
    t, d = p.t, p.t.dram
    T = cfg.T
    n_tt = T // 128
    NB = T // 512

    st = ln_state(p, f.ln)
    b_g = [bank_new(f.p_g[0]), bank_new(f.p_g[1])]
    b_u = [bank_new(f.p_u[0]), bank_new(f.p_u[1])]
    b_d = [bank_new(None), bank_new(None)]

    slots = {"wg": [Slot(p, "g%d" % i) for i in range(2)],
             "wu": [Slot(p, "g%d" % (3 + i)) for i in range(2)],
             "wd": [Slot(p, "g%d" % (6 + i)) for i in range(7)]}
    bufs = {"wg": f.wg, "wu": f.wu, "wd": f.wd}

    def stream(kind, e_i, fc):
        idx = e_i * 6 + fc
        slot = idx % len(bufs[kind])
        buf = bufs[kind][slot]
        if kind == "wd":
            slots[kind][slot].dma("gpsimd", lambda e, buf=buf, e_i=e_i, fc=fc: e.dma_start(
                out=buf[:], in_=d.ffn_wd[e_i, fc]))
        else:
            src = d.ffn_wg if kind == "wg" else d.ffn_wu
            slots[kind][slot].dma("gpsimd", lambda e, buf=buf, src=src, e_i=e_i, fc=fc: e.dma_start(
                out=buf[:].rearrange("p (k m) -> p k m", k=16),
                in_=src[e_i, fc].rearrange("k p m -> p k m")))

    for r in range(n_tt):
        p.emit("sync", lambda e, r=r: e.dma_start(
            out=f.x[:, r, :], in_=x_src[r * 128:(r + 1) * 128, :]), sig="yb", inc=16)
    xld = p.snap("yb")
    p.wait("vector", "yb", xld)
    emit_ln_transpose(p, f.ln, st, None, lambda r: f.x[:, r, :], 0, n_tt, f.xlnT, 0)
    xlnT_done = p.snap("se")

    scores_done = p.snap("se")
    p.wait("vector", "se", scores_done)
    for r in range(n_tt):
        p.drain("vector")
        p.emit("vector", lambda e, r=r: e.tensor_copy(f.srt[:], t.scores[:, r, :]))
        p.drain("vector")
        for it in range(TOP_K):
            p.emit("vector", lambda e: e.reduce_max(out=f.mcol[:, 0:1], in_=f.srt[:], axis=AX.X))
            p.drain("vector")
            if it < TOP_K - 1:
                p.emit("vector", lambda e: e.tensor_scalar(
                    out=f.sel[:], in0=f.srt[:], scalar1=f.mcol[:, 0:1], scalar2=None,
                    op0=ALU.is_ge))
                p.drain("vector")
                p.emit("vector", lambda e: e.scalar_tensor_tensor(
                    out=f.srt[:], in0=f.sel[:], scalar=-1e30, in1=f.srt[:],
                    op0=ALU.mult, op1=ALU.add))
                p.drain("vector")
        p.emit("vector", lambda e, r=r: e.tensor_scalar(
            out=f.sel[:], in0=t.scores[:, r, :], scalar1=f.mcol[:, 0:1], scalar2=None,
            op0=ALU.is_ge))
        p.drain("vector")
        p.emit("vector", lambda e, r=r: e.tensor_tensor(
            out=f.srt[:], in0=f.sel[:], in1=t.scores[:, r, :], op=ALU.mult))
        p.drain("vector")
        p.emit("vector", lambda e: e.tensor_reduce(
            out=f.mcol[:, 2:3], in_=f.srt[:], axis=AX.X, op=ALU.add))
        p.drain("vector")
        p.emit("vector", lambda e: e.tensor_scalar(
            out=f.mcol[:, 2:3], in0=f.mcol[:, 2:3], scalar1=1e-20, scalar2=None, op0=ALU.add))
        p.drain("vector")
        p.emit("vector", lambda e: e.reciprocal(out=f.mcol[:, 3:4], in_=f.mcol[:, 2:3]))
        p.drain("vector")
        p.emit("vector", lambda e, r=r: e.tensor_scalar(
            out=t.combine[:, r, 0:32], in0=f.srt[:], scalar1=f.mcol[:, 3:4], scalar2=None,
            op0=ALU.mult), sig="ve")
    comb_done = p.snap("ve")

    for fc in range(2):
        stream("wg", 0, fc)
        stream("wu", 0, fc)
    for fc in range(6):
        stream("wd", 0, fc)

    sg_free = [0, 0]
    act_free = 0
    for ei in range(34):
        for fc in range(6):
            gslot = (ei * 6 + fc) % 2
            for kind, bpair, wbuf in (("wg", b_g, f.wg[gslot]), ("wu", b_u, f.wu[gslot])):
                for nb in range(NB):
                    bank = bpair[(fc * NB + nb) % 2]
                    p.wait("tensor", bank["rd"][0], bank["rd"][1])
                    slots[kind][gslot].wait_data("tensor")
                    p.wait("tensor", "se", xlnT_done)
                    for dch in range(16):
                        p.emit("tensor", lambda e, wbuf=wbuf, dch=dch, nb=nb, bank=bank: e.matmul(
                            bank["t"][:, 0:512],
                            wbuf[:, dch * 128:(dch + 1) * 128],
                            f.xlnT[:, dch, nb * 512:(nb + 1) * 512],
                            start=(dch == 0), stop=(dch == 15)),
                            sig=("te" if dch == 15 else None))
                    bank["wr"] = ("te", p.snap("te"))
            slots["wg"][gslot].set_free("te")
            slots["wu"][gslot].set_free("te")
            nf = ei * 6 + fc + 2
            if nf < 34 * 6:
                stream("wg", nf // 6, nf % 6)
                stream("wu", nf // 6, nf % 6)
            for nb in range(NB):
                gb = b_g[(fc * NB + nb) % 2]
                ub = b_u[(fc * NB + nb) % 2]
                sgb = f.sg[nb % 2]
                p.wait("scalar", gb["wr"][0], gb["wr"][1])
                p.wait("scalar", "ve", sg_free[nb % 2])
                p.emit("scalar", lambda e, gb=gb, sgb=sgb: e.activation(
                    out=sgb[:], in_=gb["t"][:, 0:512], func=ACT.Silu), sig="se")
                gb["rd"] = ("se", p.snap("se"))
                sv = p.snap("se")
                p.wait("vector", "se", sv)
                p.wait("vector", ub["wr"][0], ub["wr"][1])
                p.wait("vector", "te", act_free)
                p.emit("vector", lambda e, ub=ub, sgb=sgb, fc=fc, nb=nb: e.tensor_tensor(
                    out=f.act[:, fc, nb * 512:(nb + 1) * 512],
                    in0=sgb[:], in1=ub["t"][:, 0:512], op=ALU.mult), sig="ve")
                ub["rd"] = ("ve", p.snap("ve"))
                sg_free[nb % 2] = p.snap("ve")
        act_done = p.snap("ve")

        for r in range(n_tt):
            for h2 in range(2):
                bank = b_d[h2]
                p.wait("tensor", bank["rd"][0], bank["rd"][1])
                p.wait("tensor", b_g[h2]["rd"][0], b_g[h2]["rd"][1])
                p.wait("tensor", b_u[h2]["rd"][0], b_u[h2]["rd"][1])
                p.wait("tensor", "ve", act_done)
                for fc in range(6):
                    dslot = (ei * 6 + fc) % 7
                    slots["wd"][dslot].wait_data("tensor")
                    for nn in range(2):
                        tgt = f.p_g[h2] if nn == 0 else f.p_u[h2]
                        p.emit("tensor", lambda e, fc=fc, r=r, h2=h2, nn=nn, dslot=dslot, tgt=tgt: e.matmul(
                            tgt[:, 0:512],
                            f.act[:, fc, r * 128:(r + 1) * 128],
                            f.wd[dslot][:, h2 * 1024 + nn * 512: h2 * 1024 + (nn + 1) * 512],
                            start=(fc == 0), stop=(fc == 5)),
                            sig=("te" if (fc == 5 and nn == 1) else None))
                bank["wr"] = ("te", p.snap("te"))
                p.wait("vector", bank["wr"][0], bank["wr"][1])
                p.wait("vector", "ve", comb_done)
                p.drain("vector")
                for nn in range(2):
                    src_ps = f.p_g[h2] if nn == 0 else f.p_u[h2]
                    p.emit("vector", lambda e, src_ps=src_ps, r=r, h2=h2, nn=nn, ei=ei: e.scalar_tensor_tensor(
                        out=f.x[:, r, h2 * 1024 + nn * 512: h2 * 1024 + (nn + 1) * 512],
                        in0=src_ps[:, 0:512],
                        scalar=t.combine[:, r, ei:ei + 1],
                        in1=f.x[:, r, h2 * 1024 + nn * 512: h2 * 1024 + (nn + 1) * 512],
                        op0=ALU.mult, op1=ALU.add), sig="ve")
                bank["rd"] = ("ve", p.snap("ve"))
                b_g[h2]["rd"] = ("ve", p.snap("ve"))
                b_u[h2]["rd"] = ("ve", p.snap("ve"))
        for fc in range(6):
            slots["wd"][(ei * 6 + fc) % 7].set_free("te")
        act_free = p.snap("te")
        if ei + 1 < 34:
            for fc in range(6):
                stream("wd", ei + 1, fc)

    xdone = p.snap("ve")
    p.wait("sync", "ve", xdone)
    for r in range(n_tt):
        p.emit("sync", lambda e, r=r: e.dma_start(
            out=x_dst[r * 128:(r + 1) * 128, :], in_=f.x[:, r, :]), sig="yb", inc=16)
    p.wait("sync", "yb")
    p.barrier()


def build_program(cfg):
    nc = bass.Bass("TRN2", target_bir_lowering=False, debug=False)
    dram = declare_io(nc, cfg)
    es_pers = ExitStack()
    pers = alloc_persistent(nc, es_pers, cfg)
    pers.dram = dram

    al = {}
    if "gla" in cfg.phases:
        es_ph = ExitStack()
        al["gla"] = alloc_gla(nc, es_ph, cfg)
        es_ph.close()
    if "ffn1" in cfg.phases or "ffn2" in cfg.phases:
        es_ph = ExitStack()
        al["ffn"] = alloc_ffn(nc, es_ph, cfg)
        es_ph.close()
    if "mamba" in cfg.phases:
        es_ph = ExitStack()
        al["mamba"] = alloc_mamba(nc, es_ph, cfg)
        es_ph.close()

    sems = {}
    sem_es = ExitStack()
    names = ["te", "ve", "se", "gp", "cc", "bar", "gb", "yb"]
    names += ["g%d" % i for i in range(16)]
    names += ["y%d" % i for i in range(16)]
    for name in names:
        sems[name] = sem_es.enter_context(nc.semaphore("s_" + name))

    block_es = ExitStack()
    block = block_es.enter_context(nc.Block())

    def make_section(eng_name):
        def section(e):
            pp = Pass(nc, eng_name, e, sems, pers, al, cfg)
            emit_all(pp, cfg)
        return section

    for eng_name in ENGINES:
        getattr(block, eng_name)(make_section(eng_name))
    block_es.close()
    sem_es.close()
    es_pers.close()
    return nc


def emit_all(p, cfg):
    d = p.t.dram
    emit_setup(p, cfg)
    x_cur = d.x_in
    if "gla" in cfg.phases:
        emit_gla(p, cfg, p.al["gla"], x_cur, d.x_hbm)
        x_cur = d.x_hbm
    if "ffn1" in cfg.phases:
        dst = d.out if cfg.phases[-1] == "ffn1" else d.x_hbm
        emit_ffn(p, cfg, p.al["ffn"], x_cur, dst)
        x_cur = dst
    if "mamba" in cfg.phases:
        emit_mamba(p, cfg, p.al["mamba"], x_cur, d.x_hbm)
        x_cur = d.x_hbm
    if "ffn2" in cfg.phases:
        emit_ffn(p, cfg, p.al["ffn"], x_cur, d.out)
        x_cur = d.out
    if x_cur is not d.out:
        for r in range(cfg.T // 128):
            p.emit("sync", lambda e, r=r, x_cur=x_cur: e.dma_start(
                out=d.out[r * 128:(r + 1) * 128, :], in_=x_cur[r * 128:(r + 1) * 128, :]),
                sig="yb", inc=16)
    p.wait("sync", "yb")
    p.wait("gpsimd", "gb")
    for i in range(16):
        p.wait("gpsimd", "g%d" % i)
    for i in range(16):
        p.wait("sync", "y%d" % i)
    p.wait("gpsimd", "cc")
    p.barrier()


def alloc_gla(nc, es, cfg):
    T = cfg.T
    HT = min(4, T // 128)
    sb = lambda n, s, dt=F32: es.enter_context(nc.sbuf_tensor(n, list(s), dt))
    ps = lambda n, s, dt=F32: es.enter_context(nc.psum_tensor(n, list(s), dt))
    g = NS()
    g.HT = HT
    g.xlnT = sb("g_xlnT", (128, 16, HT * 128), BF16)
    g.q = sb("g_q", (128, 8, T), BF16)
    g.kf = sb("g_kf", (128, 8, HT * 128), BF16)
    g.v = sb("g_v", (128, HT, 2048), BF16)
    g.kg = sb("g_kg", (128, HT, 1024), BF16)
    es_a = ExitStack()
    g.vf = es_a.enter_context(nc.sbuf_tensor("g_vf", [128, 16, HT * 128], BF16))
    es_a.close()
    g.lsig = sb("g_lsig", (128, 1024))
    g.gab = sb("g_gab", (128, 8, 128))
    g.ep = sb("g_ep", (128, 8, 128))
    g.en = sb("g_en", (128, 8, 128))
    g.ktm = sb("g_ktm", (128, 8, 128), BF16)
    g.u32 = sb("g_u32", (128, 8, 128))
    g.u16 = sb("g_u16", (128, 8, 128), BF16)
    g.o = sb("g_o", (128, HT, 2048), BF16)
    g.sgf = sb("g_sgf", (128, 16, HT * 128), BF16)
    g.sr16 = sb("g_sr16", (128, 8, 128), BF16)
    g.zf = sb("g_zf", (16, HT * 128), BF16)
    g.at = [sb("g_at0", (128, 128), BF16), sb("g_at1", (128, 128), BF16)]
    g.ont = [sb("g_ont0", (128, 128), BF16), sb("g_ont1", (128, 128), BF16)]
    g.ecol = sb("g_ecol", (128, 8))
    g.wqk = [sb("g_wqk0", (128, 2048), BF16), sb("g_wqk1", (128, 2048), BF16), sb("g_wqk2", (128, 2048), BF16)]
    g.wkg1 = sb("g_wkg1", (128, 256), BF16)
    g.wkg2 = sb("g_wkg2", (16, 1024), BF16)
    g.pP = [ps("gp_P0", (128, 512)), ps("gp_P1", (128, 512))]
    es_ln = ExitStack()
    g.ln = alloc_ln(nc, es_ln, "gl", with_scores=False, with_xt=True, tr_banks=g.pP)
    es_ln.close()
    g.bns16 = sb("g_bns16", (128, 16, 6))
    g.mv16 = sb("g_mv16", (128, 16, 2))
    g.sd16 = sb("g_sd16", (128, 16))
    g.rs16 = sb("g_rs16", (128, 16))
    g.gnt = sb("g_gnt", (128, 128), BF16)
    g.xst = [sb("g_xst0", (128, 512)), sb("g_xst1", (128, 512))]
    g.ost = [sb("g_ost0", (128, 512)), sb("g_ost1", (128, 512))]
    g.wos = [sb("g_wos0", (128, 512), BF16), sb("g_wos1", (128, 512), BF16)]
    g.pE = ps("gp_E", (128, 512))
    g.pF = ps("gp_F", (128, 512))
    g.pG = ps("gp_G", (128, 512))
    g.pH = ps("gp_H", (128, 512))
    g.pT = [ps("gp_T0", (128, 512), BF16), ps("gp_T1", (128, 512), BF16)]
    return g


def emit_gla(p, cfg, g, x_src, x_dst):
    t, d = p.t, p.t.dram
    T = cfg.T
    HT = g.HT
    HTOK = HT * 128
    n_half = T // HTOK
    st = ln_state(p, g.ln)
    bE = bank_new(g.pE)
    bF = bank_new(g.pF)
    bG = bank_new(g.pG)
    bH = bank_new(g.pH)
    bT = [bank_new(g.pT[0]), bank_new(g.pT[1])]
    bP = [bank_new(g.ln.tr[0]), bank_new(g.ln.tr[1])]
    s_wqk = [Slot(p, "g0"), Slot(p, "g1"), Slot(p, "g10")]
    s_one = Slot(p, "g2")
    s_cb = Slot(p, "g5")
    s_ost = Slot(p, "y2")
    s_sgst = Slot(p, "y3")
    s_xrt = [Slot(p, "y4"), Slot(p, "y5")]
    s_orl = Slot(p, "y6")
    s_sgrl = Slot(p, "y7")
    s_xout = [Slot(p, "y8"), Slot(p, "y9")]

    p.emit("vector", lambda e: e.memset(g.u32[:], 0.0))
    p.emit("vector", lambda e: e.memset(g.u16[:], 0.0))
    p.emit("vector", lambda e: e.memset(t.prev8[:], 0.0), sig="ve")
    uz = p.snap("ve")
    p.wait("tensor", "ve", uz)
    p.wait("scalar", "ve", uz)
    s_one.dma("gpsimd", lambda e: e.dma_start(out=g.wkg1[:], in_=d.gla_wkg1))
    s_one.dma("gpsimd", lambda e: e.dma_start(out=g.wkg2[:], in_=d.gla_wkg2))
    wsmall = s_one
    s_wo = [Slot(p, "g3"), Slot(p, "g4")]

    o_spill_done = [0, 0]

    for H in range(n_half):
        p.wait("scalar", "te", p.snap("te"))
        p.wait("vector", "te", p.snap("te"))
        p.wait("vector", "se", p.snap("se"))
        emit_ln_transpose(p, g.ln, st, x_src, None, H * HT, HT, g.xlnT, 0)
        xlnT_done = p.snap("se")

        def proj_fm(wsrc, n_m, dst_fn, post):
            for m in range(n_m):
                sl = s_wqk[m % 3]
                sl.dma("gpsimd", lambda e, wsrc=wsrc, m=m, sl=sl: e.dma_start(
                    out=g.wqk[s_wqk.index(sl)][:].rearrange("p (k m) -> p k m", k=16),
                    in_=wsrc[m].rearrange("k p m -> p k m")))
                bank = bP[m % 2]
                p.wait("tensor", bank["rd"][0], bank["rd"][1])
                sl.wait_data("tensor")
                p.wait("tensor", "se", xlnT_done)
                wbuf = g.wqk[m % 3]
                for dch in range(16):
                    for nb in range(HTOK // 512 if HTOK >= 512 else 1):
                        w = min(512, HTOK)
                        p.emit("tensor", lambda e, wbuf=wbuf, dch=dch, nb=nb, w=w, bank=bank: e.matmul(
                            bank["t"][:, 0:w],
                            wbuf[:, dch * 128:(dch + 1) * 128],
                            g.xlnT[:, dch, nb * w:(nb + 1) * w],
                            start=(dch == 0), stop=(dch == 15)),
                            sig=("te" if (dch == 15 and nb == (HTOK - 1) // 512) else None))
                bank["wr"] = ("te", p.snap("te"))
                sl.set_free("te")
                post(m, bank)

        def post_q(m, bank):
            bank_read(p, "vector", bank,
                      lambda e, m=m, bank=bank: e.tensor_copy(
                          g.q[:, m, H * HTOK:(H + 1) * HTOK], bank["t"][:, 0:HTOK]), "ve")

        def post_k(m, bank):
            bank_read(p, "vector", bank,
                      lambda e, m=m, bank=bank: e.tensor_copy(
                          g.kf[:, m, 0:HTOK], bank["t"][:, 0:HTOK]), "ve")

        def post_v(m, bank):
            bank_read(p, "vector", bank,
                      lambda e, m=m, bank=bank: e.tensor_copy(
                          g.vf[:, m, 0:HTOK], bank["t"][:, 0:HTOK]), "ve")

        def post_sg(m, bank):
            p.wait("scalar", "y3", s_sgst.ld)
            bank_read(p, "scalar", bank,
                      lambda e, m=m, bank=bank: e.activation(
                          out=g.sgf[:, m, 0:HTOK], in_=bank["t"][:, 0:HTOK],
                          func=ACT.Silu, bias=t.bg[:, m:m + 1]), "se")

        proj_fm(d.gla_wq, 8, None, post_q)
        proj_fm(d.gla_wk, 8, None, post_k)
        proj_fm(d.gla_wv, 16, None, post_v)
        proj_fm(d.gla_wg, 16, None, post_sg)
        sg_done = p.snap("se")

        p.wait("sync", "se", sg_done)
        s_sgst.dma("sync", lambda e, H=H: e.dma_start(
            out=d.sg_hbm[:, H * HTOK:(H + 1) * HTOK].rearrange("(a p) t -> p a t", p=128),
            in_=g.sgf[:]))

        vtd = p.snap("ve")
        for vm in range(16):
            for r4 in range(HT):
                bank = bT[(vm * HT + r4) % 2]
                p.wait("tensor", bank["rd"][0], bank["rd"][1])
                p.wait("tensor", "ve", vtd)
                p.emit("tensor", lambda e, vm=vm, r4=r4, bank=bank: e.transpose(
                    bank["t"][:, 0:128], g.vf[:, vm, r4 * 128:(r4 + 1) * 128], t.id16[:]),
                    sig="te")
                bank["wr"] = ("te", p.snap("te"))
                bank_read(p, "vector", bank,
                          lambda e, vm=vm, r4=r4, bank=bank: e.tensor_copy(
                              g.v[:, r4, vm * 128:(vm + 1) * 128], bank["t"][:, 0:128]), "ve")
        v_done = p.snap("ve")

        p.wait("tensor", bH["rd"][0], bH["rd"][1])
        wsmall.wait_data("tensor")
        p.wait("tensor", "se", xlnT_done)
        for dch in range(16):
            p.emit("tensor", lambda e, dch=dch: e.matmul(
                g.pH[0:16, 0:HTOK], g.wkg1[:, dch * 16:(dch + 1) * 16],
                g.xlnT[:, dch, 0:HTOK],
                start=(dch == 0), stop=(dch == 15)), sig=("te" if dch == 15 else None))
        bH["wr"] = ("te", p.snap("te"))
        bank_read(p, "vector", bH,
                  lambda e: e.tensor_copy(g.zf[:, 0:HTOK], g.pH[0:16, 0:HTOK]), "ve")
        z_done = p.snap("ve")
        bkg_banks = [bE, bF]
        for r4 in range(HT):
            for nch in range(2):
                bank = bkg_banks[nch]
                p.wait("tensor", bank["rd"][0], bank["rd"][1])
                p.wait("tensor", "ve", z_done)
                p.emit("tensor", lambda e, r4=r4, nch=nch, bank=bank: e.matmul(
                    bank["t"][:, 0:512], g.zf[:, r4 * 128:(r4 + 1) * 128],
                    g.wkg2[:, nch * 512:(nch + 1) * 512],
                    start=True, stop=True), sig="te")
                bank["wr"] = ("te", p.snap("te"))
                bank_read(p, "vector", bank,
                          lambda e, r4=r4, nch=nch, bank=bank: e.tensor_tensor(
                              out=g.kg[:, r4, nch * 512:(nch + 1) * 512],
                              in0=bank["t"][:, 0:512],
                              in1=t.bkg2[:, nch * 512:(nch + 1) * 512], op=ALU.add), "ve")
        kg_done = p.snap("ve")

        for c in range(HT):
            gch = H * HT + c
            p.drain("vector")
            p.wait("vector", "se", p.snap("se"))
            p.wait("scalar", "ve", kg_done)
            p.wait("scalar", "te", p.snap("te"))
            p.emit("scalar", lambda e, c=c: e.activation(
                out=g.lsig[:], in_=g.kg[:, c, :], func=ACT.Sigmoid), sig="se")
            p.drain("scalar")
            p.emit("scalar", lambda e: e.activation(
                out=g.lsig[:], in_=g.lsig[:], func=ACT.Ln), sig="se")
            lsig_done = p.snap("se")
            tribk = [bH, bE]
            for m in range(8):
                bk = tribk[m % 2]
                p.wait("tensor", bk["rd"][0], bk["rd"][1])
                p.wait("tensor", "se", lsig_done)
                p.emit("tensor", lambda e, m=m, bk=bk: e.matmul(
                    bk["t"][:, 0:128], g.lsig[:, m * 128:(m + 1) * 128], t.tri[:],
                    start=True, stop=True), sig="te")
                bk["wr"] = ("te", p.snap("te"))
                bank_read(p, "vector", bk,
                          lambda e, m=m, bk=bk: e.tensor_scalar(
                              out=g.gab[:, m, :], in0=bk["t"][:, 0:128],
                              scalar1=t.prev8[:, m:m + 1], scalar2=None, op0=ALU.add), "ve")
            gab_done = p.snap("ve")
            p.wait("scalar", "ve", gab_done)
            p.emit("scalar", lambda e: e.copy(out=t.prev8[:], in_=g.gab[:, :, 127:128]), sig="se")
            p.emit("scalar", lambda e: e.activation(
                out=g.ep[:].rearrange("p a b -> p (a b)"),
                in_=g.gab[:].rearrange("p a b -> p (a b)"), func=ACT.Exp, scale=G16), sig="se")
            p.emit("scalar", lambda e: e.activation(
                out=g.en[:].rearrange("p a b -> p (a b)"),
                in_=g.gab[:].rearrange("p a b -> p (a b)"), func=ACT.Exp, scale=-G16,
                bias=t.lnscol[:, 0:1]), sig="se")
            exps_done = p.snap("se")
            p.wait("vector", "se", exps_done)
            p.emit("vector", lambda e, gch=gch: e.tensor_tensor(
                out=g.q[:, :, gch * 128:(gch + 1) * 128], in0=g.q[:, :, gch * 128:(gch + 1) * 128],
                in1=g.ep[:], op=ALU.mult), sig="ve")
            p.emit("vector", lambda e, c=c: e.tensor_tensor(
                out=g.kf[:, :, c * 128:(c + 1) * 128], in0=g.kf[:, :, c * 128:(c + 1) * 128],
                in1=g.en[:], op=ALU.mult), sig="ve")
            qk_done = p.snap("ve")
            for m in range(8):
                bank = bT[m % 2]
                p.wait("tensor", bank["rd"][0], bank["rd"][1])
                p.wait("tensor", "ve", qk_done)
                p.emit("tensor", lambda e, m=m, c=c, bank=bank: e.transpose(
                    bank["t"][:, 0:128], g.kf[:, m, c * 128:(c + 1) * 128], t.id16[:]), sig="te")
                bank["wr"] = ("te", p.snap("te"))
                bank_read(p, "vector", bank,
                          lambda e, m=m, bank=bank: e.tensor_copy(
                              g.ktm[:, m, :], bank["t"][:, 0:128]), "ve")
            ktm_done = p.snap("ve")

            u16_done = p.snap("se")
            atbk = [bE, bH]
            obk = [bF, bP[0]]
            tbk = [bG, bP[1]]
            for h in range(16):
                b = (h % 2) * 64
                hm = h // 2
                pe_ = atbk[h % 2]
                pf_ = obk[h % 2]
                pg_ = tbk[h % 2]
                p.wait("tensor", pe_["rd"][0], pe_["rd"][1])
                p.wait("tensor", "ve", qk_done)
                p.emit("tensor", lambda e, b=b, hm=hm, c=c, gch=gch, pe_=pe_: e.matmul(
                    pe_["t"][:, 0:128], g.kf[b:b + 64, hm, c * 128:(c + 1) * 128],
                    g.q[b:b + 64, hm, gch * 128:(gch + 1) * 128],
                    start=True, stop=True), sig="te")
                pe_["wr"] = ("te", p.snap("te"))
                atb = g.at[h % 2]
                if not hasattr(st, "at_free"):
                    st.at_free = [0, 0]
                p.wait("vector", "te", st.at_free[h % 2])
                bank_read(p, "vector", pe_,
                          lambda e, atb=atb, pe_=pe_: e.tensor_tensor(
                              out=atb[:], in0=pe_["t"][:, 0:128], in1=t.tri[:], op=ALU.mult), "ve")
                at_done = p.snap("ve")
                p.wait("tensor", pf_["rd"][0], pf_["rd"][1])
                p.wait("tensor", "ve", at_done)
                p.wait("tensor", "ve", v_done)
                p.wait("tensor", "se", u16_done)
                p.emit("tensor", lambda e, atb=atb, c=c, h=h, pf_=pf_: e.matmul(
                    pf_["t"][:, 0:128], atb[:], g.v[:, c, h * 128:(h + 1) * 128],
                    start=True, stop=False))
                p.emit("tensor", lambda e, b=b, hm=hm, gch=gch, pf_=pf_: e.matmul(
                    pf_["t"][:, 0:128], g.q[b:b + 64, hm, gch * 128:(gch + 1) * 128],
                    g.u16[b:b + 64, hm, :], start=False, stop=True), sig="te")
                pf_["wr"] = ("te", p.snap("te"))
                st.at_free[h % 2] = p.snap("te")
                p.wait("scalar", "y2", s_ost.ld)
                bank_read(p, "scalar", pf_,
                          lambda e, c=c, h=h, pf_=pf_: e.copy(
                              out=g.o[:, c, h * 128:(h + 1) * 128], in_=pf_["t"][:, 0:128]), "se")
                p.wait("tensor", pg_["rd"][0], pg_["rd"][1])
                p.wait("tensor", "ve", ktm_done)
                p.emit("tensor", lambda e, b=b, hm=hm, c=c, h=h, pg_=pg_: e.matmul(
                    pg_["t"][b:b + 64, 0:128], g.ktm[:, hm, b:b + 64],
                    g.v[:, c, h * 128:(h + 1) * 128], start=True, stop=True), sig="te")
                pg_["wr"] = ("te", p.snap("te"))
                bank_read(p, "vector", pg_,
                          lambda e, b=b, hm=hm, pg_=pg_: e.tensor_tensor(
                              out=g.u32[b:b + 64, hm, :], in0=g.u32[b:b + 64, hm, :],
                              in1=pg_["t"][b:b + 64, 0:128], op=ALU.add), "ve")
                uadd = p.snap("ve")
                p.wait("scalar", "ve", uadd)
                p.emit("scalar", lambda e, b=b, hm=hm: e.copy(
                    out=g.u16[b:b + 64, hm, :], in_=g.u32[b:b + 64, hm, :]), sig="se")
                u16_done = p.snap("se")

        odone = p.snap("se")
        p.wait("sync", "se", odone)
        p.wait("sync", "ve", p.snap("ve"))
        s_ost.dma("sync", lambda e, H=H: e.dma_start(
            out=d.o_hbm[H * HTOK:(H + 1) * HTOK, :].rearrange("(a p) n -> p a n", p=128),
            in_=g.o[:]))
        o_spill_done[H] = s_ost.ld

    lastg = p.snap("ve")
    p.wait("scalar", "ve", lastg)
    p.emit("scalar", lambda e: e.activation(
        out=g.ecol[:], in_=g.gab[:, :, 127:128], func=ACT.Exp, scale=G16), sig="se")
    ecol_done = p.snap("se")
    p.wait("vector", "se", ecol_done)
    p.drain("vector")
    for hm in range(8):
        p.emit("vector", lambda e, hm=hm: e.tensor_scalar(
            out=g.ep[:, hm, :], in0=g.u32[:, hm, :], scalar1=g.ecol[:, hm:hm + 1],
            scalar2=None, op0=ALU.mult), sig="ve")
    snd_done = p.snap("ve")
    p.wait("gpsimd", "ve", snd_done)
    s_cb.dma("gpsimd", lambda e: e.dma_start(
        out=d.cb1_in.rearrange("(a p) v -> p a v", p=128), in_=g.ep[:]))
    s_cb.wait_data("gpsimd")
    p.emit("gpsimd", lambda e: e.collective_compute(
        "AllGather", ALU.bypass,
        replica_groups=[[0, 1], [2, 3], [4, 5], [6, 7]],
        ins=[d.cb1_in.opt()], outs=[d.cb1_out.opt()]), sig="cc")
    p.wait("gpsimd", "cc")
    s_cb.dma("gpsimd", lambda e: e.dma_start(
        out=g.sr16[:], in_=d.cb1_out[0:1024, :].rearrange("(a p) v -> p a v", p=128)))
    s_cb.wait_data("vector")
    p.drain("vector")
    p.emit("vector", lambda e: e.tensor_scalar(
        out=g.sr16[:].rearrange("p a b -> p (a b)"),
        in0=g.sr16[:].rearrange("p a b -> p (a b)"),
        scalar1=t.odd[:, 0:1], scalar2=None, op0=ALU.mult), sig="ve")
    srm = p.snap("ve")
    p.wait("scalar", "ve", srm)
    p.emit("scalar", lambda e: e.engine_nop() if False else e.nop(), sig="se")
    sr_done = p.snap("se")

    p.barrier()
    s_orl.free = ("y2", s_ost.ld)
    s_sgrl.free = ("y3", s_sgst.ld)
    for H in range(n_half):
        s_orl.dma("sync", lambda e, H=H: e.dma_start(
            out=g.o[:], in_=d.o_hbm[H * HTOK:(H + 1) * HTOK, :].rearrange("(a p) n -> p a n", p=128)))
        s_sgrl.dma("sync", lambda e, H=H: e.dma_start(
            out=g.sgf[:], in_=d.sg_hbm[:, H * HTOK:(H + 1) * HTOK].rearrange("(a p) t -> p a t", p=128)))
        s_orl.wait_data("vector")
        s_sgrl.wait_data("vector")
        orl_done = p.snap("ve")

        for c in range(HT):
            gch = H * HT + c
            crbk = [bE, bH]
            for h in range(16):
                b = (h % 2) * 64
                hm = h // 2
                pe_ = crbk[h % 2]
                p.wait("tensor", pe_["rd"][0], pe_["rd"][1])
                p.wait("tensor", "se", sr_done)
                p.emit("tensor", lambda e, b=b, hm=hm, gch=gch, pe_=pe_: e.matmul(
                    pe_["t"][:, 0:128], g.q[b:b + 64, hm, gch * 128:(gch + 1) * 128],
                    g.sr16[b:b + 64, hm, :], start=True, stop=True), sig="te")
                pe_["wr"] = ("te", p.snap("te"))
                s_orl.wait_data("vector")
                bank_read(p, "vector", pe_,
                          lambda e, c=c, h=h, pe_=pe_: e.tensor_tensor(
                              out=g.o[:, c, h * 128:(h + 1) * 128],
                              in0=pe_["t"][:, 0:128], in1=g.o[:, c, h * 128:(h + 1) * 128],
                              op=ALU.add), "ve")
            corr_done = p.snap("ve")

            p.wait("vector", "ve", corr_done)
            p.drain("vector")
            for h in range(16):
                p.emit("vector", lambda e, c=c, h=h: e.bn_stats(
                    out=g.bns16[:, h, :], in_=g.o[:, c, h * 128:(h + 1) * 128]))
            p.drain("vector")
            for h in range(16):
                p.emit("vector", lambda e, h=h: e.bn_aggr(
                    out=g.mv16[:, h, :], in_=g.bns16[:, h, 0:6]))
            p.drain("vector")
            p.emit("vector", lambda e: e.tensor_copy(g.sd16[:], g.mv16[:, :, 1:2]), sig="ve")
            gn1 = p.snap("ve")
            p.wait("scalar", "ve", gn1)
            p.emit("scalar", lambda e: e.activation(
                out=g.sd16[:], in_=g.sd16[:], func=ACT.Sqrt, bias=t.epscol[:, 0:1]), sig="se")
            gn2 = p.snap("se")
            p.wait("vector", "se", gn2)
            p.emit("vector", lambda e: e.reciprocal(out=g.rs16[:], in_=g.sd16[:]))
            p.drain("vector")
            gnt_free = [0, 0]
            ont_free = [0, 0]
            for h in range(16):
                gntb = g.at[h % 2]
                p.wait("vector", "te", gnt_free[h % 2])
                p.emit("vector", lambda e, c=c, h=h, gntb=gntb: e.tensor_scalar(
                    out=gntb[:], in0=g.o[:, c, h * 128:(h + 1) * 128],
                    scalar1=g.mv16[:, h, 0:1], scalar2=g.rs16[:, h:h + 1],
                    op0=ALU.subtract, op1=ALU.mult), sig="ve")
                onr = p.snap("ve")
                bank = bT[h % 2]
                p.wait("tensor", bank["rd"][0], bank["rd"][1])
                p.wait("tensor", "ve", onr)
                p.emit("tensor", lambda e, bank=bank, gntb=gntb: e.transpose(
                    bank["t"][:, 0:128], gntb[:], t.id16[:]), sig="te")
                bank["wr"] = ("te", p.snap("te"))
                gnt_free[h % 2] = p.snap("te")
                ontb = g.ont[h % 2]
                p.wait("vector", "ve", ont_free[h % 2])
                bank_read(p, "vector", bank,
                          lambda e, ontb=ontb, bank=bank: e.tensor_copy(ontb[:], bank["t"][:, 0:128]), "ve")
                p.drain("vector")
                p.emit("vector", lambda e, ontb=ontb, c=c, h=h: e.tensor_tensor(
                    out=g.sgf[:, h, c * 128:(c + 1) * 128], in0=ontb[:],
                    in1=g.sgf[:, h, c * 128:(c + 1) * 128], op=ALU.mult), sig="ve")
                ont_free[h % 2] = p.snap("ve")
            p_done = p.snap("ve")

        p_done_all = p.snap("ve")
        cbanks = [bank_new(g.pE), bank_new(g.pF), bank_new(g.pG), bank_new(g.pH)]
        cbanks[0]["rd"] = bE["rd"]
        cbanks[1]["rd"] = bF["rd"]
        cbanks[2]["rd"] = bG["rd"]
        cbanks[3]["rd"] = bH["rd"]
        for nch in range(4):
            for dch in range(16):
                wsl = s_wo[(nch * 16 + dch) % 2]
                wsl.dma("gpsimd", lambda e, dch=dch, nch=nch, wsl=wsl: e.dma_start(
                    out=g.wos[s_wo.index(wsl)][:],
                    in_=d.gla_wo[dch, :, nch * 512:(nch + 1) * 512]))
                for c in range(HT):
                    bank = cbanks[c]
                    if dch == 0:
                        p.wait("tensor", bank["rd"][0], bank["rd"][1])
                        p.wait("tensor", "ve", p_done_all)
                    wsl.wait_data("tensor")
                    p.emit("tensor", lambda e, c=c, dch=dch, wsl=wsl, bank=bank: e.matmul(
                        bank["t"][:, 0:512], g.sgf[:, dch, c * 128:(c + 1) * 128],
                        g.wos[s_wo.index(wsl)][:],
                        start=(dch == 0), stop=(dch == 15)),
                        sig=("te" if c == HT - 1 else None))
                if dch == 15:
                    for c in range(HT):
                        cbanks[c]["wr"] = ("te", p.snap("te"))
                wsl.set_free("te")
            for c in range(HT):
                gch = H * HT + c
                xsl = s_xrt[c % 2]
                xsl.dma("sync", lambda e, gch=gch, nch=nch, c=c: e.dma_start(
                    out=g.xst[c % 2][:],
                    in_=x_src[gch * 128:(gch + 1) * 128, nch * 512:(nch + 1) * 512]))
                xsl.wait_data("vector")
                bank = cbanks[c]
                p.wait("vector", bank["wr"][0], bank["wr"][1])
                p.wait("vector", "y8" if c % 2 == 0 else "y9", s_xout[c % 2].ld)
                p.emit("vector", lambda e, c=c, bank=bank: e.tensor_tensor(
                    out=g.ost[c % 2][:], in0=bank["t"][:, 0:512],
                    in1=g.xst[c % 2][:], op=ALU.add), sig="ve")
                bank["rd"] = ("ve", p.snap("ve"))
                xsl.set_free("ve")
                xo_done = p.snap("ve")
                p.wait("sync", "ve", xo_done)
                s_xout[c % 2].dma("sync", lambda e, gch=gch, nch=nch, c=c: e.dma_start(
                    out=x_dst[gch * 128:(gch + 1) * 128, nch * 512:(nch + 1) * 512],
                    in_=g.ost[c % 2][:]))
        bE["rd"] = cbanks[0]["rd"]
        bF["rd"] = cbanks[1]["rd"]
        bG["rd"] = cbanks[2]["rd"]
        bH["rd"] = cbanks[3]["rd"]
        bE["wr"] = cbanks[0]["wr"]
        bF["wr"] = cbanks[1]["wr"]
        bG["wr"] = cbanks[2]["wr"]
        bH["wr"] = cbanks[3]["wr"]
        s_orl.free = ("ve", p.snap("ve"))
        s_sgrl.free = ("te", p.snap("te"))
    p.wait("sync", "y8", s_xout[0].ld)
    p.wait("sync", "y9", s_xout[1].ld)
    p.barrier()


def alloc_mamba(nc, es, cfg):
    T = cfg.T
    n_tt = T // 128
    sb = lambda n, s, dt=F32: es.enter_context(nc.sbuf_tensor(n, list(s), dt))
    ps = lambda n, s, dt=F32: es.enter_context(nc.psum_tensor(n, list(s), dt))
    m = NS()
    m.xlnT = sb("m_xlnT", (128, 16, T), BF16)
    m.S32 = sb("m_S32", (128, 32, 128))
    m.S16 = sb("m_S16", (128, 32, 128), BF16)
    m.strip = [sb("m_st0", (128, 1024), BF16), sb("m_st1", (128, 1024), BF16)]
    m.gstrip = [sb("m_gs0", (128, 1024), BF16), sb("m_gs1", (128, 1024), BF16)]
    m.tails = sb("m_tails", (128, 34, 3), BF16)
    m.dts = [sb("m_dts0", (128, 32)), sb("m_dts1", (128, 32)), sb("m_dts2", (128, 32))]
    m.wtile = [sb("m_wt0", (128, 2048), BF16), sb("m_wt1", (128, 2048), BF16)]
    es_b = ExitStack()
    sbb = lambda n, s, dt=F32: es_b.enter_context(nc.sbuf_tensor(n, list(s), dt))
    m.hbc = sbb("m_hbc", (128, 34, 259), BF16)
    m.xs = sbb("m_xs", (128, 2, 4096), BF16)
    m.yq = sbb("m_yq", (128, 2, 4096), BF16)
    m.acf = sbb("m_acf", (32, 128))
    m.ac32 = sbb("m_ac32", (128, 8, 32))
    m.btm = sbb("m_btm", (128, 128), BF16)
    m.gtri = sbb("m_gtri", (128, 128), BF16)
    m.xdt_all = sbb("m_xdta", (128, 32, 128), BF16)
    m.bdec_all = sbb("m_bdeca", (128, 32, 128), BF16)
    m.dxs_all = sbb("m_dxsa", (128, 32, 128), BF16)
    m.M_all = sbb("m_Mall", (128, 32, 128), BF16)
    m.dsl4 = sbb("m_dsl4", (128, 4, 128))
    m.esl4 = sbb("m_esl4", (128, 4, 128), BF16)
    m.ytmp = sbb("m_ytmp", (128, 128))
    m.ytmp2 = sbb("m_ytmp2", (128, 128))
    m.cva = sbb("m_cva", (128, 256))
    m.cvb = sbb("m_cvb", (128, 256))
    m.cvc = sbb("m_cvc", (128, 256))
    es_b.close()
    es_t = ExitStack()
    sbt = lambda n, s, dt=F32: es_t.enter_context(nc.sbuf_tensor(n, list(s), dt))
    m.yt = sbt("m_yt", (128, 2, 4096), BF16)
    m.sgt = sbt("m_sgt", (128, 2, 4096), BF16)
    m.ynT = sbt("m_ynT", (128, 32, 256), BF16)
    m.xst = [sbt("m_xst0", (128, 512)), sbt("m_xst1", (128, 512))]
    m.ost = [sbt("m_ost0", (128, 512)), sbt("m_ost1", (128, 512))]
    m.wos = [sbt("m_wos0", (128, 512), BF16), sbt("m_wos1", (128, 512), BF16)]
    m.ssq = sbt("m_ssq", (128, 4))
    es_t.close()
    es_ln = ExitStack()
    m.ln = alloc_ln(nc, es_ln, "ml", with_scores=False, with_xt=True)
    es.enter_context(_CloseOnExit(es_ln))
    m.pA = ps("mp_A", (128, 512))
    m.pB = ps("mp_B", (128, 512))
    m.pC = ps("mp_C", (128, 512))
    m.pD = ps("mp_D", (128, 512))
    m.pT = [ps("mp_T0", (128, 512), BF16), ps("mp_T1", (128, 512), BF16)]
    return m


def emit_mamba(p, cfg, m, x_src, x_dst):
    t, d = p.t, p.t.dram
    T = cfg.T
    n_tt = T // 128
    NB = T // 512
    NQ = T // 256
    st = ln_state(p, m.ln)
    bA = bank_new(m.pA)
    bB = bank_new(m.pB)
    bC = bank_new(m.pC)
    bD = bank_new(m.pD)
    bT = [bank_new(m.pT[0]), bank_new(m.pT[1])]
    bP = [bank_new(m.ln.tr[0]), bank_new(m.ln.tr[1])]
    s_w = [Slot(p, "g0"), Slot(p, "g1")]
    s_acr = Slot(p, "g7")
    s_csp = Slot(p, "g8")
    s_cb = Slot(p, "g5")
    s_cb2 = Slot(p, "g9")
    s_hst = [Slot(p, "y2"), Slot(p, "y3")]
    s_gst = [Slot(p, "y10"), Slot(p, "y11")]
    s_yst = Slot(p, "y6")
    s_hrl = Slot(p, "y7")
    s_yrl = Slot(p, "y8")
    s_sgrl = [Slot(p, "y9"), Slot(p, "y12")]
    s_xst = [Slot(p, "y4"), Slot(p, "y5")]
    s_xout = [Slot(p, "y13"), Slot(p, "y14")]
    s_wo = [Slot(p, "g3"), Slot(p, "g4")]

    p.emit("vector", lambda e: e.memset(m.S32[:], 0.0))
    p.emit("vector", lambda e: e.memset(m.S16[:], 0.0))
    p.emit("vector", lambda e: e.memset(m.ac32[:, 4, :], 0.0), sig="ve")
    sz = p.snap("ve")
    p.wait("tensor", "ve", sz)
    p.wait("scalar", "ve", sz)

    for part in range(NB - 1, -1, -1):
        emit_ln_transpose(p, m.ln, st, x_src, None, part * 4, 4, m.xlnT, part * 512)
    xlnT_done = p.snap("se")

    ns_order = list(range(NB - 1, -1, -1))
    hprog = 0
    for ct in range(34):
        sl = s_w[ct % 2]
        sl.dma("gpsimd", lambda e, ct=ct, sl=sl: e.dma_start(
            out=m.wtile[s_w.index(sl)][:].rearrange("p (k m) -> p k m", k=16),
            in_=d.m_win_hbc[ct].rearrange("k p m -> p k m")))
        for ns in ns_order:
            bank = bP[(ct * NB + ns) % 2]
            p.wait("tensor", bank["rd"][0], bank["rd"][1])
            sl.wait_data("tensor")
            p.wait("tensor", "se", xlnT_done)
            wbuf = m.wtile[ct % 2]
            for dch in range(16):
                p.emit("tensor", lambda e, wbuf=wbuf, dch=dch, ns=ns, bank=bank: e.matmul(
                    bank["t"][:, 0:512], wbuf[:, dch * 128:(dch + 1) * 128],
                    m.xlnT[:, dch, ns * 512:(ns + 1) * 512],
                    start=(dch == 0), stop=(dch == 15)),
                    sig=("te" if dch == 15 else None))
            bank["wr"] = ("te", p.snap("te"))
            stn = (ct * NB + ns) % 2
            p.wait("vector", "y2" if stn == 0 else "y3", s_hst[stn].ld)
            bank_read(p, "vector", bank,
                      lambda e, stn=stn, bank=bank: e.tensor_copy(
                          m.strip[stn][:, 0:512], bank["t"][:, 0:512]), "ve")
            sdone = p.snap("ve")
            p.wait("sync", "ve", sdone)
            s_hst[stn].dma("sync", lambda e, ct=ct, ns=ns, stn=stn: e.dma_start(
                out=d.hbc_hbm[ct, :, 3 + ns * 512: 3 + (ns + 1) * 512], in_=m.strip[stn][:, 0:512]))
        sl.set_free("te")
        if ct == 33:
            pass
    hbc_done_y2 = s_hst[0].ld
    hbc_done_y3 = s_hst[1].ld

    p.wait("gpsimd", "y2", hbc_done_y2)
    p.wait("gpsimd", "y3", hbc_done_y3)
    s_cb2.dma("gpsimd", lambda e: e.dma_start(
        out=d.cb2_in.rearrange("(c p) k -> c p k", p=128),
        in_=d.hbc_hbm[:, :, 3 + T - 3: 3 + T]))
    s_cb2.wait_data("gpsimd")
    p.emit("gpsimd", lambda e: e.collective_compute(
        "AllGather", ALU.bypass,
        replica_groups=[[0, 1], [2, 3], [4, 5], [6, 7]],
        ins=[d.cb2_in.opt()], outs=[d.cb2_out.opt()]), sig="cc")
    p.wait("gpsimd", "cc")
    s_cb2.dma("gpsimd", lambda e: e.dma_start(
        out=m.tails[:].rearrange("p c k -> p c k"),
        in_=d.cb2_out[0:34 * 128].rearrange("(c p) k -> p c k", p=128)))
    s_cb2.wait_data("vector")
    p.drain("vector")
    p.emit("vector", lambda e: e.tensor_scalar(
        out=m.tails[:], in0=m.tails[:], scalar1=t.odd[:, 0:1], scalar2=None,
        op0=ALU.mult), sig="ve")
    tl = p.snap("ve")
    p.wait("sync", "ve", tl)
    s_hst[0].dma("sync", lambda e: e.dma_start(
        out=d.hbc_hbm[:, :, 0:3].rearrange("c p k -> p c k"), in_=m.tails[:]))
    tails_st = s_hst[0].ld

    for mt in range(32):
        sl = s_w[mt % 2]
        sl.dma("gpsimd", lambda e, mt=mt, sl=sl: e.dma_start(
            out=m.wtile[s_w.index(sl)][:].rearrange("p (k m) -> p k m", k=16),
            in_=d.m_win_gate[mt].rearrange("k p m -> p k m")))
        for ns in range(NB):
            bank = bP[(mt * NB + ns) % 2]
            p.wait("tensor", bank["rd"][0], bank["rd"][1])
            sl.wait_data("tensor")
            for dch in range(16):
                p.emit("tensor", lambda e, mt=mt, dch=dch, ns=ns, bank=bank: e.matmul(
                    bank["t"][:, 0:512], m.wtile[mt % 2][:, dch * 128:(dch + 1) * 128],
                    m.xlnT[:, dch, ns * 512:(ns + 1) * 512],
                    start=(dch == 0), stop=(dch == 15)),
                    sig=("te" if dch == 15 else None))
            bank["wr"] = ("te", p.snap("te"))
            stn = (mt * NB + ns) % 2
            p.wait("scalar", "y10" if stn == 0 else "y11", s_gst[stn].ld)
            bank_read(p, "scalar", bank,
                      lambda e, stn=stn, bank=bank: e.activation(
                          out=m.gstrip[stn][:, 0:512], in_=bank["t"][:, 0:512],
                          func=ACT.Silu), "se")
            gdone = p.snap("se")
            p.wait("sync", "se", gdone)
            s_gst[stn].dma("sync", lambda e, mt=mt, ns=ns, stn=stn: e.dma_start(
                out=d.sgate_hbm[mt * 128:(mt + 1) * 128, ns * 512:(ns + 1) * 512],
                in_=m.gstrip[stn][:, 0:512]))
        sl.set_free("te")

    for r in range(n_tt):
        bank = bC if r % 2 == 0 else bD
        p.wait("tensor", bank["rd"][0], bank["rd"][1])
        wsl = s_w[r % 2]
        wsl.dma("gpsimd", lambda e, wsl=wsl: e.dma_start(
            out=m.wtile[s_w.index(wsl)][:, 0:512], in_=d.m_win_dt))
        wsl.wait_data("tensor")
        p.wait("tensor", "se", xlnT_done)
        for dch in range(16):
            p.emit("tensor", lambda e, r=r, dch=dch, wsl=wsl, bank=bank: e.matmul(
                bank["t"][:, 0:32], m.xlnT[:, dch, r * 128:(r + 1) * 128],
                m.wtile[s_w.index(wsl)][:, dch * 32:(dch + 1) * 32],
                start=(dch == 0), stop=(dch == 15)),
                sig=("te" if dch == 15 else None))
        bank["wr"] = ("te", p.snap("te"))
        wsl.set_free("te")
        bank_read(p, "vector", bank,
                  lambda e, bank=bank: e.tensor_tensor(
                      out=m.dts[0][:], in0=bank["t"][:, 0:32], in1=t.dtbias[:], op=ALU.add), "ve")
        z1 = p.snap("ve")
        p.wait("scalar", "ve", z1)
        p.emit("scalar", lambda e: e.activation(
            out=m.dts[1][:], in_=m.dts[0][:], func=ACT.Sigmoid, scale=-1.0), sig="se")
        p.drain("scalar")
        p.emit("scalar", lambda e: e.activation(
            out=m.dts[2][:], in_=m.dts[1][:], func=ACT.Ln), sig="se")
        z3 = p.snap("se")
        p.wait("vector", "se", z3)
        p.drain("vector")
        p.emit("vector", lambda e, r=r: e.tensor_scalar(
            out=t.dt_tm[:, r, :], in0=m.dts[2][:], scalar1=-1.0, scalar2=100.0,
            op0=ALU.mult, op1=ALU.min), sig="ve")
    dt_done = p.snap("ve")

    for q in range(NQ):
        if q == 0:
            s_hrl.free = ("y2", s_hst[0].ld)
        s_hrl.dma("sync", lambda e, q=q: e.dma_start(
            out=m.hbc[:], in_=d.hbc_hbm[:, :, q * 256: q * 256 + 259].rearrange("c p k -> p c k")))
        s_hrl.wait_data("vector")
        s_hrl.wait_data("scalar")
        hrl_ld = p.snap("ve")

        for ct in range(34):
            p.wait("scalar", "ve", getattr(st, "cv_free", 0))
            p.emit("scalar", lambda e, ct=ct: e.activation(
                out=m.cva[:], in_=m.hbc[:, ct, 0:256],
                func=ACT.Copy, scale=t.convw[:, ct * 4: ct * 4 + 1], bias=0.0), sig="se")
            tap0 = p.snap("se")
            p.wait("vector", "se", tap0)
            p.emit("vector", lambda e, ct=ct: e.scalar_tensor_tensor(
                out=m.cvb[:], in0=m.hbc[:, ct, 1:257], scalar=t.convw[:, ct * 4 + 1: ct * 4 + 2],
                in1=m.cva[:], op0=ALU.mult, op1=ALU.add), sig="ve")
            p.drain("vector")
            p.emit("vector", lambda e, ct=ct: e.scalar_tensor_tensor(
                out=m.cvc[:], in0=m.hbc[:, ct, 2:258], scalar=t.convw[:, ct * 4 + 2: ct * 4 + 3],
                in1=m.cvb[:], op0=ALU.mult, op1=ALU.add), sig="ve")
            p.drain("vector")
            p.emit("vector", lambda e, ct=ct: e.scalar_tensor_tensor(
                out=m.cvb[:], in0=m.hbc[:, ct, 3:259], scalar=t.convw[:, ct * 4 + 3: ct * 4 + 4],
                in1=m.cvc[:], op0=ALU.mult, op1=ALU.add), sig="ve")
            cv = p.snap("ve")
            p.wait("scalar", "ve", cv)
            p.emit("scalar", lambda e, ct=ct: e.activation(
                out=m.hbc[:, ct, 3:259], in_=m.cvb[:], func=ACT.Silu,
                bias=t.convb[:, ct:ct + 1]), sig="se")
            st.cv_free = p.snap("se")
        conv_done = p.snap("se")

        p.wait("vector", "se", conv_done)
        for c2 in range(2):
            p.emit("vector", lambda e, q=q, c2=c2: e.tensor_copy(
                t.c_keep[:, (q * 2 + c2) * 128:(q * 2 + c2 + 1) * 128],
                m.hbc[:, 33, 3 + c2 * 128: 3 + (c2 + 1) * 128]), sig="ve")
        for ct in range(32):
            for c2 in range(2):
                bank = bT[(ct * 2 + c2) % 2]
                p.wait("tensor", bank["rd"][0], bank["rd"][1])
                p.wait("tensor", "se", conv_done)
                p.emit("tensor", lambda e, ct=ct, c2=c2, bank=bank: e.transpose(
                    bank["t"][:, 0:128], m.hbc[:, ct, 3 + c2 * 128: 3 + (c2 + 1) * 128],
                    t.id16[:]), sig="te")
                bank["wr"] = ("te", p.snap("te"))
                bank_read(p, "vector", bank,
                          lambda e, ct=ct, c2=c2, bank=bank: e.tensor_copy(
                              m.xs[:, c2, ct * 128:(ct + 1) * 128], bank["t"][:, 0:128]), "ve")
        xs_done = p.snap("ve")

        for c2 in range(2):
            gch = q * 2 + c2
            s16_prev = p.snap("se")
            p.wait("vector", "ve", dt_done)
            p.wait("vector", "se", p.snap("se"))
            p.drain("vector")
            p.emit("vector", lambda e, gch=gch: e.scalar_tensor_tensor(
                out=m.ac32[:, 0, :], in0=t.dt_tm[:, gch, :], scalar=-1.0, in1=t.apos[:],
                op0=ALU.mult, op1=ALU.mult), sig="ve")
            ad_done = p.snap("ve")
            p.wait("tensor", bC["rd"][0], bC["rd"][1])
            p.wait("tensor", bD["rd"][0], bD["rd"][1])
            p.wait("tensor", "ve", ad_done)
            p.emit("tensor", lambda e: e.matmul(
                m.pC[:, 0:32], t.tri[:], m.ac32[:, 0, :], start=True, stop=True), sig="te")
            bC["wr"] = ("te", p.snap("te"))
            p.emit("tensor", lambda e: e.matmul(
                m.pD[0:32, 0:128], m.ac32[:, 0, :], t.tri[:], start=True, stop=True), sig="te")
            bD["wr"] = ("te", p.snap("te"))
            bank_read(p, "vector", bC,
                      lambda e, gch=gch: e.tensor_copy(t.acum_tm[:, gch, :], m.pC[:, 0:32]), "ve")
            bank_read(p, "vector", bD,
                      lambda e: e.tensor_copy(m.acf[:], m.pD[0:32, 0:128]), "ve")
            ac_done = p.snap("ve")
            p.wait("gpsimd", "ve", ac_done)
            s_acr.dma("gpsimd", lambda e: e.dma_start(out=t.acrow[:, 0:4096], in_=m.acf[:]))
            p.wait("tensor", bB["rd"][0], bB["rd"][1])
            s_acr.wait_data("tensor")
            p.emit("tensor", lambda e: e.matmul(
                m.pB[:, 0:32], t.ones1[:], t.acrow[0:1, 127:4096:128],
                start=True, stop=True), sig="te")
            bB["wr"] = ("te", p.snap("te"))
            bank_read(p, "vector", bB,
                      lambda e: e.tensor_copy(m.ac32[:, 1, :], m.pB[:, 0:32]), "ve")
            csp = p.snap("ve")
            p.wait("scalar", "ve", csp)
            p.emit("scalar", lambda e: e.activation(
                out=m.ac32[:, 2, :], in_=m.ac32[:, 1, :], func=ACT.Exp), sig="se")
            p.wait("vector", "ve", ac_done)
            p.drain("vector")
            p.emit("vector", lambda e, gch=gch: e.tensor_tensor(
                out=m.ac32[:, 3, :], in0=t.acum_tm[:, gch, :], in1=m.ac32[:, 1, :],
                op=ALU.subtract), sig="ve")
            d1 = p.snap("ve")
            p.wait("scalar", "ve", d1)
            p.emit("scalar", lambda e: e.activation(
                out=m.ac32[:, 3, :], in_=m.ac32[:, 3, :], func=ACT.Exp, scale=-1.0), sig="se")
            p.emit("scalar", lambda e, gch=gch: e.activation(
                out=m.ac32[:, 5, :], in_=t.acum_tm[:, gch, :], func=ACT.Exp), sig="se")
            p.emit("vector", lambda e, gch=gch: e.tensor_tensor(
                out=m.ac32[:, 6, :], in0=t.acum_tm[:, gch, :], in1=m.ac32[:, 4, :],
                op=ALU.add), sig="ve")
            d2 = p.snap("ve")
            p.wait("scalar", "ve", d2)
            p.emit("scalar", lambda e, gch=gch: e.activation(
                out=t.sdecabs[:, gch, :], in_=m.ac32[:, 6, :], func=ACT.Exp), sig="se")
            p.drain("vector")
            p.emit("vector", lambda e: e.tensor_tensor(
                out=m.ac32[:, 4, :], in0=m.ac32[:, 4, :], in1=m.ac32[:, 1, :], op=ALU.add), sig="ve")
            pan_done_se = p.snap("se")
            pan_done_ve = p.snap("ve")

            p.wait("tensor", bD["rd"][0], bD["rd"][1])
            p.wait("tensor", "se", conv_done)
            p.emit("tensor", lambda e, c2=c2: e.matmul(
                m.pD[:, 0:128], m.hbc[:, 32, 3 + c2 * 128: 3 + (c2 + 1) * 128],
                m.hbc[:, 33, 3 + c2 * 128: 3 + (c2 + 1) * 128], start=True, stop=True), sig="te")
            bD["wr"] = ("te", p.snap("te"))
            bank_read(p, "vector", bD,
                      lambda e: e.tensor_tensor(
                          out=m.gtri[:], in0=m.pD[:, 0:128], in1=t.tri[:], op=ALU.mult), "ve")
            bank = bT[0]
            p.wait("tensor", bank["rd"][0], bank["rd"][1])
            p.wait("tensor", "se", conv_done)
            p.emit("tensor", lambda e, c2=c2, bank=bank: e.transpose(
                bank["t"][:, 0:128], m.hbc[:, 32, 3 + c2 * 128: 3 + (c2 + 1) * 128],
                t.id16[:]), sig="te")
            bank["wr"] = ("te", p.snap("te"))
            bank_read(p, "vector", bank,
                      lambda e, bank=bank: e.tensor_copy(m.btm[:], bank["t"][:, 0:128]), "ve")
            gq_done = p.snap("ve")

            p.wait("vector", "ve", xs_done)
            p.wait("vector", "se", pan_done_se)
            p.wait("vector", "te", getattr(st, "mscr_free", 0))
            p.drain("vector")
            p.emit("vector", lambda e, c2=c2, gch=gch: e.tensor_tensor(
                out=m.xdt_all[:], in0=m.xs[:, c2, :].rearrange("p (h v) -> p h v", v=128),
                in1=t.dt_tm[:, gch, :].unsqueeze(2).broadcast_to((128, 32, 128)),
                op=ALU.mult), sig="ve")
            p.emit("vector", lambda e: e.tensor_tensor(
                out=m.bdec_all[:], in0=m.btm[:].unsqueeze(1).broadcast_to((128, 32, 128)),
                in1=m.ac32[:, 3, :].unsqueeze(2).broadcast_to((128, 32, 128)),
                op=ALU.mult), sig="ve")
            p.emit("vector", lambda e, c2=c2: e.tensor_tensor(
                out=m.dxs_all[:], in0=m.xs[:, c2, :].rearrange("p (h v) -> p h v", v=128),
                in1=t.dpanel[:].unsqueeze(2).broadcast_to((128, 32, 128)),
                op=ALU.mult), sig="ve")
            batched_done = p.snap("ve")

            prev_em = 0
            prev_mtt = 0
            for hg in range(8):
                bank = bB if hg % 2 == 0 else bD
                p.wait("tensor", bank["rd"][0], bank["rd"][1])
                s_acr.wait_data("tensor")
                p.emit("tensor", lambda e, hg=hg, bank=bank: e.matmul(
                    bank["t"][:, 0:512], t.ones1[:], t.acrow[0:1, hg * 512:(hg + 1) * 512],
                    start=True, stop=True), sig="te")
                bank["wr"] = ("te", p.snap("te"))
                p.wait("vector", "ve", gq_done)
                p.wait("vector", "se", prev_em)
                bank_read(p, "vector", bank,
                          lambda e, hg=hg, gch=gch, bank=bank: e.tensor_tensor(
                              out=m.dsl4[:], in0=bank["t"][:, 0:512].rearrange("p (h v) -> p h v", v=128),
                              in1=t.acum_tm[:, gch, hg * 4:(hg + 1) * 4].unsqueeze(2).broadcast_to((128, 4, 128)),
                              op=ALU.subtract), "ve")
                p.drain("vector")
                p.emit("vector", lambda e: e.tensor_scalar(
                    out=m.dsl4[:], in0=m.dsl4[:], scalar1=0.0, scalar2=None,
                    op0=ALU.min), sig="ve")
                dm = p.snap("ve")
                p.wait("scalar", "ve", dm)
                p.wait("scalar", "ve", prev_mtt)
                p.emit("scalar", lambda e: e.activation(
                    out=m.esl4[:], in_=m.dsl4[:], func=ACT.Exp), sig="se")
                em = p.snap("se")
                prev_em = em
                p.wait("vector", "se", em)
                p.emit("vector", lambda e, hg=hg: e.tensor_tensor(
                    out=m.M_all[:, hg * 4:(hg + 1) * 4, :], in0=m.esl4[:],
                    in1=m.gtri[:].unsqueeze(1).broadcast_to((128, 4, 128)),
                    op=ALU.mult), sig="ve")
                prev_mtt = p.snap("ve")
            M_done = p.snap("ve")

            ya = [bA, bank_new(m.ln.tr[0])]
            yb_ = [bank_new(m.pB), bank_new(m.ln.tr[1])]
            tc_ = [bank_new(m.pC), bank_new(m.pD)]
            ya[1]["rd"] = bP[0]["rd"]
            yb_[0]["rd"] = bB["rd"]
            yb_[1]["rd"] = bP[1]["rd"]
            tc_[0]["rd"] = bC["rd"]
            tc_[1]["rd"] = bD["rd"]
            for h in range(32):
                pa = ya[h % 2]
                pb = yb_[h % 2]
                pc = tc_[h % 2]
                p.wait("tensor", pa["rd"][0], pa["rd"][1])
                p.wait("tensor", "ve", M_done)
                p.emit("tensor", lambda e, h=h, pa=pa: e.matmul(
                    pa["t"][:, 0:128], m.M_all[:, h, :], m.xdt_all[:, h, :],
                    start=True, stop=True), sig="te")
                pa["wr"] = ("te", p.snap("te"))
                p.wait("tensor", pb["rd"][0], pb["rd"][1])
                p.wait("tensor", "se", s16_prev)
                p.emit("tensor", lambda e, c2=c2, h=h, pb=pb: e.matmul(
                    pb["t"][:, 0:128], m.hbc[:, 33, 3 + c2 * 128: 3 + (c2 + 1) * 128],
                    m.S16[:, h, :], start=True, stop=True), sig="te")
                pb["wr"] = ("te", p.snap("te"))
                p.wait("tensor", pc["rd"][0], pc["rd"][1])
                p.emit("tensor", lambda e, h=h, pc=pc: e.matmul(
                    pc["t"][:, 0:128], m.bdec_all[:, h, :], m.xdt_all[:, h, :],
                    start=True, stop=True), sig="te")
                pc["wr"] = ("te", p.snap("te"))
                ytb = m.ytmp if h % 2 == 0 else m.ytmp2
                p.wait("vector", pa["wr"][0], pa["wr"][1])
                p.emit("vector", lambda e, h=h, pa=pa, ytb=ytb: e.tensor_tensor(
                    out=ytb[:], in0=pa["t"][:, 0:128], in1=m.dxs_all[:, h, :],
                    op=ALU.add), sig="ve")
                pa["rd"] = ("ve", p.snap("ve"))
                p.wait("vector", pb["wr"][0], pb["wr"][1])
                p.wait("vector", "y6", s_yst.ld)
                p.drain("vector")
                p.emit("vector", lambda e, c2=c2, h=h, pb=pb, ytb=ytb: e.scalar_tensor_tensor(
                    out=m.yq[:, c2, h * 128:(h + 1) * 128], in0=pb["t"][:, 0:128],
                    scalar=m.ac32[:, 5, h:h + 1], in1=ytb[:],
                    op0=ALU.mult, op1=ALU.add), sig="ve")
                pb["rd"] = ("ve", p.snap("ve"))
                p.wait("vector", pc["wr"][0], pc["wr"][1])
                p.emit("vector", lambda e, h=h, pc=pc: e.scalar_tensor_tensor(
                    out=m.S32[:, h, :], in0=m.S32[:, h, :], scalar=m.ac32[:, 2, h:h + 1],
                    in1=pc["t"][:, 0:128], op0=ALU.mult, op1=ALU.add), sig="ve")
                pc["rd"] = ("ve", p.snap("ve"))
                sup = p.snap("ve")
                p.wait("scalar", "ve", sup)
                p.emit("scalar", lambda e, h=h: e.copy(
                    out=m.S16[:, h, :], in_=m.S32[:, h, :]), sig="se")
            st.mscr_free = p.snap("te")
            bA["rd"] = ya[0]["rd"]
            bA["wr"] = ya[0]["wr"]
            bP[0]["rd"] = ya[1]["rd"]
            bB["rd"] = yb_[0]["rd"]
            bB["wr"] = yb_[0]["wr"]
            bP[1]["rd"] = yb_[1]["rd"]
            bC["rd"] = tc_[0]["rd"]
            bC["wr"] = tc_[0]["wr"]
            bD["rd"] = tc_[1]["rd"]
            bD["wr"] = tc_[1]["wr"]
        yq_done = p.snap("ve")
        p.wait("sync", "ve", yq_done)
        s_yst.dma("sync", lambda e, q=q: e.dma_start(
            out=d.y_hbm[q * 256:(q + 1) * 256, :].rearrange("(a p) n -> p a n", p=128),
            in_=m.yq[:]))

    slast = p.snap("se")
    p.wait("gpsimd", "se", slast)
    p.wait("gpsimd", "ve", p.snap("ve"))
    s_cb.dma("gpsimd", lambda e: e.dma_start(
        out=d.cb3_in.rearrange("(a p) v -> p a v", p=128), in_=m.S32[:]))
    s_cb.wait_data("gpsimd")
    p.emit("gpsimd", lambda e: e.collective_compute(
        "AllGather", ALU.bypass,
        replica_groups=[[0, 1], [2, 3], [4, 5], [6, 7]],
        ins=[d.cb3_in.opt()], outs=[d.cb3_out.opt()]), sig="cc")
    p.wait("gpsimd", "cc")
    s_cb.dma("gpsimd", lambda e: e.dma_start(
        out=m.S32[:], in_=d.cb3_out[0:4096, :].rearrange("(a p) v -> p a v", p=128)))
    s_cb.wait_data("vector")
    p.drain("vector")
    p.emit("vector", lambda e: e.tensor_scalar(
        out=m.S32[:].rearrange("p a b -> p (a b)"),
        in0=m.S32[:].rearrange("p a b -> p (a b)"),
        scalar1=t.odd[:, 0:1], scalar2=None, op0=ALU.mult), sig="ve")
    sm = p.snap("ve")
    p.wait("scalar", "ve", sm)
    p.emit("scalar", lambda e: e.copy(
        out=m.S16[:].rearrange("p a b -> p (a b)"),
        in_=m.S32[:].rearrange("p a b -> p (a b)")), sig="se")
    srdone = p.snap("se")

    s_yrl.free = ("y6", s_yst.ld)
    p.wait("sync", "y10", s_gst[0].ld)
    p.wait("sync", "y11", s_gst[1].ld)
    for q in range(NQ):
        s_yrl.dma("sync", lambda e, q=q: e.dma_start(
            out=m.yt[:], in_=d.y_hbm[q * 256:(q + 1) * 256, :].rearrange("(a p) n -> p a n", p=128)))
        s_yrl.wait_data("vector")
        for r2 in range(2):
            ssl = s_sgrl[r2]
            ssl.dma("sync", lambda e, q=q, r2=r2: e.dma_start(
                out=m.sgt[:, r2, :],
                in_=d.sgate_hbm[:, q * 256 + r2 * 128: q * 256 + (r2 + 1) * 128],
                transpose=True))
            ssl.wait_data("vector")
        for c2 in range(2):
            gch = q * 2 + c2
            for h in range(32):
                p.wait("tensor", bB["rd"][0], bB["rd"][1])
                p.wait("tensor", "se", srdone)
                p.emit("tensor", lambda e, gch=gch, h=h: e.matmul(
                    m.pB[:, 0:128], t.c_keep[:, gch * 128:(gch + 1) * 128],
                    m.S16[:, h, :], start=True, stop=True), sig="te")
                bB["wr"] = ("te", p.snap("te"))
                bank_read(p, "vector", bB,
                          lambda e, c2=c2, h=h, gch=gch: e.scalar_tensor_tensor(
                              out=m.yt[:, c2, h * 128:(h + 1) * 128], in0=m.pB[:, 0:128],
                              scalar=t.sdecabs[:, gch, h:h + 1],
                              in1=m.yt[:, c2, h * 128:(h + 1) * 128],
                              op0=ALU.mult, op1=ALU.add), "ve")
        corr_done = p.snap("ve")
        for r2 in range(2):
            p.wait("vector", "ve", corr_done)
            p.drain("vector")
            p.emit("vector", lambda e, r2=r2: e.tensor_tensor(
                out=m.yt[:, r2, :], in0=m.yt[:, r2, :], in1=m.sgt[:, r2, :], op=ALU.mult), sig="ve")
            yg1 = p.snap("ve")
            p.wait("scalar", "ve", yg1)
            p.emit("scalar", lambda e, r2=r2: e.activation(
                out=m.sgt[:, r2, :], in_=m.yt[:, r2, :], func=ACT.Square,
                accum_out=m.ssq[:, r2:r2 + 1]), sig="se")
            rms1 = p.snap("se")
            p.wait("scalar", "se", rms1)
            p.drain("scalar")
            p.emit("scalar", lambda e, r2=r2: e.activation(
                out=m.ssq[:, 2 + r2:3 + r2], in_=m.ssq[:, r2:r2 + 1], func=ACT.Sqrt,
                scale=1.0 / 4096.0, bias=t.epscol[:, 0:1]), sig="se")
            rms2 = p.snap("se")
            p.wait("vector", "se", rms2)
            p.emit("vector", lambda e, r2=r2: e.reciprocal(
                out=m.ssq[:, r2:r2 + 1], in_=m.ssq[:, 2 + r2:3 + r2]))
            p.drain("vector")
            p.emit("vector", lambda e, r2=r2: e.tensor_scalar(
                out=m.yt[:, r2, :], in0=m.yt[:, r2, :], scalar1=m.ssq[:, r2:r2 + 1],
                scalar2=None, op0=ALU.mult), sig="ve")
            p.drain("vector")
            p.emit("vector", lambda e, r2=r2: e.tensor_tensor(
                out=m.yt[:, r2, :], in0=m.yt[:, r2, :], in1=t.normw[:], op=ALU.mult), sig="ve")
        yn_done = p.snap("ve")
        for fch in range(32):
            for r2 in range(2):
                bank = bT[(fch * 2 + r2) % 2]
                p.wait("tensor", bank["rd"][0], bank["rd"][1])
                p.wait("tensor", "ve", yn_done)
                p.emit("tensor", lambda e, fch=fch, r2=r2, bank=bank: e.transpose(
                    bank["t"][:, 0:128], m.yt[:, r2, fch * 128:(fch + 1) * 128], t.id16[:]),
                    sig="te")
                bank["wr"] = ("te", p.snap("te"))
                bank_read(p, "vector", bank,
                          lambda e, fch=fch, r2=r2, bank=bank: e.tensor_copy(
                              m.ynT[:, fch, r2 * 128:(r2 + 1) * 128], bank["t"][:, 0:128]), "ve")
        ynT_done = p.snap("ve")
        obanks = [bank_new(m.pA), bank_new(m.pC)]
        obanks[0]["rd"] = bA["rd"]
        obanks[1]["rd"] = bC["rd"]
        for nch in range(4):
            for fch in range(32):
                wsl = s_wo[(nch * 32 + fch) % 2]
                wsl.dma("gpsimd", lambda e, fch=fch, nch=nch, wsl=wsl: e.dma_start(
                    out=m.wos[s_wo.index(wsl)][:],
                    in_=d.m_wout[fch, :, nch * 512:(nch + 1) * 512]))
                for r2 in range(2):
                    bank = obanks[r2]
                    if fch == 0:
                        p.wait("tensor", bank["rd"][0], bank["rd"][1])
                        p.wait("tensor", "ve", ynT_done)
                    wsl.wait_data("tensor")
                    p.emit("tensor", lambda e, r2=r2, fch=fch, wsl=wsl, bank=bank: e.matmul(
                        bank["t"][:, 0:512], m.ynT[:, fch, r2 * 128:(r2 + 1) * 128],
                        m.wos[s_wo.index(wsl)][:],
                        start=(fch == 0), stop=(fch == 31)),
                        sig=("te" if r2 == 1 else None))
                if fch == 31:
                    for r2 in range(2):
                        obanks[r2]["wr"] = ("te", p.snap("te"))
                wsl.set_free("te")
            for r2 in range(2):
                gt = q * 2 + r2
                xsl = s_xst[r2]
                xsl.dma("sync", lambda e, gt=gt, nch=nch: e.dma_start(
                    out=m.xst[gt % 2][:],
                    in_=x_src[gt * 128:(gt + 1) * 128, nch * 512:(nch + 1) * 512]))
                xsl.wait_data("vector")
                bank = obanks[r2]
                p.wait("vector", bank["wr"][0], bank["wr"][1])
                p.wait("vector", "y13" if r2 == 0 else "y14", s_xout[r2].ld)
                p.emit("vector", lambda e, r2=r2, gt=gt, bank=bank: e.tensor_tensor(
                    out=m.ost[r2][:], in0=bank["t"][:, 0:512], in1=m.xst[gt % 2][:],
                    op=ALU.add), sig="ve")
                bank["rd"] = ("ve", p.snap("ve"))
                xsl.set_free("ve")
                od = p.snap("ve")
                p.wait("sync", "ve", od)
                s_xout[r2].dma("sync", lambda e, gt=gt, nch=nch, r2=r2: e.dma_start(
                    out=x_dst[gt * 128:(gt + 1) * 128, nch * 512:(nch + 1) * 512],
                    in_=m.ost[r2][:]))
        bA["rd"] = obanks[0]["rd"]
        bC["rd"] = obanks[1]["rd"]
        bA["wr"] = obanks[0]["wr"]
        bC["wr"] = obanks[1]["wr"]
        s_yrl.free = ("te", p.snap("te"))
        for r2 in range(2):
            s_sgrl[r2].free = ("ve", p.snap("ve"))
    p.wait("sync", "y13", s_xout[0].ld)
    p.wait("sync", "y14", s_xout[1].ld)
    p.barrier()


def kernel(**inputs):
    cfg = NS(T=1024, phases=("gla", "ffn1", "mamba", "ffn2"))
    g = host_layout(inputs)
    maps = percore_inputs(inputs, g, cfg.T)
    nc = build_program(cfg)
    res = run_bass_kernel_spmd(nc, maps, core_ids=list(range(NCORES)))
    out = np.empty((4, 2048, 2048), dtype=np.float32)
    for c in range(NCORES):
        b, half = c // 2, c % 2
        out[b, half * cfg.T:(half + 1) * cfg.T] = res.results[c]["out"]
    return out


# version 14 (final)
# speedup vs baseline: 1.0820x; 1.0318x over previous
import math
import numpy as np
import ml_dtypes
from contextlib import ExitStack
from types import SimpleNamespace as NS

import concourse.bass as bass
import concourse.mybir as mybir
from concourse.bass_utils import run_bass_kernel_spmd

F32 = mybir.dt.float32
BF16 = mybir.dt.bfloat16
ACT = mybir.ActivationFunctionType
ALU = mybir.AluOpType
AX = mybir.AxisListType

NCORES = 8
GLA_H, GLA_DK, GLA_DV = 16, 64, 128
LNSCALE = -0.5 * math.log(128.0)
G16 = 1.0 / 16.0
TOP_K = 6
EPS = 1e-5

ENGINES = ("tensor", "vector", "scalar", "sync", "gpsimd")


class Pass:

    def __init__(self, nc, eng_name, e, sems, t, al, cfg):
        self.nc, self.eng, self.e = nc, eng_name, e
        self.sem, self.t, self.al, self.cfg = sems, t, al, cfg
        self.cnt = {k: 0 for k in sems if k != "bar"}
        self.waited = {k: 0 for k in sems if k != "bar"}
        self.bar_val = 0

    def emit(self, eng, f, sig=None, inc=1):
        if self.eng == eng:
            inst = f(self.e)
            if sig is not None:
                inst.then_inc(self.sem[sig], inc)
        if sig is not None:
            self.cnt[sig] += inc

    def wait(self, eng, sig, val=None):
        v = self.cnt[sig] if val is None else val
        if v <= 0:
            return
        if self.eng == eng and v > self.waited[sig]:
            self.e.wait_ge(self.sem[sig], v)
            self.waited[sig] = v

    def drain(self, eng):
        if self.eng == eng:
            self.e.drain()

    def snap(self, sig):
        return self.cnt[sig]

    def barrier(self):
        self.bar_val += len(ENGINES)
        self.e.sem_inc(self.sem["bar"], 1)
        self.e.wait_ge(self.sem["bar"], self.bar_val)


def bank_new(t):
    return {"t": t, "rd": ("ve", 0), "wr": ("te", 0)}


class Slot:

    def __init__(self, p, sem):
        self.p, self.sem = p, sem
        self.free = ("te", 0)
        self.ld = 0

    def dma(self, eng, f):
        self.p.wait(eng, *self.free)
        self.p.emit(eng, f, sig=self.sem, inc=16)
        self.ld = self.p.cnt[self.sem]

    def wait_data(self, eng):
        self.p.wait(eng, self.sem, self.ld)

    def set_free(self, sig):
        self.free = (sig, self.p.snap(sig))


def bank_read(p, eng, bank, f, sig):
    p.wait(eng, bank["wr"][0], bank["wr"][1])
    p.emit(eng, f, sig=sig)
    bank["rd"] = (sig, p.snap(sig))


def host_layout(inp):
    g = {}
    f32 = np.float32
    asc = np.ascontiguousarray
    g["gla_wq"] = asc(inp["L0_attn_a_Wq"].T.reshape(16, 128, 8, 128).transpose(2, 0, 1, 3))
    g["gla_wk"] = asc(inp["L0_attn_a_Wk"].T.reshape(16, 128, 8, 128).transpose(2, 0, 1, 3))
    g["gla_wv"] = asc(inp["L0_attn_a_Wv"].T.reshape(16, 128, 16, 128).transpose(2, 0, 1, 3))
    g["gla_wg"] = asc(inp["L0_attn_a_Wg"].T.reshape(16, 128, 16, 128).transpose(2, 0, 1, 3))
    g["gla_wkg1"] = asc(inp["L0_attn_a_Wkg1"].T.reshape(16, 128, 16).transpose(1, 0, 2).reshape(128, 256))
    g["gla_wkg2"] = asc(inp["L0_attn_a_Wkg2"].T)
    g["gla_wo"] = asc(inp["L0_attn_a_Wo"].T.reshape(16, 128, 2048))
    g["gla_bkg2"] = asc(inp["L0_attn_a_bkg2"].reshape(1, 1024))
    g["gla_bg"] = asc(inp["L0_attn_a_bg"].reshape(16, 128).T)

    wg = inp["L0_ffn_wg"].reshape(32, 16, 128, 6, 128).transpose(0, 3, 1, 2, 4)
    sg = inp["L0_ffn_sg"].reshape(16, 128, 2, 6, 128).transpose(2, 3, 0, 1, 4)
    g["ffn_wg"] = asc(np.concatenate([wg, sg], axis=0))
    wu = inp["L0_ffn_wu"].reshape(32, 16, 128, 6, 128).transpose(0, 3, 1, 2, 4)
    su = inp["L0_ffn_su"].reshape(16, 128, 2, 6, 128).transpose(2, 3, 0, 1, 4)
    g["ffn_wu"] = asc(np.concatenate([wu, su], axis=0))
    wd = inp["L0_ffn_wd"].reshape(32, 6, 128, 2048)
    sd = inp["L0_ffn_sd"].reshape(2, 6, 128, 2048)
    g["ffn_wd"] = asc(np.concatenate([wd, sd], axis=0))
    g["ffn_gatew"] = asc(inp["L0_ffn_gate_w"].reshape(16, 128, 32).transpose(1, 0, 2).reshape(128, 512))

    ipw = inp["L0_attn_b_in_proj_w"]
    g["m_win_gate"] = asc(ipw[0:4096].T.reshape(16, 128, 32, 128).transpose(2, 0, 1, 3))
    g["m_win_hbc"] = asc(ipw[4096:8448].T.reshape(16, 128, 34, 128).transpose(2, 0, 1, 3))
    g["m_win_dt"] = asc(ipw[8448:8480].T.reshape(16, 128, 32).transpose(1, 0, 2).reshape(128, 512))
    g["m_convw"] = asc(inp["L0_attn_b_conv_w"].reshape(34, 128, 4).transpose(1, 0, 2).reshape(128, 136))
    g["m_convb"] = asc(inp["L0_attn_b_conv_b"].reshape(34, 128).T)
    g["m_alog"] = asc(inp["L0_attn_b_A_log"].reshape(1, 32))
    g["m_dtb"] = asc(inp["L0_attn_b_dt_bias"].reshape(1, 32))
    g["m_d"] = asc(inp["L0_attn_b_D"].reshape(1, 32))
    g["m_normw"] = asc(inp["L0_attn_b_norm_w"].reshape(1, 4096))
    g["m_wout"] = asc(inp["L0_attn_b_out_proj_w"].T.reshape(32, 128, 2048))

    s = np.arange(128)[:, None]
    tt = np.arange(128)[None, :]
    g["tri"] = (s <= tt).astype(f32)
    g["ones1"] = np.ones((1, 128), dtype=f32)
    g["ident32"] = np.eye(128, dtype=f32)
    g["ident16"] = np.eye(128, dtype=f32).astype(ml_dtypes.bfloat16)
    return g


def percore_inputs(inp, g, T):
    x = np.asarray(inp["L0_attn_a_x"], dtype=np.float32)
    maps = []
    for c in range(NCORES):
        b, half = c // 2, c % 2
        m = dict(g)
        m["x_in"] = np.ascontiguousarray(x[b, half * T:(half + 1) * T])
        m["odd_col"] = np.full((128, 1), float(half), np.float32)
        maps.append(m)
    return maps


def declare_io(nc, cfg):
    T = cfg.T
    dr = NS()
    di = lambda n, s, dt=F32: nc.dram_tensor(n, list(s), dt, kind="ExternalInput").ap()
    dint = lambda n, s, dt=F32: nc.dram_tensor(n, list(s), dt).ap()
    dr.x_in = di("x_in", (T, 2048))
    dr.odd_col = di("odd_col", (128, 1))
    dr.tri = di("tri", (128, 128))
    dr.ident32 = di("ident32", (128, 128))
    dr.ident16 = di("ident16", (128, 128), BF16)
    dr.ones1 = di("ones1", (1, 128))
    dr.gla_wq = di("gla_wq", (8, 16, 128, 128))
    dr.gla_wk = di("gla_wk", (8, 16, 128, 128))
    dr.gla_wv = di("gla_wv", (16, 16, 128, 128))
    dr.gla_wg = di("gla_wg", (16, 16, 128, 128))
    dr.gla_wkg1 = di("gla_wkg1", (128, 256))
    dr.gla_wkg2 = di("gla_wkg2", (16, 1024))
    dr.gla_wo = di("gla_wo", (16, 128, 2048))
    dr.gla_bkg2 = di("gla_bkg2", (1, 1024))
    dr.gla_bg = di("gla_bg", (128, 16))
    dr.ffn_wg = di("ffn_wg", (34, 6, 16, 128, 128))
    dr.ffn_wu = di("ffn_wu", (34, 6, 16, 128, 128))
    dr.ffn_wd = di("ffn_wd", (34, 6, 128, 2048))
    dr.ffn_gatew = di("ffn_gatew", (128, 512))
    dr.m_win_gate = di("m_win_gate", (32, 16, 128, 128))
    dr.m_win_hbc = di("m_win_hbc", (34, 16, 128, 128))
    dr.m_win_dt = di("m_win_dt", (128, 512))
    dr.m_convw = di("m_convw", (128, 136))
    dr.m_convb = di("m_convb", (128, 34))
    dr.m_alog = di("m_alog", (1, 32))
    dr.m_dtb = di("m_dtb", (1, 32))
    dr.m_d = di("m_d", (1, 32))
    dr.m_normw = di("m_normw", (1, 4096))
    dr.m_wout = di("m_wout", (32, 128, 2048))
    dr.out = nc.dram_tensor("out", [T, 2048], F32, kind="ExternalOutput").ap()
    dr.x_hbm = dint("x_hbm", (T, 2048))
    dr.o_hbm = dint("o_hbm", (T, 2048), BF16)
    dr.sg_hbm = dint("sg_hbm", (2048, T), BF16)
    dr.hbc_hbm = dint("hbc_hbm", (34, 128, 3 + T), BF16)
    dr.sgate_hbm = dint("sgate_hbm", (4096, T), BF16)
    dr.y_hbm = dint("y_hbm", (T, 4096), BF16)
    dr.xs_hbm = dint("xs_hbm", (4096, 256), BF16)
    dr.acr_hbm = dint("acr_hbm", (1, 4096))
    dr.cb1_in = dint("cb1_in", (1024, 128))
    dr.cb1_out = dint("cb1_out", (2048, 128))
    dr.cb2_in = dint("cb2_in", (34 * 128, 3))
    dr.cb2_out = dint("cb2_out", (2 * 34 * 128, 3))
    dr.cb3_in = dint("cb3_in", (4096, 128))
    dr.cb3_out = dint("cb3_out", (8192, 128))
    return dr


def alloc_persistent(nc, es, cfg):
    t = NS()
    sb = lambda n, s, dt=F32: es.enter_context(nc.sbuf_tensor(n, list(s), dt))
    t.tri = sb("c_tri", (128, 128))
    t.id32 = sb("c_id32", (128, 128))
    t.id16 = sb("c_id16", (128, 128), BF16)
    t.odd = sb("c_odd", (128, 1))
    t.bkg2 = sb("c_bkg2", (128, 1024))
    t.bg = sb("c_bg", (128, 16))
    t.apos = sb("c_apos", (128, 32))
    t.dpanel = sb("c_dpanel", (128, 32))
    t.dtbias = sb("c_dtbias", (128, 32))
    t.normw = sb("c_normw", (128, 4096), BF16)
    t.convw = sb("c_convw", (128, 136))
    t.convb = sb("c_convb", (128, 34))
    t.gatew = sb("c_gatew", (128, 512))
    t.ones1 = sb("c_ones1", (1, 128))
    n_tt = cfg.T // 128
    t.combine = sb("c_combine", (128, n_tt, 34))
    t.scores = sb("c_scores", (128, n_tt, 32))
    t.dt_tm = sb("c_dt", (128, n_tt, 32))
    t.acum_tm = sb("c_acum", (128, n_tt, 32))
    t.sdecabs = sb("c_sdecabs", (128, n_tt, 32))
    t.c_keep = sb("c_ckeep", (128, cfg.T), BF16)
    t.acrow = sb("c_acrow", (1, 4096))
    t.m32 = sb("c_m32", (128, 12, 32))
    t.prev8 = sb("c_prev8", (128, 8))
    t.epscol = sb("c_epscol", (128, 1))
    t.lnscol = sb("c_lnscol", (128, 1))
    return t


def emit_setup(p, cfg):
    t, d = p.t, p.t.dram
    loads = [
        (lambda: (t.tri[:], d.tri)), (lambda: (t.id32[:], d.ident32)),
        (lambda: (t.id16[:], d.ident16)), (lambda: (t.odd[:], d.odd_col)),
        (lambda: (t.gatew[:], d.ffn_gatew)), (lambda: (t.convw[:], d.m_convw)),
        (lambda: (t.convb[:], d.m_convb)), (lambda: (t.bg[:], d.gla_bg)),
        (lambda: (t.ones1[:], d.ones1)),
    ]
    for lf in loads:
        p.emit("sync", lambda e, lf=lf: e.dma_start(out=lf()[0], in_=lf()[1]), sig="yb", inc=16)
    bcasts = [
        (lambda: t.bkg2[:], lambda: d.gla_bkg2, 1024),
        (lambda: t.apos[:], lambda: d.m_alog, 32),
        (lambda: t.dpanel[:], lambda: d.m_d, 32),
        (lambda: t.dtbias[:], lambda: d.m_dtb, 32),
        (lambda: t.normw[:], lambda: d.m_normw, 4096),
    ]
    for dstf, srcf, n in bcasts:
        p.emit("gpsimd", lambda e, dstf=dstf, srcf=srcf, n=n:
               e.dma_start(out=dstf(), in_=srcf().broadcast_to((128, n))), sig="gb", inc=16)
    p.wait("gpsimd", "gb")
    p.wait("sync", "yb")
    p.barrier()
    p.emit("scalar", lambda e: e.activation(out=t.apos[:], in_=t.apos[:], func=ACT.Exp), sig="se")
    p.emit("vector", lambda e: e.memset(t.epscol[:], EPS))
    p.emit("vector", lambda e: e.memset(t.lnscol[:], LNSCALE))
    p.emit("vector", lambda e: e.memset(t.combine[:], 0.0))
    p.drain("vector")
    p.emit("vector", lambda e: e.memset(t.combine[:, :, 32:34], 1.0), sig="ve")
    p.barrier()


def alloc_ln(nc, es, tag, with_scores, with_xt, tr_banks=None):
    sb = lambda n, s, dt=F32: es.enter_context(nc.sbuf_tensor(tag + n, list(s), dt))
    ps = lambda n, s, dt=F32: es.enter_context(nc.psum_tensor(tag + n, list(s), dt))
    ph = NS()
    ph.with_scores = with_scores
    if with_xt:
        ph.x_t = [sb("_xt0", (128, 2048)), sb("_xt1", (128, 2048))]
    else:
        ph.x_t = None
    ph.xln = [sb("_xln0", (128, 2048))]
    ph.t32 = [sb("_t320", (128, 128)), sb("_t321", (128, 128))]
    ph.bns = sb("_bns", (128, 4, 6))
    ph.mv = sb("_mv", (128, 2))
    ph.sd = sb("_sd", (128, 1))
    ph.rs = sb("_rs", (128, 1))
    ph.tr = tr_banks if tr_banks is not None else [ps("_ptr0", (128, 512)), ps("_ptr1", (128, 512))]
    if with_scores:
        ph.sc = [ps("_psc0", (128, 512)), ps("_psc1", (128, 512))]
    return ph


def ln_state(p, ph, xslot_sems=("y0", "y1")):
    st = NS()
    st.b_tr = [bank_new(ph.tr[0]), bank_new(ph.tr[1])]
    if ph.with_scores:
        st.b_sc = [bank_new(ph.sc[0]), bank_new(ph.sc[1])]
    st.xslots = [Slot(p, xslot_sems[0]), Slot(p, xslot_sems[1])]
    st.xln_free_te = [0, 0]
    st.t32_free_se = [0, 0]
    st.t32_free_te = [0, 0]
    return st


def emit_ln_transpose(p, ph, st, x_hbm_src, x_sb_tiles, r0, n_tiles, xlnT, col_off):
    t = p.t
    scores = ph.with_scores
    for i in range(n_tiles):
        r = r0 + i
        if x_sb_tiles is None:
            xbuf = ph.x_t[i % 2]
            sl = st.xslots[i % 2]
            sl.dma("sync", lambda e, xbuf=xbuf, r=r: e.dma_start(
                out=xbuf[:], in_=x_hbm_src[r * 128:(r + 1) * 128, :]))
            sl.wait_data("vector")
            getx = lambda xbuf=xbuf: xbuf[:]
        else:
            getx = lambda r=r: x_sb_tiles(r)
        p.drain("vector")
        for jj in range(4):
            p.emit("vector", lambda e, getx=getx, jj=jj: e.bn_stats(
                out=ph.bns[:, jj, 0:6], in_=getx()[:, jj * 512:(jj + 1) * 512]))
        p.drain("vector")
        p.emit("vector", lambda e: e.bn_aggr(out=ph.mv[:, 0:2], in_=ph.bns[:, 0:4, 0:6]), sig="ve")
        p.wait("scalar", "ve")
        p.emit("scalar", lambda e: e.activation(out=ph.sd[:, 0:1], in_=ph.mv[:, 1:2],
                                                func=ACT.Sqrt, bias=t.epscol[:, 0:1]), sig="se")
        p.wait("vector", "se")
        p.emit("vector", lambda e: e.reciprocal(out=ph.rs[:, 0:1], in_=ph.sd[:, 0:1]))
        p.drain("vector")
        xl = ph.xln[0]
        p.wait("vector", "te", st.xln_free_te[0])
        p.emit("vector", lambda e, getx=getx, xl=xl: e.tensor_scalar(
            out=xl[:], in0=getx(), scalar1=ph.mv[:, 0:1], scalar2=ph.rs[:, 0:1],
            op0=ALU.subtract, op1=ALU.mult), sig="ve")
        if x_sb_tiles is None:
            st.xslots[i % 2].set_free("ve")
        xln_done = p.snap("ve")

        for dch in range(16):
            bank = st.b_tr[dch % 2]
            p.wait("tensor", bank["rd"][0], bank["rd"][1])
            p.wait("tensor", "ve", xln_done)
            p.emit("tensor", lambda e, xl=xl, dch=dch, bank=bank: e.transpose(
                bank["t"][:, 0:128], xl[:, dch * 128:(dch + 1) * 128], t.id32[:]), sig="te")
            bank["wr"] = ("te", p.snap("te"))
            s32 = ph.t32[dch % 2]
            p.wait("vector", "se", st.t32_free_se[dch % 2])
            p.wait("vector", "te", st.t32_free_te[dch % 2])
            bank_read(p, "vector", bank,
                      lambda e, s32=s32, bank=bank: e.tensor_copy(s32[:], bank["t"][:, 0:128]),
                      "ve")
            ev = p.snap("ve")
            p.wait("scalar", "ve", ev)
            p.emit("scalar", lambda e, s32=s32, dch=dch, i=i: e.copy(
                out=xlnT[:, dch, col_off + i * 128: col_off + (i + 1) * 128], in_=s32[:]),
                sig="se")
            st.t32_free_se[dch % 2] = p.snap("se")
            if scores:
                scb = st.b_sc[r % 2]
                p.wait("tensor", scb["rd"][0], scb["rd"][1])
                p.wait("tensor", "ve", ev)
                p.emit("tensor", lambda e, s32=s32, dch=dch, scb=scb: e.matmul(
                    scb["t"][:, 0:32], s32[:], t.gatew[:, dch * 32:(dch + 1) * 32],
                    start=(dch == 0), stop=(dch == 15)),
                    sig=("te" if dch == 15 else None))
                if dch == 15:
                    scb["wr"] = ("te", p.snap("te"))
                st.t32_free_te[dch % 2] = p.snap("te")
        st.xln_free_te[0] = p.snap("te")
        if scores:
            scb = st.b_sc[r % 2]
            bank_read(p, "scalar", scb,
                      lambda e, scb=scb, r=r: e.activation(
                          out=t.scores[:, r, :], in_=scb["t"][:, 0:32], func=ACT.Sigmoid),
                      "se")


def alloc_ffn(nc, es, cfg):
    T = cfg.T
    n_tt = T // 128
    sb = lambda n, s, dt=F32: es.enter_context(nc.sbuf_tensor(n, list(s), dt))
    f = NS()
    f.x = sb("f_x", (128, n_tt, 2048))
    f.xlnT = sb("f_xlnT", (128, 16, T), BF16)
    f.act = sb("f_act", (128, 6, T), BF16)
    f.sg = [sb("f_sg0", (128, 512), BF16), sb("f_sg1", (128, 512), BF16)]
    f.wg = [sb("f_wg%d" % i, (128, 2048), BF16) for i in range(2)]
    f.wu = [sb("f_wu%d" % i, (128, 2048), BF16) for i in range(2)]
    f.wd = [sb("f_wd%d" % i, (128, 2048), BF16) for i in range(7)]
    f.srt = sb("f_srt", (128, 32))
    f.sel = sb("f_sel", (128, 32))
    f.mcol = sb("f_mcol", (128, 4))
    es_ln = ExitStack()
    f.ln = alloc_ln(nc, es_ln, "fl", with_scores=True, with_xt=False)
    es.enter_context(_CloseOnExit(es_ln))
    ps = lambda n, s, dt=F32: es.enter_context(nc.psum_tensor(n, list(s), dt))
    f.p_g = [ps("fp_g0", (128, 512)), ps("fp_g1", (128, 512))]
    f.p_u = [ps("fp_u0", (128, 512)), ps("fp_u1", (128, 512))]
    return f


class _CloseOnExit:
    def __init__(self, es):
        self.es = es

    def __enter__(self):
        return self

    def __exit__(self, *a):
        self.es.close()
        return False


def emit_ffn(p, cfg, f, x_src, x_dst):
    t, d = p.t, p.t.dram
    T = cfg.T
    n_tt = T // 128
    NB = T // 512

    st = ln_state(p, f.ln)
    b_g = [bank_new(f.p_g[0]), bank_new(f.p_g[1])]
    b_u = [bank_new(f.p_u[0]), bank_new(f.p_u[1])]
    b_d = [bank_new(None), bank_new(None)]

    slots = {"wg": [Slot(p, "g%d" % i) for i in range(2)],
             "wu": [Slot(p, "g%d" % (3 + i)) for i in range(2)],
             "wd": [Slot(p, "g%d" % (6 + i)) for i in range(7)]}
    bufs = {"wg": f.wg, "wu": f.wu, "wd": f.wd}

    def stream(kind, e_i, fc):
        idx = e_i * 6 + fc
        slot = idx % len(bufs[kind])
        buf = bufs[kind][slot]
        if kind == "wd":
            slots[kind][slot].dma("gpsimd", lambda e, buf=buf, e_i=e_i, fc=fc: e.dma_start(
                out=buf[:], in_=d.ffn_wd[e_i, fc]))
        else:
            src = d.ffn_wg if kind == "wg" else d.ffn_wu
            slots[kind][slot].dma("gpsimd", lambda e, buf=buf, src=src, e_i=e_i, fc=fc: e.dma_start(
                out=buf[:].rearrange("p (k m) -> p k m", k=16),
                in_=src[e_i, fc].rearrange("k p m -> p k m")))

    for r in range(n_tt):
        p.emit("sync", lambda e, r=r: e.dma_start(
            out=f.x[:, r, :], in_=x_src[r * 128:(r + 1) * 128, :]), sig="yb", inc=16)
    xld = p.snap("yb")
    p.wait("vector", "yb", xld)
    emit_ln_transpose(p, f.ln, st, None, lambda r: f.x[:, r, :], 0, n_tt, f.xlnT, 0)
    xlnT_done = p.snap("se")

    scores_done = p.snap("se")
    p.wait("vector", "se", scores_done)
    for r in range(n_tt):
        p.drain("vector")
        p.emit("vector", lambda e, r=r: e.tensor_copy(f.srt[:], t.scores[:, r, :]))
        p.drain("vector")
        for it in range(TOP_K):
            p.emit("vector", lambda e: e.reduce_max(out=f.mcol[:, 0:1], in_=f.srt[:], axis=AX.X))
            p.drain("vector")
            if it < TOP_K - 1:
                p.emit("vector", lambda e: e.tensor_scalar(
                    out=f.sel[:], in0=f.srt[:], scalar1=f.mcol[:, 0:1], scalar2=None,
                    op0=ALU.is_ge))
                p.drain("vector")
                p.emit("vector", lambda e: e.scalar_tensor_tensor(
                    out=f.srt[:], in0=f.sel[:], scalar=-1e30, in1=f.srt[:],
                    op0=ALU.mult, op1=ALU.add))
                p.drain("vector")
        p.emit("vector", lambda e, r=r: e.tensor_scalar(
            out=f.sel[:], in0=t.scores[:, r, :], scalar1=f.mcol[:, 0:1], scalar2=None,
            op0=ALU.is_ge))
        p.drain("vector")
        p.emit("vector", lambda e, r=r: e.tensor_tensor(
            out=f.srt[:], in0=f.sel[:], in1=t.scores[:, r, :], op=ALU.mult))
        p.drain("vector")
        p.emit("vector", lambda e: e.tensor_reduce(
            out=f.mcol[:, 2:3], in_=f.srt[:], axis=AX.X, op=ALU.add))
        p.drain("vector")
        p.emit("vector", lambda e: e.tensor_scalar(
            out=f.mcol[:, 2:3], in0=f.mcol[:, 2:3], scalar1=1e-20, scalar2=None, op0=ALU.add))
        p.drain("vector")
        p.emit("vector", lambda e: e.reciprocal(out=f.mcol[:, 3:4], in_=f.mcol[:, 2:3]))
        p.drain("vector")
        p.emit("vector", lambda e, r=r: e.tensor_scalar(
            out=t.combine[:, r, 0:32], in0=f.srt[:], scalar1=f.mcol[:, 3:4], scalar2=None,
            op0=ALU.mult), sig="ve")
    comb_done = p.snap("ve")

    for fc in range(2):
        stream("wg", 0, fc)
        stream("wu", 0, fc)
    for fc in range(6):
        stream("wd", 0, fc)

    sg_free = [0, 0]
    act_free = 0
    for ei in range(34):
        for fc in range(6):
            gslot = (ei * 6 + fc) % 2
            for kind, bpair, wbuf in (("wg", b_g, f.wg[gslot]), ("wu", b_u, f.wu[gslot])):
                for nb in range(NB):
                    bank = bpair[(fc * NB + nb) % 2]
                    p.wait("tensor", bank["rd"][0], bank["rd"][1])
                    slots[kind][gslot].wait_data("tensor")
                    p.wait("tensor", "se", xlnT_done)
                    for dch in range(16):
                        p.emit("tensor", lambda e, wbuf=wbuf, dch=dch, nb=nb, bank=bank: e.matmul(
                            bank["t"][:, 0:512],
                            wbuf[:, dch * 128:(dch + 1) * 128],
                            f.xlnT[:, dch, nb * 512:(nb + 1) * 512],
                            start=(dch == 0), stop=(dch == 15)),
                            sig=("te" if dch == 15 else None))
                    bank["wr"] = ("te", p.snap("te"))
            slots["wg"][gslot].set_free("te")
            slots["wu"][gslot].set_free("te")
            nf = ei * 6 + fc + 2
            if nf < 34 * 6:
                stream("wg", nf // 6, nf % 6)
                stream("wu", nf // 6, nf % 6)
            for nb in range(NB):
                gb = b_g[(fc * NB + nb) % 2]
                ub = b_u[(fc * NB + nb) % 2]
                sgb = f.sg[nb % 2]
                p.wait("scalar", gb["wr"][0], gb["wr"][1])
                p.wait("scalar", "ve", sg_free[nb % 2])
                p.emit("scalar", lambda e, gb=gb, sgb=sgb: e.activation(
                    out=sgb[:], in_=gb["t"][:, 0:512], func=ACT.Silu), sig="se")
                gb["rd"] = ("se", p.snap("se"))
                sv = p.snap("se")
                p.wait("vector", "se", sv)
                p.wait("vector", ub["wr"][0], ub["wr"][1])
                p.wait("vector", "te", act_free)
                p.emit("vector", lambda e, ub=ub, sgb=sgb, fc=fc, nb=nb: e.tensor_tensor(
                    out=f.act[:, fc, nb * 512:(nb + 1) * 512],
                    in0=sgb[:], in1=ub["t"][:, 0:512], op=ALU.mult), sig="ve")
                ub["rd"] = ("ve", p.snap("ve"))
                sg_free[nb % 2] = p.snap("ve")
        act_done = p.snap("ve")

        for r in range(n_tt):
            for h2 in range(2):
                bank = b_d[h2]
                p.wait("tensor", bank["rd"][0], bank["rd"][1])
                p.wait("tensor", b_g[h2]["rd"][0], b_g[h2]["rd"][1])
                p.wait("tensor", b_u[h2]["rd"][0], b_u[h2]["rd"][1])
                p.wait("tensor", "ve", act_done)
                for fc in range(6):
                    dslot = (ei * 6 + fc) % 7
                    slots["wd"][dslot].wait_data("tensor")
                    for nn in range(2):
                        tgt = f.p_g[h2] if nn == 0 else f.p_u[h2]
                        p.emit("tensor", lambda e, fc=fc, r=r, h2=h2, nn=nn, dslot=dslot, tgt=tgt: e.matmul(
                            tgt[:, 0:512],
                            f.act[:, fc, r * 128:(r + 1) * 128],
                            f.wd[dslot][:, h2 * 1024 + nn * 512: h2 * 1024 + (nn + 1) * 512],
                            start=(fc == 0), stop=(fc == 5)),
                            sig=("te" if (fc == 5 and nn == 1) else None))
                bank["wr"] = ("te", p.snap("te"))
                p.wait("vector", bank["wr"][0], bank["wr"][1])
                p.wait("vector", "ve", comb_done)
                p.drain("vector")
                for nn in range(2):
                    src_ps = f.p_g[h2] if nn == 0 else f.p_u[h2]
                    p.emit("vector", lambda e, src_ps=src_ps, r=r, h2=h2, nn=nn, ei=ei: e.scalar_tensor_tensor(
                        out=f.x[:, r, h2 * 1024 + nn * 512: h2 * 1024 + (nn + 1) * 512],
                        in0=src_ps[:, 0:512],
                        scalar=t.combine[:, r, ei:ei + 1],
                        in1=f.x[:, r, h2 * 1024 + nn * 512: h2 * 1024 + (nn + 1) * 512],
                        op0=ALU.mult, op1=ALU.add), sig="ve")
                bank["rd"] = ("ve", p.snap("ve"))
                b_g[h2]["rd"] = ("ve", p.snap("ve"))
                b_u[h2]["rd"] = ("ve", p.snap("ve"))
        for fc in range(6):
            slots["wd"][(ei * 6 + fc) % 7].set_free("te")
        act_free = p.snap("te")
        if ei + 1 < 34:
            for fc in range(6):
                stream("wd", ei + 1, fc)

    xdone = p.snap("ve")
    p.wait("sync", "ve", xdone)
    for r in range(n_tt):
        p.emit("sync", lambda e, r=r: e.dma_start(
            out=x_dst[r * 128:(r + 1) * 128, :], in_=f.x[:, r, :]), sig="yb", inc=16)
    p.wait("sync", "yb")
    p.barrier()


def build_program(cfg):
    nc = bass.Bass("TRN2", target_bir_lowering=False, debug=False)
    dram = declare_io(nc, cfg)
    es_pers = ExitStack()
    pers = alloc_persistent(nc, es_pers, cfg)
    pers.dram = dram

    al = {}
    if "gla" in cfg.phases:
        es_ph = ExitStack()
        al["gla"] = alloc_gla(nc, es_ph, cfg)
        es_ph.close()
    if "ffn1" in cfg.phases or "ffn2" in cfg.phases:
        es_ph = ExitStack()
        al["ffn"] = alloc_ffn(nc, es_ph, cfg)
        es_ph.close()
    if "mamba" in cfg.phases:
        es_ph = ExitStack()
        al["mamba"] = alloc_mamba(nc, es_ph, cfg)
        es_ph.close()

    sems = {}
    sem_es = ExitStack()
    names = ["te", "ve", "se", "gp", "cc", "bar", "gb", "yb"]
    names += ["g%d" % i for i in range(16)]
    names += ["y%d" % i for i in range(18)]
    for name in names:
        sems[name] = sem_es.enter_context(nc.semaphore("s_" + name))

    block_es = ExitStack()
    block = block_es.enter_context(nc.Block())

    def make_section(eng_name):
        def section(e):
            pp = Pass(nc, eng_name, e, sems, pers, al, cfg)
            emit_all(pp, cfg)
        return section

    for eng_name in ENGINES:
        getattr(block, eng_name)(make_section(eng_name))
    block_es.close()
    sem_es.close()
    es_pers.close()
    return nc


def emit_all(p, cfg):
    d = p.t.dram
    emit_setup(p, cfg)
    x_cur = d.x_in
    if "gla" in cfg.phases:
        emit_gla(p, cfg, p.al["gla"], x_cur, d.x_hbm)
        x_cur = d.x_hbm
    if "ffn1" in cfg.phases:
        dst = d.out if cfg.phases[-1] == "ffn1" else d.x_hbm
        emit_ffn(p, cfg, p.al["ffn"], x_cur, dst)
        x_cur = dst
    if "mamba" in cfg.phases:
        emit_mamba(p, cfg, p.al["mamba"], x_cur, d.x_hbm)
        x_cur = d.x_hbm
    if "ffn2" in cfg.phases:
        emit_ffn(p, cfg, p.al["ffn"], x_cur, d.out)
        x_cur = d.out
    if x_cur is not d.out:
        for r in range(cfg.T // 128):
            p.emit("sync", lambda e, r=r, x_cur=x_cur: e.dma_start(
                out=d.out[r * 128:(r + 1) * 128, :], in_=x_cur[r * 128:(r + 1) * 128, :]),
                sig="yb", inc=16)
    p.wait("sync", "yb")
    p.wait("gpsimd", "gb")
    for i in range(16):
        p.wait("gpsimd", "g%d" % i)
    for i in range(18):
        p.wait("sync", "y%d" % i)
    p.wait("gpsimd", "cc")
    p.barrier()


def alloc_gla(nc, es, cfg):
    T = cfg.T
    HT = min(4, T // 128)
    sb = lambda n, s, dt=F32: es.enter_context(nc.sbuf_tensor(n, list(s), dt))
    ps = lambda n, s, dt=F32: es.enter_context(nc.psum_tensor(n, list(s), dt))
    g = NS()
    g.HT = HT
    g.xlnT = sb("g_xlnT", (128, 16, HT * 128), BF16)
    g.q = sb("g_q", (128, 8, T), BF16)
    g.kf = sb("g_kf", (128, 8, HT * 128), BF16)
    g.v = sb("g_v", (128, HT, 2048), BF16)
    g.kg = sb("g_kg", (128, HT, 1024), BF16)
    es_a = ExitStack()
    g.vf = es_a.enter_context(nc.sbuf_tensor("g_vf", [128, 16, HT * 128], BF16))
    es_a.close()
    g.lsig = sb("g_lsig", (128, 1024))
    g.gab = sb("g_gab", (128, 8, 128))
    g.ep = sb("g_ep", (128, 8, 128))
    g.en = sb("g_en", (128, 8, 128))
    g.ktm = sb("g_ktm", (128, 8, 128), BF16)
    g.u32 = sb("g_u32", (128, 8, 128))
    g.u16 = sb("g_u16", (128, 8, 128), BF16)
    g.o = sb("g_o", (128, HT, 2048), BF16)
    g.sgf = sb("g_sgf", (128, 16, HT * 128), BF16)
    g.sr16 = sb("g_sr16", (128, 8, 128), BF16)
    g.zf = sb("g_zf", (16, HT * 128), BF16)
    g.at = [sb("g_at0", (128, 128), BF16), sb("g_at1", (128, 128), BF16)]
    g.ont = [sb("g_ont0", (128, 128), BF16), sb("g_ont1", (128, 128), BF16)]
    g.ecol = sb("g_ecol", (128, 8))
    g.wqk = [sb("g_wqk0", (128, 2048), BF16), sb("g_wqk1", (128, 2048), BF16), sb("g_wqk2", (128, 2048), BF16)]
    g.wkg1 = sb("g_wkg1", (128, 256), BF16)
    g.wkg2 = sb("g_wkg2", (16, 1024), BF16)
    g.pP = [ps("gp_P0", (128, 512)), ps("gp_P1", (128, 512))]
    es_ln = ExitStack()
    g.ln = alloc_ln(nc, es_ln, "gl", with_scores=False, with_xt=True, tr_banks=g.pP)
    es_ln.close()
    g.bns16 = sb("g_bns16", (128, 16, 6))
    g.mv16 = sb("g_mv16", (128, 16, 2))
    g.sd16 = sb("g_sd16", (128, 16))
    g.rs16 = sb("g_rs16", (128, 16))
    g.gnt = sb("g_gnt", (128, 128), BF16)
    g.xst = [sb("g_xst0", (128, 512)), sb("g_xst1", (128, 512))]
    g.ost = [sb("g_ost0", (128, 512)), sb("g_ost1", (128, 512))]
    g.wos = [sb("g_wos0", (128, 512), BF16), sb("g_wos1", (128, 512), BF16)]
    g.pE = ps("gp_E", (128, 512))
    g.pF = ps("gp_F", (128, 512))
    g.pG = ps("gp_G", (128, 512))
    g.pH = ps("gp_H", (128, 512))
    g.pT = [ps("gp_T0", (128, 512), BF16), ps("gp_T1", (128, 512), BF16)]
    return g


def emit_gla(p, cfg, g, x_src, x_dst):
    t, d = p.t, p.t.dram
    T = cfg.T
    HT = g.HT
    HTOK = HT * 128
    n_half = T // HTOK
    st = ln_state(p, g.ln)
    bE = bank_new(g.pE)
    bF = bank_new(g.pF)
    bG = bank_new(g.pG)
    bH = bank_new(g.pH)
    bT = [bank_new(g.pT[0]), bank_new(g.pT[1])]
    bP = [bank_new(g.ln.tr[0]), bank_new(g.ln.tr[1])]
    s_wqk = [Slot(p, "g0"), Slot(p, "g1"), Slot(p, "g10")]
    s_one = Slot(p, "g2")
    s_cb = Slot(p, "g5")
    s_ost = Slot(p, "y2")
    s_sgst = Slot(p, "y3")
    s_xrt = [Slot(p, "y4"), Slot(p, "y5")]
    s_orl = Slot(p, "y6")
    s_sgrl = Slot(p, "y7")
    s_xout = [Slot(p, "y8"), Slot(p, "y9")]

    p.emit("vector", lambda e: e.memset(g.u32[:], 0.0))
    p.emit("vector", lambda e: e.memset(g.u16[:], 0.0))
    p.emit("vector", lambda e: e.memset(t.prev8[:], 0.0), sig="ve")
    uz = p.snap("ve")
    p.wait("tensor", "ve", uz)
    p.wait("scalar", "ve", uz)
    s_one.dma("gpsimd", lambda e: e.dma_start(out=g.wkg1[:], in_=d.gla_wkg1))
    s_one.dma("gpsimd", lambda e: e.dma_start(out=g.wkg2[:], in_=d.gla_wkg2))
    wsmall = s_one
    s_wo = [Slot(p, "g3"), Slot(p, "g4")]

    o_spill_done = [0, 0]

    for H in range(n_half):
        p.wait("scalar", "te", p.snap("te"))
        p.wait("vector", "te", p.snap("te"))
        p.wait("vector", "se", p.snap("se"))
        emit_ln_transpose(p, g.ln, st, x_src, None, H * HT, HT, g.xlnT, 0)
        xlnT_done = p.snap("se")

        def proj_fm(wsrc, n_m, dst_fn, post):
            for m in range(n_m):
                sl = s_wqk[m % 3]
                sl.dma("gpsimd", lambda e, wsrc=wsrc, m=m, sl=sl: e.dma_start(
                    out=g.wqk[s_wqk.index(sl)][:].rearrange("p (k m) -> p k m", k=16),
                    in_=wsrc[m].rearrange("k p m -> p k m")))
                bank = bP[m % 2]
                p.wait("tensor", bank["rd"][0], bank["rd"][1])
                sl.wait_data("tensor")
                p.wait("tensor", "se", xlnT_done)
                wbuf = g.wqk[m % 3]
                for dch in range(16):
                    for nb in range(HTOK // 512 if HTOK >= 512 else 1):
                        w = min(512, HTOK)
                        p.emit("tensor", lambda e, wbuf=wbuf, dch=dch, nb=nb, w=w, bank=bank: e.matmul(
                            bank["t"][:, 0:w],
                            wbuf[:, dch * 128:(dch + 1) * 128],
                            g.xlnT[:, dch, nb * w:(nb + 1) * w],
                            start=(dch == 0), stop=(dch == 15)),
                            sig=("te" if (dch == 15 and nb == (HTOK - 1) // 512) else None))
                bank["wr"] = ("te", p.snap("te"))
                sl.set_free("te")
                post(m, bank)

        def post_q(m, bank):
            bank_read(p, "vector", bank,
                      lambda e, m=m, bank=bank: e.tensor_copy(
                          g.q[:, m, H * HTOK:(H + 1) * HTOK], bank["t"][:, 0:HTOK]), "ve")

        def post_k(m, bank):
            bank_read(p, "vector", bank,
                      lambda e, m=m, bank=bank: e.tensor_copy(
                          g.kf[:, m, 0:HTOK], bank["t"][:, 0:HTOK]), "ve")

        def post_v(m, bank):
            bank_read(p, "vector", bank,
                      lambda e, m=m, bank=bank: e.tensor_copy(
                          g.vf[:, m, 0:HTOK], bank["t"][:, 0:HTOK]), "ve")

        def post_sg(m, bank):
            p.wait("scalar", "y3", s_sgst.ld)
            bank_read(p, "scalar", bank,
                      lambda e, m=m, bank=bank: e.activation(
                          out=g.sgf[:, m, 0:HTOK], in_=bank["t"][:, 0:HTOK],
                          func=ACT.Silu, bias=t.bg[:, m:m + 1]), "se")

        proj_fm(d.gla_wq, 8, None, post_q)
        proj_fm(d.gla_wk, 8, None, post_k)
        proj_fm(d.gla_wv, 16, None, post_v)
        proj_fm(d.gla_wg, 16, None, post_sg)
        sg_done = p.snap("se")

        p.wait("sync", "se", sg_done)
        s_sgst.dma("sync", lambda e, H=H: e.dma_start(
            out=d.sg_hbm[:, H * HTOK:(H + 1) * HTOK].rearrange("(a p) t -> p a t", p=128),
            in_=g.sgf[:]))

        vtd = p.snap("ve")
        for vm in range(16):
            for r4 in range(HT):
                bank = bT[(vm * HT + r4) % 2]
                p.wait("tensor", bank["rd"][0], bank["rd"][1])
                p.wait("tensor", "ve", vtd)
                p.emit("tensor", lambda e, vm=vm, r4=r4, bank=bank: e.transpose(
                    bank["t"][:, 0:128], g.vf[:, vm, r4 * 128:(r4 + 1) * 128], t.id16[:]),
                    sig="te")
                bank["wr"] = ("te", p.snap("te"))
                bank_read(p, "vector", bank,
                          lambda e, vm=vm, r4=r4, bank=bank: e.tensor_copy(
                              g.v[:, r4, vm * 128:(vm + 1) * 128], bank["t"][:, 0:128]), "ve")
        v_done = p.snap("ve")

        p.wait("tensor", bH["rd"][0], bH["rd"][1])
        wsmall.wait_data("tensor")
        p.wait("tensor", "se", xlnT_done)
        for dch in range(16):
            p.emit("tensor", lambda e, dch=dch: e.matmul(
                g.pH[0:16, 0:HTOK], g.wkg1[:, dch * 16:(dch + 1) * 16],
                g.xlnT[:, dch, 0:HTOK],
                start=(dch == 0), stop=(dch == 15)), sig=("te" if dch == 15 else None))
        bH["wr"] = ("te", p.snap("te"))
        bank_read(p, "vector", bH,
                  lambda e: e.tensor_copy(g.zf[:, 0:HTOK], g.pH[0:16, 0:HTOK]), "ve")
        z_done = p.snap("ve")
        bkg_banks = [bE, bF]
        for r4 in range(HT):
            for nch in range(2):
                bank = bkg_banks[nch]
                p.wait("tensor", bank["rd"][0], bank["rd"][1])
                p.wait("tensor", "ve", z_done)
                p.emit("tensor", lambda e, r4=r4, nch=nch, bank=bank: e.matmul(
                    bank["t"][:, 0:512], g.zf[:, r4 * 128:(r4 + 1) * 128],
                    g.wkg2[:, nch * 512:(nch + 1) * 512],
                    start=True, stop=True), sig="te")
                bank["wr"] = ("te", p.snap("te"))
                bank_read(p, "vector", bank,
                          lambda e, r4=r4, nch=nch, bank=bank: e.tensor_tensor(
                              out=g.kg[:, r4, nch * 512:(nch + 1) * 512],
                              in0=bank["t"][:, 0:512],
                              in1=t.bkg2[:, nch * 512:(nch + 1) * 512], op=ALU.add), "ve")
        kg_done = p.snap("ve")

        for c in range(HT):
            gch = H * HT + c
            p.drain("vector")
            p.wait("vector", "se", p.snap("se"))
            p.wait("scalar", "ve", kg_done)
            p.wait("scalar", "te", p.snap("te"))
            p.emit("scalar", lambda e, c=c: e.activation(
                out=g.lsig[:], in_=g.kg[:, c, :], func=ACT.Sigmoid), sig="se")
            p.drain("scalar")
            p.emit("scalar", lambda e: e.activation(
                out=g.lsig[:], in_=g.lsig[:], func=ACT.Ln), sig="se")
            lsig_done = p.snap("se")
            tribk = [bH, bE]
            for m in range(8):
                bk = tribk[m % 2]
                p.wait("tensor", bk["rd"][0], bk["rd"][1])
                p.wait("tensor", "se", lsig_done)
                p.emit("tensor", lambda e, m=m, bk=bk: e.matmul(
                    bk["t"][:, 0:128], g.lsig[:, m * 128:(m + 1) * 128], t.tri[:],
                    start=True, stop=True), sig="te")
                bk["wr"] = ("te", p.snap("te"))
                bank_read(p, "vector", bk,
                          lambda e, m=m, bk=bk: e.tensor_scalar(
                              out=g.gab[:, m, :], in0=bk["t"][:, 0:128],
                              scalar1=t.prev8[:, m:m + 1], scalar2=None, op0=ALU.add), "ve")
            gab_done = p.snap("ve")
            p.wait("scalar", "ve", gab_done)
            p.emit("scalar", lambda e: e.copy(out=t.prev8[:], in_=g.gab[:, :, 127:128]), sig="se")
            p.emit("scalar", lambda e: e.activation(
                out=g.ep[:].rearrange("p a b -> p (a b)"),
                in_=g.gab[:].rearrange("p a b -> p (a b)"), func=ACT.Exp, scale=G16), sig="se")
            p.emit("scalar", lambda e: e.activation(
                out=g.en[:].rearrange("p a b -> p (a b)"),
                in_=g.gab[:].rearrange("p a b -> p (a b)"), func=ACT.Exp, scale=-G16,
                bias=t.lnscol[:, 0:1]), sig="se")
            exps_done = p.snap("se")
            p.wait("vector", "se", exps_done)
            p.emit("vector", lambda e, gch=gch: e.tensor_tensor(
                out=g.q[:, :, gch * 128:(gch + 1) * 128], in0=g.q[:, :, gch * 128:(gch + 1) * 128],
                in1=g.ep[:], op=ALU.mult), sig="ve")
            p.emit("vector", lambda e, c=c: e.tensor_tensor(
                out=g.kf[:, :, c * 128:(c + 1) * 128], in0=g.kf[:, :, c * 128:(c + 1) * 128],
                in1=g.en[:], op=ALU.mult), sig="ve")
            qk_done = p.snap("ve")
            for m in range(8):
                bank = bT[m % 2]
                p.wait("tensor", bank["rd"][0], bank["rd"][1])
                p.wait("tensor", "ve", qk_done)
                p.emit("tensor", lambda e, m=m, c=c, bank=bank: e.transpose(
                    bank["t"][:, 0:128], g.kf[:, m, c * 128:(c + 1) * 128], t.id16[:]), sig="te")
                bank["wr"] = ("te", p.snap("te"))
                bank_read(p, "vector", bank,
                          lambda e, m=m, bank=bank: e.tensor_copy(
                              g.ktm[:, m, :], bank["t"][:, 0:128]), "ve")
            ktm_done = p.snap("ve")

            u16_done = p.snap("se")
            atbk = [bE, bH]
            obk = [bF, bP[0]]
            tbk = [bG, bP[1]]
            for h in range(16):
                b = (h % 2) * 64
                hm = h // 2
                pe_ = atbk[h % 2]
                pf_ = obk[h % 2]
                pg_ = tbk[h % 2]
                p.wait("tensor", pe_["rd"][0], pe_["rd"][1])
                p.wait("tensor", "ve", qk_done)
                p.emit("tensor", lambda e, b=b, hm=hm, c=c, gch=gch, pe_=pe_: e.matmul(
                    pe_["t"][:, 0:128], g.kf[b:b + 64, hm, c * 128:(c + 1) * 128],
                    g.q[b:b + 64, hm, gch * 128:(gch + 1) * 128],
                    start=True, stop=True), sig="te")
                pe_["wr"] = ("te", p.snap("te"))
                atb = g.at[h % 2]
                if not hasattr(st, "at_free"):
                    st.at_free = [0, 0]
                p.wait("vector", "te", st.at_free[h % 2])
                bank_read(p, "vector", pe_,
                          lambda e, atb=atb, pe_=pe_: e.tensor_tensor(
                              out=atb[:], in0=pe_["t"][:, 0:128], in1=t.tri[:], op=ALU.mult), "ve")
                at_done = p.snap("ve")
                p.wait("tensor", pf_["rd"][0], pf_["rd"][1])
                p.wait("tensor", "ve", at_done)
                p.wait("tensor", "ve", v_done)
                p.wait("tensor", "se", u16_done)
                p.emit("tensor", lambda e, atb=atb, c=c, h=h, pf_=pf_: e.matmul(
                    pf_["t"][:, 0:128], atb[:], g.v[:, c, h * 128:(h + 1) * 128],
                    start=True, stop=False))
                p.emit("tensor", lambda e, b=b, hm=hm, gch=gch, pf_=pf_: e.matmul(
                    pf_["t"][:, 0:128], g.q[b:b + 64, hm, gch * 128:(gch + 1) * 128],
                    g.u16[b:b + 64, hm, :], start=False, stop=True), sig="te")
                pf_["wr"] = ("te", p.snap("te"))
                st.at_free[h % 2] = p.snap("te")
                p.wait("scalar", "y2", s_ost.ld)
                bank_read(p, "scalar", pf_,
                          lambda e, c=c, h=h, pf_=pf_: e.copy(
                              out=g.o[:, c, h * 128:(h + 1) * 128], in_=pf_["t"][:, 0:128]), "se")
                p.wait("tensor", pg_["rd"][0], pg_["rd"][1])
                p.wait("tensor", "ve", ktm_done)
                p.emit("tensor", lambda e, b=b, hm=hm, c=c, h=h, pg_=pg_: e.matmul(
                    pg_["t"][b:b + 64, 0:128], g.ktm[:, hm, b:b + 64],
                    g.v[:, c, h * 128:(h + 1) * 128], start=True, stop=True), sig="te")
                pg_["wr"] = ("te", p.snap("te"))
                bank_read(p, "vector", pg_,
                          lambda e, b=b, hm=hm, pg_=pg_: e.tensor_tensor(
                              out=g.u32[b:b + 64, hm, :], in0=g.u32[b:b + 64, hm, :],
                              in1=pg_["t"][b:b + 64, 0:128], op=ALU.add), "ve")
                uadd = p.snap("ve")
                p.wait("scalar", "ve", uadd)
                p.emit("scalar", lambda e, b=b, hm=hm: e.copy(
                    out=g.u16[b:b + 64, hm, :], in_=g.u32[b:b + 64, hm, :]), sig="se")
                u16_done = p.snap("se")

        odone = p.snap("se")
        p.wait("sync", "se", odone)
        p.wait("sync", "ve", p.snap("ve"))
        s_ost.dma("sync", lambda e, H=H: e.dma_start(
            out=d.o_hbm[H * HTOK:(H + 1) * HTOK, :].rearrange("(a p) n -> p a n", p=128),
            in_=g.o[:]))
        o_spill_done[H] = s_ost.ld

    lastg = p.snap("ve")
    p.wait("scalar", "ve", lastg)
    p.emit("scalar", lambda e: e.activation(
        out=g.ecol[:], in_=g.gab[:, :, 127:128], func=ACT.Exp, scale=G16), sig="se")
    ecol_done = p.snap("se")
    p.wait("vector", "se", ecol_done)
    p.drain("vector")
    for hm in range(8):
        p.emit("vector", lambda e, hm=hm: e.tensor_scalar(
            out=g.ep[:, hm, :], in0=g.u32[:, hm, :], scalar1=g.ecol[:, hm:hm + 1],
            scalar2=None, op0=ALU.mult), sig="ve")
    snd_done = p.snap("ve")
    p.wait("gpsimd", "ve", snd_done)
    s_cb.dma("gpsimd", lambda e: e.dma_start(
        out=d.cb1_in.rearrange("(a p) v -> p a v", p=128), in_=g.ep[:]))
    s_cb.wait_data("gpsimd")
    p.emit("gpsimd", lambda e: e.collective_compute(
        "AllGather", ALU.bypass,
        replica_groups=[[0, 1], [2, 3], [4, 5], [6, 7]],
        ins=[d.cb1_in.opt()], outs=[d.cb1_out.opt()]), sig="cc")
    p.wait("gpsimd", "cc")
    s_cb.dma("gpsimd", lambda e: e.dma_start(
        out=g.sr16[:], in_=d.cb1_out[0:1024, :].rearrange("(a p) v -> p a v", p=128)))
    s_cb.wait_data("vector")
    p.drain("vector")
    p.emit("vector", lambda e: e.tensor_scalar(
        out=g.sr16[:].rearrange("p a b -> p (a b)"),
        in0=g.sr16[:].rearrange("p a b -> p (a b)"),
        scalar1=t.odd[:, 0:1], scalar2=None, op0=ALU.mult), sig="ve")
    srm = p.snap("ve")
    p.wait("scalar", "ve", srm)
    p.emit("scalar", lambda e: e.engine_nop() if False else e.nop(), sig="se")
    sr_done = p.snap("se")

    p.barrier()
    s_orl.free = ("y2", s_ost.ld)
    s_sgrl.free = ("y3", s_sgst.ld)
    for H in range(n_half):
        s_orl.dma("sync", lambda e, H=H: e.dma_start(
            out=g.o[:], in_=d.o_hbm[H * HTOK:(H + 1) * HTOK, :].rearrange("(a p) n -> p a n", p=128)))
        s_sgrl.dma("sync", lambda e, H=H: e.dma_start(
            out=g.sgf[:], in_=d.sg_hbm[:, H * HTOK:(H + 1) * HTOK].rearrange("(a p) t -> p a t", p=128)))
        s_orl.wait_data("vector")
        s_sgrl.wait_data("vector")
        orl_done = p.snap("ve")

        for c in range(HT):
            gch = H * HT + c
            crbk = [bE, bH]
            for h in range(16):
                b = (h % 2) * 64
                hm = h // 2
                pe_ = crbk[h % 2]
                p.wait("tensor", pe_["rd"][0], pe_["rd"][1])
                p.wait("tensor", "se", sr_done)
                p.emit("tensor", lambda e, b=b, hm=hm, gch=gch, pe_=pe_: e.matmul(
                    pe_["t"][:, 0:128], g.q[b:b + 64, hm, gch * 128:(gch + 1) * 128],
                    g.sr16[b:b + 64, hm, :], start=True, stop=True), sig="te")
                pe_["wr"] = ("te", p.snap("te"))
                s_orl.wait_data("vector")
                bank_read(p, "vector", pe_,
                          lambda e, c=c, h=h, pe_=pe_: e.tensor_tensor(
                              out=g.o[:, c, h * 128:(h + 1) * 128],
                              in0=pe_["t"][:, 0:128], in1=g.o[:, c, h * 128:(h + 1) * 128],
                              op=ALU.add), "ve")
            corr_done = p.snap("ve")

            p.wait("vector", "ve", corr_done)
            p.drain("vector")
            for h in range(16):
                p.emit("vector", lambda e, c=c, h=h: e.bn_stats(
                    out=g.bns16[:, h, :], in_=g.o[:, c, h * 128:(h + 1) * 128]))
            p.drain("vector")
            for h in range(16):
                p.emit("vector", lambda e, h=h: e.bn_aggr(
                    out=g.mv16[:, h, :], in_=g.bns16[:, h, 0:6]))
            p.drain("vector")
            p.emit("vector", lambda e: e.tensor_copy(g.sd16[:], g.mv16[:, :, 1:2]), sig="ve")
            gn1 = p.snap("ve")
            p.wait("scalar", "ve", gn1)
            p.emit("scalar", lambda e: e.activation(
                out=g.sd16[:], in_=g.sd16[:], func=ACT.Sqrt, bias=t.epscol[:, 0:1]), sig="se")
            gn2 = p.snap("se")
            p.wait("vector", "se", gn2)
            p.emit("vector", lambda e: e.reciprocal(out=g.rs16[:], in_=g.sd16[:]))
            p.drain("vector")
            gnt_free = [0, 0]
            ont_free = [0, 0]
            for h in range(16):
                gntb = g.at[h % 2]
                p.wait("vector", "te", gnt_free[h % 2])
                p.emit("vector", lambda e, c=c, h=h, gntb=gntb: e.tensor_scalar(
                    out=gntb[:], in0=g.o[:, c, h * 128:(h + 1) * 128],
                    scalar1=g.mv16[:, h, 0:1], scalar2=g.rs16[:, h:h + 1],
                    op0=ALU.subtract, op1=ALU.mult), sig="ve")
                onr = p.snap("ve")
                bank = bT[h % 2]
                p.wait("tensor", bank["rd"][0], bank["rd"][1])
                p.wait("tensor", "ve", onr)
                p.emit("tensor", lambda e, bank=bank, gntb=gntb: e.transpose(
                    bank["t"][:, 0:128], gntb[:], t.id16[:]), sig="te")
                bank["wr"] = ("te", p.snap("te"))
                gnt_free[h % 2] = p.snap("te")
                ontb = g.ont[h % 2]
                p.wait("vector", "ve", ont_free[h % 2])
                bank_read(p, "vector", bank,
                          lambda e, ontb=ontb, bank=bank: e.tensor_copy(ontb[:], bank["t"][:, 0:128]), "ve")
                p.drain("vector")
                p.emit("vector", lambda e, ontb=ontb, c=c, h=h: e.tensor_tensor(
                    out=g.sgf[:, h, c * 128:(c + 1) * 128], in0=ontb[:],
                    in1=g.sgf[:, h, c * 128:(c + 1) * 128], op=ALU.mult), sig="ve")
                ont_free[h % 2] = p.snap("ve")
            p_done = p.snap("ve")

        p_done_all = p.snap("ve")
        cbanks = [bank_new(g.pE), bank_new(g.pF), bank_new(g.pG), bank_new(g.pH)]
        cbanks[0]["rd"] = bE["rd"]
        cbanks[1]["rd"] = bF["rd"]
        cbanks[2]["rd"] = bG["rd"]
        cbanks[3]["rd"] = bH["rd"]
        for nch in range(4):
            for dch in range(16):
                wsl = s_wo[(nch * 16 + dch) % 2]
                wsl.dma("gpsimd", lambda e, dch=dch, nch=nch, wsl=wsl: e.dma_start(
                    out=g.wos[s_wo.index(wsl)][:],
                    in_=d.gla_wo[dch, :, nch * 512:(nch + 1) * 512]))
                for c in range(HT):
                    bank = cbanks[c]
                    if dch == 0:
                        p.wait("tensor", bank["rd"][0], bank["rd"][1])
                        p.wait("tensor", "ve", p_done_all)
                    wsl.wait_data("tensor")
                    p.emit("tensor", lambda e, c=c, dch=dch, wsl=wsl, bank=bank: e.matmul(
                        bank["t"][:, 0:512], g.sgf[:, dch, c * 128:(c + 1) * 128],
                        g.wos[s_wo.index(wsl)][:],
                        start=(dch == 0), stop=(dch == 15)),
                        sig=("te" if c == HT - 1 else None))
                if dch == 15:
                    for c in range(HT):
                        cbanks[c]["wr"] = ("te", p.snap("te"))
                wsl.set_free("te")
            for c in range(HT):
                gch = H * HT + c
                xsl = s_xrt[c % 2]
                xsl.dma("sync", lambda e, gch=gch, nch=nch, c=c: e.dma_start(
                    out=g.xst[c % 2][:],
                    in_=x_src[gch * 128:(gch + 1) * 128, nch * 512:(nch + 1) * 512]))
                xsl.wait_data("vector")
                bank = cbanks[c]
                p.wait("vector", bank["wr"][0], bank["wr"][1])
                p.wait("vector", "y8" if c % 2 == 0 else "y9", s_xout[c % 2].ld)
                p.emit("vector", lambda e, c=c, bank=bank: e.tensor_tensor(
                    out=g.ost[c % 2][:], in0=bank["t"][:, 0:512],
                    in1=g.xst[c % 2][:], op=ALU.add), sig="ve")
                bank["rd"] = ("ve", p.snap("ve"))
                xsl.set_free("ve")
                xo_done = p.snap("ve")
                p.wait("sync", "ve", xo_done)
                s_xout[c % 2].dma("sync", lambda e, gch=gch, nch=nch, c=c: e.dma_start(
                    out=x_dst[gch * 128:(gch + 1) * 128, nch * 512:(nch + 1) * 512],
                    in_=g.ost[c % 2][:]))
        bE["rd"] = cbanks[0]["rd"]
        bF["rd"] = cbanks[1]["rd"]
        bG["rd"] = cbanks[2]["rd"]
        bH["rd"] = cbanks[3]["rd"]
        bE["wr"] = cbanks[0]["wr"]
        bF["wr"] = cbanks[1]["wr"]
        bG["wr"] = cbanks[2]["wr"]
        bH["wr"] = cbanks[3]["wr"]
        s_orl.free = ("ve", p.snap("ve"))
        s_sgrl.free = ("te", p.snap("te"))
    p.wait("sync", "y8", s_xout[0].ld)
    p.wait("sync", "y9", s_xout[1].ld)
    p.barrier()


def alloc_mamba(nc, es, cfg):
    T = cfg.T
    n_tt = T // 128
    sb = lambda n, s, dt=F32: es.enter_context(nc.sbuf_tensor(n, list(s), dt))
    ps = lambda n, s, dt=F32: es.enter_context(nc.psum_tensor(n, list(s), dt))
    m = NS()
    m.xlnT = sb("m_xlnT", (128, 16, T), BF16)
    m.S32 = sb("m_S32", (128, 32, 128))
    m.S16 = sb("m_S16", (128, 32, 128), BF16)
    m.strip = [sb("m_st0", (128, 512), BF16), sb("m_st1", (128, 512), BF16)]
    m.gstrip = [sb("m_gs0", (128, 512), BF16), sb("m_gs1", (128, 512), BF16)]
    m.tails = sb("m_tails", (128, 34, 3), BF16)
    m.dts = [sb("m_dts0", (128, 32)), sb("m_dts1", (128, 32)), sb("m_dts2", (128, 32))]
    m.wtile = [sb("m_wt0", (128, 2048), BF16), sb("m_wt1", (128, 2048), BF16)]
    m.dident = sb("m_dident", (128, 32, 128), BF16)
    es_b = ExitStack()
    sbb = lambda n, s, dt=F32: es_b.enter_context(nc.sbuf_tensor(n, list(s), dt))
    m.hbc = sbb("m_hbc", (128, 34, 259), BF16)
    m.xs = sbb("m_xs", (128, 2, 4096), BF16)
    m.yq = sbb("m_yq", (128, 2, 4096), BF16)
    m.acf = sbb("m_acf", (32, 128))
    m.ac32 = sbb("m_ac32", (128, 8, 32))
    m.btm = sbb("m_btm", (128, 128), BF16)
    m.gtri = sbb("m_gtri", (128, 128), BF16)
    m.xdt_all = sbb("m_xdta", (128, 32, 128), BF16)
    m.bdec_all = sbb("m_bdeca", (128, 32, 128), BF16)
    m.csc_all = sbb("m_csca", (128, 32, 128), BF16)
    m.M_all = sbb("m_Mall", (128, 32, 128), BF16)
    m.dsl4 = sbb("m_dsl4", (128, 4, 128))
    m.esl4 = sbb("m_esl4", (128, 4, 128), BF16)
    m.esd4 = sbb("m_esd4", (128, 4, 128), BF16)
    m.cva = sbb("m_cva", (128, 256))
    m.cvb = sbb("m_cvb", (128, 256))
    es_b.close()
    es_t = ExitStack()
    sbt = lambda n, s, dt=F32: es_t.enter_context(nc.sbuf_tensor(n, list(s), dt))
    m.yt = sbt("m_yt", (128, 2, 4096), BF16)
    m.sgt = sbt("m_sgt", (128, 2, 4096), BF16)
    m.ynT = sbt("m_ynT", (128, 32, 256), BF16)
    m.xst = [sbt("m_xst0", (128, 512)), sbt("m_xst1", (128, 512))]
    m.ost = [sbt("m_ost0", (128, 512)), sbt("m_ost1", (128, 512))]
    m.wos = [sbt("m_wos0", (128, 512), BF16), sbt("m_wos1", (128, 512), BF16)]
    m.ssq = sbt("m_ssq", (128, 4))
    es_t.close()
    m.pP = [es.enter_context(nc.psum_tensor("mp_P0", [128, 512], F32)),
            es.enter_context(nc.psum_tensor("mp_P1", [128, 512], F32))]
    es_ln = ExitStack()
    m.ln = alloc_ln(nc, es_ln, "ml", with_scores=False, with_xt=True, tr_banks=m.pP)
    es_ln.close()
    m.pA = ps("mp_A", (128, 512))
    m.pB = ps("mp_B", (128, 512))
    m.pC = ps("mp_C", (128, 512))
    m.pD = ps("mp_D", (128, 512))
    m.pT = [ps("mp_T0", (128, 512), BF16), ps("mp_T1", (128, 512), BF16)]
    return m


def emit_mamba(p, cfg, m, x_src, x_dst):
    t, d = p.t, p.t.dram
    T = cfg.T
    n_tt = T // 128
    NB = T // 512
    NQ = T // 256
    st = ln_state(p, m.ln)
    bA = bank_new(m.pA)
    bB = bank_new(m.pB)
    bC = bank_new(m.pC)
    bD = bank_new(m.pD)
    bT = [bank_new(m.pT[0]), bank_new(m.pT[1])]
    bP = [bank_new(m.ln.tr[0]), bank_new(m.ln.tr[1])]
    s_w = [Slot(p, "g0"), Slot(p, "g1")]
    s_acr = Slot(p, "g7")
    s_csp = Slot(p, "g8")
    s_cb = Slot(p, "g5")
    s_cb2 = Slot(p, "g9")
    s_hst = [Slot(p, "y2"), Slot(p, "y3")]
    s_gst = [Slot(p, "y10"), Slot(p, "y11")]
    s_yst = Slot(p, "y6")
    s_hrl = Slot(p, "y7")
    s_yrl = Slot(p, "y8")
    s_sgrl = [Slot(p, "y9"), Slot(p, "y12")]
    s_xst = [Slot(p, "y4"), Slot(p, "y5")]
    s_xout = [Slot(p, "y13"), Slot(p, "y14")]
    s_wo = [Slot(p, "g3"), Slot(p, "g4")]
    s_xsp = Slot(p, "y15")
    s_xtl = Slot(p, "y16")

    p.emit("vector", lambda e: e.memset(m.S32[:], 0.0))
    p.emit("vector", lambda e: e.memset(m.S16[:], 0.0))
    p.emit("vector", lambda e: e.tensor_tensor(
        out=m.dident[:], in0=t.id16[:].unsqueeze(1).broadcast_to((128, 32, 128)),
        in1=t.dpanel[:].unsqueeze(2).broadcast_to((128, 32, 128)), op=ALU.mult))
    p.emit("vector", lambda e: e.memset(m.ac32[:, 4, :], 0.0), sig="ve")
    sz = p.snap("ve")
    p.wait("tensor", "ve", sz)
    p.wait("scalar", "ve", sz)

    for part in range(NB - 1, -1, -1):
        emit_ln_transpose(p, m.ln, st, x_src, None, part * 4, 4, m.xlnT, part * 512)
    xlnT_done = p.snap("se")

    ns_order = list(range(NB - 1, -1, -1))
    hprog = 0
    for ct in range(34):
        sl = s_w[ct % 2]
        sl.dma("gpsimd", lambda e, ct=ct, sl=sl: e.dma_start(
            out=m.wtile[s_w.index(sl)][:].rearrange("p (k m) -> p k m", k=16),
            in_=d.m_win_hbc[ct].rearrange("k p m -> p k m")))
        for ns in ns_order:
            bank = bP[(ct * NB + ns) % 2]
            p.wait("tensor", bank["rd"][0], bank["rd"][1])
            sl.wait_data("tensor")
            p.wait("tensor", "se", xlnT_done)
            wbuf = m.wtile[ct % 2]
            for dch in range(16):
                p.emit("tensor", lambda e, wbuf=wbuf, dch=dch, ns=ns, bank=bank: e.matmul(
                    bank["t"][:, 0:512], wbuf[:, dch * 128:(dch + 1) * 128],
                    m.xlnT[:, dch, ns * 512:(ns + 1) * 512],
                    start=(dch == 0), stop=(dch == 15)),
                    sig=("te" if dch == 15 else None))
            bank["wr"] = ("te", p.snap("te"))
            stn = (ct * NB + ns) % 2
            p.wait("vector", "y2" if stn == 0 else "y3", s_hst[stn].ld)
            bank_read(p, "vector", bank,
                      lambda e, stn=stn, bank=bank: e.tensor_copy(
                          m.strip[stn][:, 0:512], bank["t"][:, 0:512]), "ve")
            sdone = p.snap("ve")
            p.wait("sync", "ve", sdone)
            s_hst[stn].dma("sync", lambda e, ct=ct, ns=ns, stn=stn: e.dma_start(
                out=d.hbc_hbm[ct, :, 3 + ns * 512: 3 + (ns + 1) * 512], in_=m.strip[stn][:, 0:512]))
        sl.set_free("te")
        if ct == 33:
            pass
    hbc_done_y2 = s_hst[0].ld
    hbc_done_y3 = s_hst[1].ld

    p.wait("gpsimd", "y2", hbc_done_y2)
    p.wait("gpsimd", "y3", hbc_done_y3)
    s_cb2.dma("gpsimd", lambda e: e.dma_start(
        out=d.cb2_in.rearrange("(c p) k -> c p k", p=128),
        in_=d.hbc_hbm[:, :, 3 + T - 3: 3 + T]))
    s_cb2.wait_data("gpsimd")
    p.emit("gpsimd", lambda e: e.collective_compute(
        "AllGather", ALU.bypass,
        replica_groups=[[0, 1], [2, 3], [4, 5], [6, 7]],
        ins=[d.cb2_in.opt()], outs=[d.cb2_out.opt()]), sig="cc")
    p.wait("gpsimd", "cc")
    s_cb2.dma("gpsimd", lambda e: e.dma_start(
        out=m.tails[:].rearrange("p c k -> p c k"),
        in_=d.cb2_out[0:34 * 128].rearrange("(c p) k -> p c k", p=128)))
    s_cb2.wait_data("vector")
    p.drain("vector")
    p.emit("vector", lambda e: e.tensor_scalar(
        out=m.tails[:], in0=m.tails[:], scalar1=t.odd[:, 0:1], scalar2=None,
        op0=ALU.mult), sig="ve")
    tl = p.snap("ve")
    p.wait("sync", "ve", tl)
    s_hst[0].dma("sync", lambda e: e.dma_start(
        out=d.hbc_hbm[:, :, 0:3].rearrange("c p k -> p c k"), in_=m.tails[:]))
    tails_st = s_hst[0].ld

    for mt in range(32):
        sl = s_w[mt % 2]
        sl.dma("gpsimd", lambda e, mt=mt, sl=sl: e.dma_start(
            out=m.wtile[s_w.index(sl)][:].rearrange("p (k m) -> p k m", k=16),
            in_=d.m_win_gate[mt].rearrange("k p m -> p k m")))
        for ns in range(NB):
            bank = bP[(mt * NB + ns) % 2]
            p.wait("tensor", bank["rd"][0], bank["rd"][1])
            sl.wait_data("tensor")
            for dch in range(16):
                p.emit("tensor", lambda e, mt=mt, dch=dch, ns=ns, bank=bank: e.matmul(
                    bank["t"][:, 0:512], m.wtile[mt % 2][:, dch * 128:(dch + 1) * 128],
                    m.xlnT[:, dch, ns * 512:(ns + 1) * 512],
                    start=(dch == 0), stop=(dch == 15)),
                    sig=("te" if dch == 15 else None))
            bank["wr"] = ("te", p.snap("te"))
            stn = (mt * NB + ns) % 2
            p.wait("scalar", "y10" if stn == 0 else "y11", s_gst[stn].ld)
            bank_read(p, "scalar", bank,
                      lambda e, stn=stn, bank=bank: e.activation(
                          out=m.gstrip[stn][:, 0:512], in_=bank["t"][:, 0:512],
                          func=ACT.Silu), "se")
            gdone = p.snap("se")
            p.wait("sync", "se", gdone)
            s_gst[stn].dma("sync", lambda e, mt=mt, ns=ns, stn=stn: e.dma_start(
                out=d.sgate_hbm[mt * 128:(mt + 1) * 128, ns * 512:(ns + 1) * 512],
                in_=m.gstrip[stn][:, 0:512]))
        sl.set_free("te")

    for r in range(n_tt):
        bank = bC if r % 2 == 0 else bD
        p.wait("tensor", bank["rd"][0], bank["rd"][1])
        wsl = s_w[r % 2]
        wsl.dma("gpsimd", lambda e, wsl=wsl: e.dma_start(
            out=m.wtile[s_w.index(wsl)][:, 0:512], in_=d.m_win_dt))
        wsl.wait_data("tensor")
        p.wait("tensor", "se", xlnT_done)
        for dch in range(16):
            p.emit("tensor", lambda e, r=r, dch=dch, wsl=wsl, bank=bank: e.matmul(
                bank["t"][:, 0:32], m.xlnT[:, dch, r * 128:(r + 1) * 128],
                m.wtile[s_w.index(wsl)][:, dch * 32:(dch + 1) * 32],
                start=(dch == 0), stop=(dch == 15)),
                sig=("te" if dch == 15 else None))
        bank["wr"] = ("te", p.snap("te"))
        wsl.set_free("te")
        bank_read(p, "vector", bank,
                  lambda e, bank=bank: e.tensor_tensor(
                      out=m.dts[0][:], in0=bank["t"][:, 0:32], in1=t.dtbias[:], op=ALU.add), "ve")
        z1 = p.snap("ve")
        p.wait("scalar", "ve", z1)
        p.emit("scalar", lambda e: e.activation(
            out=m.dts[1][:], in_=m.dts[0][:], func=ACT.Sigmoid, scale=-1.0), sig="se")
        p.drain("scalar")
        p.emit("scalar", lambda e: e.activation(
            out=m.dts[2][:], in_=m.dts[1][:], func=ACT.Ln), sig="se")
        z3 = p.snap("se")
        p.wait("vector", "se", z3)
        p.drain("vector")
        p.emit("vector", lambda e, r=r: e.tensor_scalar(
            out=t.dt_tm[:, r, :], in0=m.dts[2][:], scalar1=-1.0, scalar2=100.0,
            op0=ALU.mult, op1=ALU.min), sig="ve")
    dt_done = p.snap("ve")

    for q in range(NQ):
        if q == 0:
            s_hrl.free = ("y2", s_hst[0].ld)
            p.wait("sync", "se", p.snap("se"))
            p.wait("sync", "ve", p.snap("ve"))
            p.wait("sync", "te", p.snap("te"))
        s_hrl.dma("sync", lambda e, q=q: e.dma_start(
            out=m.hbc[:], in_=d.hbc_hbm[:, :, q * 256: q * 256 + 259].rearrange("c p k -> p c k")))
        s_hrl.wait_data("vector")
        s_hrl.wait_data("scalar")
        hrl_ld = p.snap("ve")

        for ct in range(34):
            p.wait("scalar", "ve", getattr(st, "cv_free", 0))
            p.emit("scalar", lambda e, ct=ct: e.activation(
                out=m.cva[:], in_=m.hbc[:, ct, 0:256],
                func=ACT.Copy, scale=t.convw[:, ct * 4: ct * 4 + 1], bias=0.0), sig="se")
            tap0 = p.snap("se")
            p.wait("vector", "se", tap0)
            p.emit("vector", lambda e, ct=ct: e.scalar_tensor_tensor(
                out=m.cvb[:], in0=m.hbc[:, ct, 1:257], scalar=t.convw[:, ct * 4 + 1: ct * 4 + 2],
                in1=m.cva[:], op0=ALU.mult, op1=ALU.add), sig="ve")
            p.drain("vector")
            p.emit("vector", lambda e, ct=ct: e.scalar_tensor_tensor(
                out=m.cva[:], in0=m.hbc[:, ct, 2:258], scalar=t.convw[:, ct * 4 + 2: ct * 4 + 3],
                in1=m.cvb[:], op0=ALU.mult, op1=ALU.add), sig="ve")
            p.drain("vector")
            p.emit("vector", lambda e, ct=ct: e.scalar_tensor_tensor(
                out=m.cvb[:], in0=m.hbc[:, ct, 3:259], scalar=t.convw[:, ct * 4 + 3: ct * 4 + 4],
                in1=m.cva[:], op0=ALU.mult, op1=ALU.add), sig="ve")
            cv = p.snap("ve")
            p.wait("scalar", "ve", cv)
            p.emit("scalar", lambda e, ct=ct: e.activation(
                out=m.hbc[:, ct, 3:259], in_=m.cvb[:], func=ACT.Silu,
                bias=t.convb[:, ct:ct + 1]), sig="se")
            st.cv_free = p.snap("se")
        conv_done = p.snap("se")

        p.wait("vector", "se", conv_done)
        for c2 in range(2):
            p.emit("vector", lambda e, q=q, c2=c2: e.tensor_copy(
                t.c_keep[:, (q * 2 + c2) * 128:(q * 2 + c2 + 1) * 128],
                m.hbc[:, 33, 3 + c2 * 128: 3 + (c2 + 1) * 128]), sig="ve")
        p.wait("sync", "se", conv_done)
        s_xsp.dma("sync", lambda e: e.dma_start(
            out=d.xs_hbm.rearrange("(c p) k -> p c k", p=128), in_=m.hbc[:, 0:32, 3:259]))
        p.wait("sync", "y15", s_xsp.ld)
        for c2 in range(2):
            s_xtl.dma("sync", lambda e, c2=c2: e.dma_start(
                out=m.xs[:, c2, :], in_=d.xs_hbm[:, c2 * 128:(c2 + 1) * 128],
                transpose=True))
        s_xtl.wait_data("vector")
        s_xtl.wait_data("tensor")
        xs_done = p.snap("ve")

        for c2 in range(2):
            gch = q * 2 + c2
            s16_prev = p.snap("se")
            p.wait("vector", "ve", dt_done)
            p.wait("vector", "se", p.snap("se"))
            p.drain("vector")
            p.emit("vector", lambda e, gch=gch: e.scalar_tensor_tensor(
                out=m.ac32[:, 0, :], in0=t.dt_tm[:, gch, :], scalar=-1.0, in1=t.apos[:],
                op0=ALU.mult, op1=ALU.mult), sig="ve")
            ad_done = p.snap("ve")
            p.wait("tensor", bC["rd"][0], bC["rd"][1])
            p.wait("tensor", bD["rd"][0], bD["rd"][1])
            p.wait("tensor", "ve", ad_done)
            p.emit("tensor", lambda e: e.matmul(
                m.pC[:, 0:32], t.tri[:], m.ac32[:, 0, :], start=True, stop=True), sig="te")
            bC["wr"] = ("te", p.snap("te"))
            p.emit("tensor", lambda e: e.matmul(
                m.pD[0:32, 0:128], m.ac32[:, 0, :], t.tri[:], start=True, stop=True), sig="te")
            bD["wr"] = ("te", p.snap("te"))
            bank_read(p, "vector", bC,
                      lambda e, gch=gch: e.tensor_copy(t.acum_tm[:, gch, :], m.pC[:, 0:32]), "ve")
            bank_read(p, "vector", bD,
                      lambda e: e.tensor_copy(m.acf[:], m.pD[0:32, 0:128]), "ve")
            ac_done = p.snap("ve")
            p.wait("gpsimd", "ve", ac_done)
            s_acr.dma("gpsimd", lambda e: e.dma_start(
                out=d.acr_hbm.rearrange("o (h l) -> (o h) l", l=128), in_=m.acf[:]))
            s_acr.wait_data("gpsimd")
            s_acr.dma("gpsimd", lambda e: e.dma_start(out=t.acrow[:, 0:4096], in_=d.acr_hbm))
            p.wait("tensor", bB["rd"][0], bB["rd"][1])
            s_acr.wait_data("tensor")
            p.emit("tensor", lambda e: e.matmul(
                m.pB[:, 0:32], t.ones1[:], t.acrow[0:1, 127:4096:128],
                start=True, stop=True), sig="te")
            bB["wr"] = ("te", p.snap("te"))
            bank_read(p, "vector", bB,
                      lambda e: e.tensor_copy(m.ac32[:, 1, :], m.pB[:, 0:32]), "ve")
            csp = p.snap("ve")
            p.wait("scalar", "ve", csp)
            p.emit("scalar", lambda e: e.activation(
                out=m.ac32[:, 2, :], in_=m.ac32[:, 1, :], func=ACT.Exp), sig="se")
            p.wait("vector", "ve", ac_done)
            p.drain("vector")
            p.emit("vector", lambda e, gch=gch: e.tensor_tensor(
                out=m.ac32[:, 3, :], in0=t.acum_tm[:, gch, :], in1=m.ac32[:, 1, :],
                op=ALU.subtract), sig="ve")
            d1 = p.snap("ve")
            p.wait("scalar", "ve", d1)
            p.emit("scalar", lambda e: e.activation(
                out=m.ac32[:, 3, :], in_=m.ac32[:, 3, :], func=ACT.Exp, scale=-1.0), sig="se")
            p.emit("scalar", lambda e, gch=gch: e.activation(
                out=m.ac32[:, 5, :], in_=t.acum_tm[:, gch, :], func=ACT.Exp), sig="se")
            p.emit("vector", lambda e, gch=gch: e.tensor_tensor(
                out=m.ac32[:, 6, :], in0=t.acum_tm[:, gch, :], in1=m.ac32[:, 4, :],
                op=ALU.add), sig="ve")
            d2 = p.snap("ve")
            p.wait("scalar", "ve", d2)
            p.emit("scalar", lambda e, gch=gch: e.activation(
                out=t.sdecabs[:, gch, :], in_=m.ac32[:, 6, :], func=ACT.Exp), sig="se")
            p.drain("vector")
            p.emit("vector", lambda e: e.tensor_tensor(
                out=m.ac32[:, 4, :], in0=m.ac32[:, 4, :], in1=m.ac32[:, 1, :], op=ALU.add), sig="ve")
            pan_done_se = p.snap("se")
            pan_done_ve = p.snap("ve")

            p.wait("tensor", bD["rd"][0], bD["rd"][1])
            p.wait("tensor", "se", conv_done)
            p.emit("tensor", lambda e, c2=c2: e.matmul(
                m.pD[:, 0:128], m.hbc[:, 32, 3 + c2 * 128: 3 + (c2 + 1) * 128],
                m.hbc[:, 33, 3 + c2 * 128: 3 + (c2 + 1) * 128], start=True, stop=True), sig="te")
            bD["wr"] = ("te", p.snap("te"))
            bank_read(p, "vector", bD,
                      lambda e: e.tensor_tensor(
                          out=m.gtri[:], in0=m.pD[:, 0:128], in1=t.tri[:], op=ALU.mult), "ve")
            bank = bT[0]
            p.wait("tensor", bank["rd"][0], bank["rd"][1])
            p.wait("tensor", "se", conv_done)
            p.emit("tensor", lambda e, c2=c2, bank=bank: e.transpose(
                bank["t"][:, 0:128], m.hbc[:, 32, 3 + c2 * 128: 3 + (c2 + 1) * 128],
                t.id16[:]), sig="te")
            bank["wr"] = ("te", p.snap("te"))
            bank_read(p, "vector", bank,
                      lambda e, bank=bank: e.tensor_copy(m.btm[:], bank["t"][:, 0:128]), "ve")
            gq_done = p.snap("ve")

            p.wait("vector", "ve", xs_done)
            p.wait("vector", "se", pan_done_se)
            p.wait("vector", "te", getattr(st, "mscr_free", 0))
            p.drain("vector")
            p.emit("vector", lambda e, c2=c2, gch=gch: e.tensor_tensor(
                out=m.xdt_all[:], in0=m.xs[:, c2, :].rearrange("p (h v) -> p h v", v=128),
                in1=t.dt_tm[:, gch, :].unsqueeze(2).broadcast_to((128, 32, 128)),
                op=ALU.mult), sig="ve")
            p.emit("vector", lambda e: e.tensor_tensor(
                out=m.bdec_all[:], in0=m.btm[:].unsqueeze(1).broadcast_to((128, 32, 128)),
                in1=m.ac32[:, 3, :].unsqueeze(2).broadcast_to((128, 32, 128)),
                op=ALU.mult), sig="ve")
            batched_done = p.snap("ve")

            prev_em = 0
            prev_mtt = 0
            for hg in range(8):
                bank = bB if hg % 2 == 0 else bD
                p.wait("tensor", bank["rd"][0], bank["rd"][1])
                s_acr.wait_data("tensor")
                p.emit("tensor", lambda e, hg=hg, bank=bank: e.matmul(
                    bank["t"][:, 0:512], t.ones1[:], t.acrow[0:1, hg * 512:(hg + 1) * 512],
                    start=True, stop=True), sig="te")
                bank["wr"] = ("te", p.snap("te"))
                p.wait("vector", "ve", gq_done)
                p.wait("vector", "se", prev_em)
                bank_read(p, "vector", bank,
                          lambda e, hg=hg, gch=gch, bank=bank: e.tensor_tensor(
                              out=m.dsl4[:], in0=bank["t"][:, 0:512].rearrange("p (h v) -> p h v", v=128),
                              in1=t.acum_tm[:, gch, hg * 4:(hg + 1) * 4].unsqueeze(2).broadcast_to((128, 4, 128)),
                              op=ALU.subtract), "ve")
                p.drain("vector")
                p.emit("vector", lambda e: e.tensor_scalar(
                    out=m.dsl4[:], in0=m.dsl4[:], scalar1=0.0, scalar2=None,
                    op0=ALU.min), sig="ve")
                dm = p.snap("ve")
                p.wait("scalar", "ve", dm)
                p.wait("scalar", "ve", prev_mtt)
                p.emit("scalar", lambda e, bank=bank: e.activation(
                    out=m.esd4[:], in_=bank["t"][:, 0:512].rearrange("p (h v) -> p h v", v=128),
                    func=ACT.Exp), sig="se")
                bank["rd"] = ("se", p.snap("se"))
                p.emit("scalar", lambda e: e.activation(
                    out=m.esl4[:], in_=m.dsl4[:], func=ACT.Exp), sig="se")
                em = p.snap("se")
                prev_em = em
                p.wait("vector", "se", em)
                p.emit("vector", lambda e, hg=hg, c2=c2: e.tensor_tensor(
                    out=m.csc_all[:, hg * 4:(hg + 1) * 4, :],
                    in0=m.hbc[:, 33, 3 + c2 * 128: 3 + (c2 + 1) * 128].unsqueeze(1).broadcast_to((128, 4, 128)),
                    in1=m.esd4[:], op=ALU.mult), sig="ve")
                p.wait("vector", "se", em)
                p.emit("vector", lambda e, hg=hg: e.tensor_tensor(
                    out=m.M_all[:, hg * 4:(hg + 1) * 4, :], in0=m.esl4[:],
                    in1=m.gtri[:].unsqueeze(1).broadcast_to((128, 4, 128)),
                    op=ALU.mult), sig="ve")
                prev_mtt = p.snap("ve")
            M_done = p.snap("ve")

            ya = [bA, bank_new(m.ln.tr[0]), bank_new(m.pB), bank_new(m.ln.tr[1])]
            ya[1]["rd"] = bP[0]["rd"]
            ya[2]["rd"] = bB["rd"]
            ya[3]["rd"] = bP[1]["rd"]
            tc_ = [bank_new(m.pC), bank_new(m.pD)]
            tc_[0]["rd"] = bC["rd"]
            tc_[1]["rd"] = bD["rd"]
            for h in range(32):
                pa = ya[h % 4]
                pc = tc_[h % 2]
                p.wait("tensor", pa["rd"][0], pa["rd"][1])
                p.wait("tensor", "ve", M_done)
                p.wait("tensor", "se", s16_prev)
                p.emit("tensor", lambda e, h=h, pa=pa: e.matmul(
                    pa["t"][:, 0:128], m.M_all[:, h, :], m.xdt_all[:, h, :],
                    start=True, stop=False))
                p.emit("tensor", lambda e, h=h, c2=c2, pa=pa: e.matmul(
                    pa["t"][:, 0:128], m.dident[:, h, :],
                    m.xs[:, c2, h * 128:(h + 1) * 128], start=False, stop=False))
                p.emit("tensor", lambda e, h=h, pa=pa: e.matmul(
                    pa["t"][:, 0:128], m.csc_all[:, h, :], m.S16[:, h, :],
                    start=False, stop=True), sig="te")
                pa["wr"] = ("te", p.snap("te"))
                p.wait("tensor", pc["rd"][0], pc["rd"][1])
                p.emit("tensor", lambda e, h=h, pc=pc: e.matmul(
                    pc["t"][:, 0:128], m.bdec_all[:, h, :], m.xdt_all[:, h, :],
                    start=True, stop=True), sig="te")
                pc["wr"] = ("te", p.snap("te"))
                p.wait("vector", pa["wr"][0], pa["wr"][1])
                p.wait("vector", "y6", s_yst.ld)
                p.emit("vector", lambda e, c2=c2, h=h, pa=pa: e.tensor_copy(
                    m.yq[:, c2, h * 128:(h + 1) * 128], pa["t"][:, 0:128]), sig="ve")
                pa["rd"] = ("ve", p.snap("ve"))
                p.wait("vector", pc["wr"][0], pc["wr"][1])
                p.emit("vector", lambda e, h=h, pc=pc: e.scalar_tensor_tensor(
                    out=m.S32[:, h, :], in0=m.S32[:, h, :], scalar=m.ac32[:, 2, h:h + 1],
                    in1=pc["t"][:, 0:128], op0=ALU.mult, op1=ALU.add), sig="ve")
                pc["rd"] = ("ve", p.snap("ve"))
                sup = p.snap("ve")
                p.wait("scalar", "ve", sup)
                p.emit("scalar", lambda e, h=h: e.copy(
                    out=m.S16[:, h, :], in_=m.S32[:, h, :]), sig="se")
            st.mscr_free = p.snap("te")
            bA["rd"] = ya[0]["rd"]
            bA["wr"] = ya[0]["wr"]
            bP[0]["rd"] = ya[1]["rd"]
            bB["rd"] = ya[2]["rd"]
            bB["wr"] = ya[2]["wr"]
            bP[1]["rd"] = ya[3]["rd"]
            bC["rd"] = tc_[0]["rd"]
            bC["wr"] = tc_[0]["wr"]
            bD["rd"] = tc_[1]["rd"]
            bD["wr"] = tc_[1]["wr"]
        s_xsp.free = ("y16", s_xtl.ld)
        s_xtl.free = ("te", p.snap("te"))
        yq_done = p.snap("ve")
        p.wait("sync", "ve", yq_done)
        s_yst.dma("sync", lambda e, q=q: e.dma_start(
            out=d.y_hbm[q * 256:(q + 1) * 256, :].rearrange("(a p) n -> p a n", p=128),
            in_=m.yq[:]))

    slast = p.snap("se")
    p.wait("gpsimd", "se", slast)
    p.wait("gpsimd", "ve", p.snap("ve"))
    s_cb.dma("gpsimd", lambda e: e.dma_start(
        out=d.cb3_in.rearrange("(a p) v -> p a v", p=128), in_=m.S32[:]))
    s_cb.wait_data("gpsimd")
    p.emit("gpsimd", lambda e: e.collective_compute(
        "AllGather", ALU.bypass,
        replica_groups=[[0, 1], [2, 3], [4, 5], [6, 7]],
        ins=[d.cb3_in.opt()], outs=[d.cb3_out.opt()]), sig="cc")
    p.wait("gpsimd", "cc")
    s_cb.dma("gpsimd", lambda e: e.dma_start(
        out=m.S32[:], in_=d.cb3_out[0:4096, :].rearrange("(a p) v -> p a v", p=128)))
    s_cb.wait_data("vector")
    p.drain("vector")
    p.emit("vector", lambda e: e.tensor_scalar(
        out=m.S32[:].rearrange("p a b -> p (a b)"),
        in0=m.S32[:].rearrange("p a b -> p (a b)"),
        scalar1=t.odd[:, 0:1], scalar2=None, op0=ALU.mult), sig="ve")
    sm = p.snap("ve")
    p.wait("scalar", "ve", sm)
    p.emit("scalar", lambda e: e.copy(
        out=m.S16[:].rearrange("p a b -> p (a b)"),
        in_=m.S32[:].rearrange("p a b -> p (a b)")), sig="se")
    srdone = p.snap("se")

    s_yrl.free = ("y6", s_yst.ld)
    p.wait("sync", "y10", s_gst[0].ld)
    p.wait("sync", "y11", s_gst[1].ld)
    for q in range(NQ):
        s_yrl.dma("sync", lambda e, q=q: e.dma_start(
            out=m.yt[:], in_=d.y_hbm[q * 256:(q + 1) * 256, :].rearrange("(a p) n -> p a n", p=128)))
        s_yrl.wait_data("vector")
        for r2 in range(2):
            ssl = s_sgrl[r2]
            ssl.dma("sync", lambda e, q=q, r2=r2: e.dma_start(
                out=m.sgt[:, r2, :],
                in_=d.sgate_hbm[:, q * 256 + r2 * 128: q * 256 + (r2 + 1) * 128],
                transpose=True))
            ssl.wait_data("vector")
        for c2 in range(2):
            gch = q * 2 + c2
            for h in range(32):
                p.wait("tensor", bB["rd"][0], bB["rd"][1])
                p.wait("tensor", "se", srdone)
                p.emit("tensor", lambda e, gch=gch, h=h: e.matmul(
                    m.pB[:, 0:128], t.c_keep[:, gch * 128:(gch + 1) * 128],
                    m.S16[:, h, :], start=True, stop=True), sig="te")
                bB["wr"] = ("te", p.snap("te"))
                bank_read(p, "vector", bB,
                          lambda e, c2=c2, h=h, gch=gch: e.scalar_tensor_tensor(
                              out=m.yt[:, c2, h * 128:(h + 1) * 128], in0=m.pB[:, 0:128],
                              scalar=t.sdecabs[:, gch, h:h + 1],
                              in1=m.yt[:, c2, h * 128:(h + 1) * 128],
                              op0=ALU.mult, op1=ALU.add), "ve")
        corr_done = p.snap("ve")
        for r2 in range(2):
            p.wait("vector", "ve", corr_done)
            p.drain("vector")
            p.emit("vector", lambda e, r2=r2: e.tensor_tensor(
                out=m.yt[:, r2, :], in0=m.yt[:, r2, :], in1=m.sgt[:, r2, :], op=ALU.mult), sig="ve")
            yg1 = p.snap("ve")
            p.wait("scalar", "ve", yg1)
            p.emit("scalar", lambda e, r2=r2: e.activation(
                out=m.sgt[:, r2, :], in_=m.yt[:, r2, :], func=ACT.Square,
                accum_out=m.ssq[:, r2:r2 + 1]), sig="se")
            rms1 = p.snap("se")
            p.wait("scalar", "se", rms1)
            p.drain("scalar")
            p.emit("scalar", lambda e, r2=r2: e.activation(
                out=m.ssq[:, 2 + r2:3 + r2], in_=m.ssq[:, r2:r2 + 1], func=ACT.Sqrt,
                scale=1.0 / 4096.0, bias=t.epscol[:, 0:1]), sig="se")
            rms2 = p.snap("se")
            p.wait("vector", "se", rms2)
            p.emit("vector", lambda e, r2=r2: e.reciprocal(
                out=m.ssq[:, r2:r2 + 1], in_=m.ssq[:, 2 + r2:3 + r2]))
            p.drain("vector")
            p.emit("vector", lambda e, r2=r2: e.tensor_scalar(
                out=m.yt[:, r2, :], in0=m.yt[:, r2, :], scalar1=m.ssq[:, r2:r2 + 1],
                scalar2=None, op0=ALU.mult), sig="ve")
            p.drain("vector")
            p.emit("vector", lambda e, r2=r2: e.tensor_tensor(
                out=m.yt[:, r2, :], in0=m.yt[:, r2, :], in1=t.normw[:], op=ALU.mult), sig="ve")
        yn_done = p.snap("ve")
        for fch in range(32):
            for r2 in range(2):
                bank = bT[(fch * 2 + r2) % 2]
                p.wait("tensor", bank["rd"][0], bank["rd"][1])
                p.wait("tensor", "ve", yn_done)
                p.emit("tensor", lambda e, fch=fch, r2=r2, bank=bank: e.transpose(
                    bank["t"][:, 0:128], m.yt[:, r2, fch * 128:(fch + 1) * 128], t.id16[:]),
                    sig="te")
                bank["wr"] = ("te", p.snap("te"))
                bank_read(p, "vector", bank,
                          lambda e, fch=fch, r2=r2, bank=bank: e.tensor_copy(
                              m.ynT[:, fch, r2 * 128:(r2 + 1) * 128], bank["t"][:, 0:128]), "ve")
        ynT_done = p.snap("ve")
        obanks = [bank_new(m.pA), bank_new(m.pC)]
        obanks[0]["rd"] = bA["rd"]
        obanks[1]["rd"] = bC["rd"]
        for nch in range(4):
            for fch in range(32):
                wsl = s_wo[(nch * 32 + fch) % 2]
                wsl.dma("gpsimd", lambda e, fch=fch, nch=nch, wsl=wsl: e.dma_start(
                    out=m.wos[s_wo.index(wsl)][:],
                    in_=d.m_wout[fch, :, nch * 512:(nch + 1) * 512]))
                for r2 in range(2):
                    bank = obanks[r2]
                    if fch == 0:
                        p.wait("tensor", bank["rd"][0], bank["rd"][1])
                        p.wait("tensor", "ve", ynT_done)
                    wsl.wait_data("tensor")
                    p.emit("tensor", lambda e, r2=r2, fch=fch, wsl=wsl, bank=bank: e.matmul(
                        bank["t"][:, 0:512], m.ynT[:, fch, r2 * 128:(r2 + 1) * 128],
                        m.wos[s_wo.index(wsl)][:],
                        start=(fch == 0), stop=(fch == 31)),
                        sig=("te" if r2 == 1 else None))
                if fch == 31:
                    for r2 in range(2):
                        obanks[r2]["wr"] = ("te", p.snap("te"))
                wsl.set_free("te")
            for r2 in range(2):
                gt = q * 2 + r2
                xsl = s_xst[r2]
                xsl.dma("sync", lambda e, gt=gt, nch=nch: e.dma_start(
                    out=m.xst[gt % 2][:],
                    in_=x_src[gt * 128:(gt + 1) * 128, nch * 512:(nch + 1) * 512]))
                xsl.wait_data("vector")
                bank = obanks[r2]
                p.wait("vector", bank["wr"][0], bank["wr"][1])
                p.wait("vector", "y13" if r2 == 0 else "y14", s_xout[r2].ld)
                p.emit("vector", lambda e, r2=r2, gt=gt, bank=bank: e.tensor_tensor(
                    out=m.ost[r2][:], in0=bank["t"][:, 0:512], in1=m.xst[gt % 2][:],
                    op=ALU.add), sig="ve")
                bank["rd"] = ("ve", p.snap("ve"))
                xsl.set_free("ve")
                od = p.snap("ve")
                p.wait("sync", "ve", od)
                s_xout[r2].dma("sync", lambda e, gt=gt, nch=nch, r2=r2: e.dma_start(
                    out=x_dst[gt * 128:(gt + 1) * 128, nch * 512:(nch + 1) * 512],
                    in_=m.ost[r2][:]))
        bA["rd"] = obanks[0]["rd"]
        bC["rd"] = obanks[1]["rd"]
        bA["wr"] = obanks[0]["wr"]
        bC["wr"] = obanks[1]["wr"]
        s_yrl.free = ("te", p.snap("te"))
        for r2 in range(2):
            s_sgrl[r2].free = ("ve", p.snap("ve"))
    p.wait("sync", "y13", s_xout[0].ld)
    p.wait("sync", "y14", s_xout[1].ld)
    p.barrier()


def kernel(**inputs):
    cfg = NS(T=1024, phases=("gla", "ffn1", "mamba", "ffn2"))
    g = host_layout(inputs)
    maps = percore_inputs(inputs, g, cfg.T)
    nc = build_program(cfg)
    res = run_bass_kernel_spmd(nc, maps, core_ids=list(range(NCORES)))
    out = np.empty((4, 2048, 2048), dtype=np.float32)
    for c in range(NCORES):
        b, half = c // 2, c % 2
        out[b, half * cfg.T:(half + 1) * cfg.T] = res.results[c]["out"]
    return out
